# version 05
# baseline (speedup 1.0000x reference)
import math
import contextlib
import numpy as np

import concourse.bass as bass
import concourse.mybir as mybir
from concourse.bass_utils import run_bass_kernel_spmd

F32 = mybir.dt.float32
BF16 = mybir.dt.bfloat16
AF = mybir.ActivationFunctionType
ALU = mybir.AluOpType
AX = mybir.AxisListType

NEG = -1.0e30


class CFG:
    def __init__(self, mini=False, midi=False):
        if midi:
            self.D = 1024
            self.NTOK = 2048
            self.KVH = 8
            self.G = 1
        elif mini:
            self.D = 512
            self.NTOK = 512
            self.KVH = 2
            self.G = 2
        else:
            self.D = 4096
            self.NTOK = 2048
            self.KVH = 8
            self.G = 4
        self.HD = 128
        self.NI = self.D // 128
        self.NTILE = self.NTOK // 128
        self.QBN = max(1, self.NTOK // 512)
        self.QBT = self.NTOK // self.QBN
        self.QBTILE = self.QBT // 128
        self.TBN = max(1, self.NTOK // 1024)
        self.TBT = self.NTOK // self.TBN
        self.NH = self.TBT // 512
        self.E = 8
        self.R = 64
        self.LK = 128
        self.NP = 4
        self.KVDIM = self.KVH * self.HD
        self.NQH = self.KVH * self.G
        self.VH = max(1, self.KVDIM // 512)
        self.VW = self.KVDIM // self.VH
        self.SEQ = 2 * self.NTOK


class Ctr:
    def __init__(self, nc, name):
        self.sem = nc.alloc_semaphore(name)
        self.n = 0

    def inc(self, instr, k=1):
        instr.then_inc(self.sem, k)
        self.n += k
        return self.n

    def wait(self, eng, val=None):
        eng.wait_ge(self.sem, self.n if val is None else val)


class Stream:

    def __init__(self, nc, name, eng, nsem=2):
        self.sems = [nc.alloc_semaphore(f"{name}{i}") for i in range(nsem)]
        self.cnts = [0] * nsem
        self.eng = eng
        self.nsem = nsem
        self.ng = 0

    def put(self, out_ap, in_ap, rot=True, **kw):
        if rot:
            self.ng += 1
        k = self.ng % self.nsem
        self.eng.dma_start(out_ap, in_ap, **kw).then_inc(self.sems[k], 16)
        self.cnts[k] += 16
        return (k, self.cnts[k])

    def snap(self):
        k = self.ng % self.nsem
        return (k, self.cnts[k])

    def wait(self, eng):
        for k in range(self.nsem):
            if self.cnts[k]:
                eng.wait_ge(self.sems[k], self.cnts[k])

    def wait_put(self, eng):
        k = self.ng % self.nsem
        eng.wait_ge(self.sems[k], self.cnts[k])

    def wait_at(self, eng, sn):
        eng.wait_ge(self.sems[sn[0]], sn[1])

    def align(self, k):
        self.ng += (-self.ng) % k


def cdiv(a, b):
    return (a + b - 1) // b


def build(cfg: CFG):
    c = cfg
    nc = bass.Bass("TRN2", target_bir_lowering=False, debug=False, num_devices=8)

    i_x = nc.dram_tensor("x", [c.D, c.NTOK], F32, kind="ExternalInput").ap()
    i_masks = nc.dram_tensor("masks", [2, 128, 128], BF16, kind="ExternalInput").ap()
    i_worg = nc.dram_tensor("worg4", [c.NI, 128, c.NI, 128], F32, kind="ExternalInput").ap()
    i_wq = nc.dram_tensor("wq4", [c.NQH, 128, c.NI, 128], F32, kind="ExternalInput").ap()
    i_wo = nc.dram_tensor("wo4", [c.NI, 128, c.NQH, 128], F32, kind="ExternalInput").ap()
    i_wk = nc.dram_tensor("wk4", [128, c.NI, c.KVDIM], F32, kind="ExternalInput").ap()
    i_wv = nc.dram_tensor("wv4", [128, c.NI, c.KVDIM], F32, kind="ExternalInput").ap()
    i_down = nc.dram_tensor("down4", [c.NP, 128, c.NI, 128], F32, kind="ExternalInput").ap()
    i_up = nc.dram_tensor("up4", [c.NI, 128, c.NP, 128], F32, kind="ExternalInput").ap()
    i_wqf = nc.dram_tensor("wqf4", [128, c.NI, 128], F32, kind="ExternalInput").ap()
    i_keys = nc.dram_tensor("keysT", [c.LK, c.E], F32, kind="ExternalInput").ap()
    o_out = nc.dram_tensor("out", [c.D, c.NTOK], F32, kind="ExternalOutput").ap()

    d_x2 = nc.dram_tensor("x2d", [c.D, c.NTOK], F32, kind="Internal").ap()
    d_x3 = nc.dram_tensor("x3d", [c.D, c.NTOK], F32, kind="Internal").ap()
    d_y2 = nc.dram_tensor("y2d", [c.NI, 128, c.NTOK], BF16, kind="Internal").ap()
    kblk = c.KVH * c.NTILE * 128 * 128
    knorm_elems = c.KVH * c.NTOK
    cc_rows = cdiv(2 * kblk + knorm_elems, 2048)
    cc_elems = cc_rows * 2048
    t_kvin = nc.dram_tensor("kvin", [cc_rows, 2048], BF16, kind="Internal")
    t_kvout = nc.dram_tensor("kvout", [2 * cc_rows, 2048], BF16, kind="Internal")
    kvin_flat = t_kvin.ap().rearrange("a b -> (a b)")
    kvout_flat = t_kvout.ap().rearrange("a b -> (a b)")
    t_ys = [nc.dram_tensor(f"ysi{i}", [128, c.NI], F32, kind="Internal") for i in range(2)]
    t_yso = [nc.dram_tensor(f"yso{i}", [256, c.NI], F32, kind="Internal") for i in range(2)]

    def k_tile_in(h, j_loc):
        off = (h * c.NTILE + j_loc) * 16384
        return kvin_flat[off:off + 16384].rearrange("(p q) -> p q", p=128)

    def v_tile_in(h, j_loc):
        off = kblk + (j_loc * c.KVH + h) * 16384
        return kvin_flat[off:off + 16384].rearrange("(p q) -> p q", p=128)

    def knorm_in():
        off = 2 * kblk
        return kvin_flat[off:off + knorm_elems].rearrange("(a b) -> a b", a=1)

    def k_tile_out(rank, h, j_loc):
        off = rank * cc_elems + (h * c.NTILE + j_loc) * 16384
        return kvout_flat[off:off + 16384].rearrange("(p q) -> p q", p=128)

    def v_tile_out(rank, h, j_loc):
        off = rank * cc_elems + kblk + (j_loc * c.KVH + h) * 16384
        return kvout_flat[off:off + 16384].rearrange("(p q) -> p q", p=128)

    def knorm_out(rank):
        off = rank * cc_elems + 2 * kblk
        return kvout_flat[off:off + knorm_elems].rearrange("(a b) -> a b", a=1)

    SY, GP, SC, VE, TE = nc.sync, nc.gpsimd, nc.scalar, nc.vector, nc.tensor
    tms = Ctr(nc, "tms")
    vms = Ctr(nc, "vms")
    ams = Ctr(nc, "ams")
    vch = Ctr(nc, "vch")
    ach = Ctr(nc, "ach")

    def vstep(instr):
        vch.inc(instr)
        VE.wait_ge(vch.sem, vch.n)
        return instr

    def astep(instr):
        ach.inc(instr)
        SC.wait_ge(ach.sem, ach.n)
        return instr
    ccs = Ctr(nc, "ccs")
    bar = Ctr(nc, "bar")

    s_x = Stream(nc, "s_x", SY)
    s_r = Stream(nc, "s_r", SY)
    s_y = Stream(nc, "s_y", SY)
    s_k = Stream(nc, "s_k", SY, nsem=4)
    s_v = Stream(nc, "s_v", SY, nsem=4)
    s_g = Stream(nc, "s_g", SY)
    g_x = Stream(nc, "g_x", GP)
    g_dpb = Stream(nc, "g_dpb", GP, nsem=1)
    g_wob = Stream(nc, "g_wob", GP)
    g_upb = Stream(nc, "g_upb", GP)
    g_wq = Stream(nc, "g_wq", GP)
    g_wwo = Stream(nc, "g_wwo", GP)
    g_misc = Stream(nc, "g_misc", GP, nsem=1)
    g_s = Stream(nc, "g_s", GP, nsem=1)
    g_kn = Stream(nc, "g_kn", GP, nsem=1)
    g_ks = Stream(nc, "g_ks", GP)
    g_vs = Stream(nc, "g_vs", GP)
    c_s = Stream(nc, "c_s", SC)
    s_y2w = Stream(nc, "s_y2w", SC)

    def barrier():
        for e in (TE, VE, SC, GP, SY):
            e.sem_inc(bar.sem, 1)
        bar.n += 5
        for e in (TE, VE, SC, GP, SY):
            e.wait_ge(bar.sem, bar.n)

    sb_keys = nc.alloc_sbuf_tensor("sb_keys", [c.LK, c.E], BF16)
    sb_masks = nc.alloc_sbuf_tensor("sb_masks", [128, 2, 128], BF16)
    sb_ones_r = nc.alloc_sbuf_tensor("sb_ones_r", [1, 128], BF16)
    sb_ones_c = nc.alloc_sbuf_tensor("sb_ones_c", [128, 1], BF16)
    sb_coefv = nc.alloc_sbuf_tensor("sb_coefv", [128, c.NP], F32)
    sb_kmax2 = nc.alloc_sbuf_tensor("sb_kmax2", [1, c.KVH], F32)
    sb_tiny = nc.alloc_sbuf_tensor("sb_tiny", [128, 32], F32)
    sb_row = nc.alloc_sbuf_tensor("sb_row", [1, 1536], F32)
    sb_rowb = nc.alloc_sbuf_tensor("sb_rowb", [1, 2048], BF16)
    sb_qcol = nc.alloc_sbuf_tensor("sb_qcol", [128, 1], BF16)

    PS = [nc.alloc_psum_tensor(f"ps{i}", [128, 512], F32) for i in range(8)]

    VE.memset(sb_ones_r[:], 1.0)
    VE.memset(sb_ones_c[:], 1.0)
    vms.inc(VE.memset(sb_coefv[:], 0.0))
    s_g.put(sb_masks[:, 0, :], i_masks[0])
    s_g.put(sb_masks[:, 1, :], i_masks[1], rot=False)
    g_misc.put(sb_keys[:], i_keys[:])

    ISQ_HD = 1.0 / math.sqrt(c.HD)
    ISQ_LK = 1.0 / math.sqrt(c.LK)

    R_M = 0

    def ln_stats_512(x_dst_rstdb, x_dst_mrb, ps_sum, ps_sq, col0):
        m_row = sb_row[0:1, 0:512]
        v_row = sb_row[0:1, 512:1024]
        t_row = sb_row[0:1, 1024:1536]
        tms.wait(VE)
        VE.tensor_scalar(m_row, ps_sum[0:1, :], 1.0 / c.D, None, ALU.mult)
        vstep(VE.tensor_scalar(v_row, ps_sq[0:1, :], 1.0 / c.D, None, ALU.mult))
        vstep(VE.tensor_mul(t_row, m_row, m_row))
        vv = vms.inc(VE.scalar_tensor_tensor(v_row, v_row, 1e-5, t_row, ALU.add, ALU.subtract))
        SC.wait_ge(vms.sem, vv)
        astep(SC.activation(t_row, v_row, AF.Ln))
        aa = ams.inc(SC.activation(v_row, t_row, AF.Exp, scale=-0.5))
        VE.wait_ge(ams.sem, aa)
        vstep(VE.tensor_mul(m_row, m_row, v_row))
        vstep(VE.tensor_copy(sb_rowb[0:1, 0:512], v_row))
        VE.tensor_sub(sb_rowb[0:1, 512:1024], v_row, sb_rowb[0:1, 0:512])
        vstep(VE.tensor_copy(sb_rowb[0:1, 1024:1536], m_row))
        vv = vms.inc(VE.tensor_sub(sb_rowb[0:1, 1536:2048], m_row, sb_rowb[0:1, 1024:1536]))
        TE.wait_ge(vms.sem, vv)
        TE.matmul(PS[4][:, :], sb_ones_r[:], sb_rowb[0:1, 0:512], start=True, stop=False)
        TE.matmul(PS[4][:, :], sb_ones_r[:], sb_rowb[0:1, 512:1024], start=False, stop=True)
        TE.matmul(PS[5][:, :], sb_ones_r[:], sb_rowb[0:1, 1024:1536], start=True, stop=False)
        tt = tms.inc(TE.matmul(PS[5][:, :], sb_ones_r[:], sb_rowb[0:1, 1536:2048], start=False, stop=True))
        SC.wait_ge(tms.sem, tt)
        SC.copy(x_dst_rstdb[:, col0:col0 + 512], PS[4][:, :])
        return ams.inc(SC.copy(x_dst_mrb[:, col0:col0 + 512], PS[5][:, :]))

    def ffn_phase(tag, x_src, x_dst, ysi, yso):
        es = contextlib.ExitStack()
        with es:
            y1 = es.enter_context(nc.sbuf_tensor(f"{tag}_y1", [128, c.NI, c.NTOK], BF16))
            xb = [es.enter_context(nc.sbuf_tensor(f"{tag}_xb{i}", [128, c.TBT], F32)) for i in range(2)]
            xc2 = [es.enter_context(nc.sbuf_tensor(f"{tag}_xc{i}", [128, c.TBT], BF16)) for i in range(2)]
            sq2 = [es.enter_context(nc.sbuf_tensor(f"{tag}_sq0", [128, c.TBT], BF16))]
            rstdb = es.enter_context(nc.sbuf_tensor(f"{tag}_rstdb", [128, c.TBT], F32))
            mrb = es.enter_context(nc.sbuf_tensor(f"{tag}_mrb", [128, c.TBT], F32))
            ysum = es.enter_context(nc.sbuf_tensor(f"{tag}_ysum", [128, c.NI], F32))
            ysg = es.enter_context(nc.sbuf_tensor(f"{tag}_ysg", [128, 2 * c.NI], F32))
            ysb = es.enter_context(nc.sbuf_tensor(f"{tag}_ysb", [128, c.NI], BF16))
            hsb = es.enter_context(nc.sbuf_tensor(f"{tag}_h", [128, c.NP, c.NTOK], BF16))
            dpb = es.enter_context(nc.sbuf_tensor(f"{tag}_dp", [128, c.NI, 128], BF16))
            wob = [es.enter_context(nc.sbuf_tensor(f"{tag}_wo{i}", [128, c.NI, 128], BF16)) for i in range(2)]
            upb = [es.enter_context(nc.sbuf_tensor(f"{tag}_up{i}", [128, c.NP, 128], BF16)) for i in range(2)]
            res = [es.enter_context(nc.sbuf_tensor(f"{tag}_res0", [128, c.TBT], F32))]

            c_s.wait(SY)
            c_s.wait(GP)
            vms.inc(VE.memset(ysum[:], 0.0))

            stat_mm = {}
            for tb in range(c.TBN):
                t0 = tb * c.TBT
                for i in range(c.NI):
                    sl = i % 2
                    it = tb * c.NI + i
                    if it >= 2:
                        GP.wait_ge(tms.sem, stat_mm[it - 2])
                    g_x.put(xc2[sl][:], x_src[i * 128:(i + 1) * 128, t0:t0 + c.TBT])
                    g_x.wait_put(VE)
                    if it >= 1:
                        VE.wait_ge(tms.sem, stat_mm[it - 1])
                    v1 = vms.inc(VE.tensor_mul(sq2[0][:], xc2[sl][:], xc2[sl][:]))
                    g_x.wait_put(TE)
                    TE.wait_ge(vms.sem, v1)
                    for hh in range(c.NH):
                        TE.matmul(PS[0 + hh][0:1, :], sb_ones_c[:], xc2[sl][:, hh * 512:(hh + 1) * 512],
                                  start=(i == 0), stop=(i == c.NI - 1))
                        last = TE.matmul(PS[2 + hh][0:1, :], sb_ones_c[:], sq2[0][:, hh * 512:(hh + 1) * 512],
                                         start=(i == 0), stop=(i == c.NI - 1))
                    stat_mm[it] = tms.inc(last)
                for hh in range(c.NH):
                    aa = ln_stats_512(rstdb, mrb, PS[0 + hh], PS[2 + hh], hh * 512)
                for i in range(c.NI):
                    sl = i % 2
                    vms.wait(SY)
                    s_x.put(xb[sl][:], x_src[i * 128:(i + 1) * 128, t0:t0 + c.TBT])
                    s_x.wait_put(VE)
                    VE.wait_ge(ams.sem, aa)
                    vstep(VE.tensor_mul(xb[sl][:], xb[sl][:], rstdb[:]))
                    vstep(VE.tensor_sub(y1[:, i, t0:t0 + c.TBT], xb[sl][:], mrb[:]))
                    vstep(VE.tensor_reduce(sb_tiny[:, 0:1], y1[:, i, t0:t0 + c.TBT], AX.X, ALU.add))
                    vms.inc(VE.tensor_add(ysum[:, i:i + 1], ysum[:, i:i + 1], sb_tiny[:, 0:1]))

            vms.wait(GP)
            g_s.put(ysi.ap()[:], ysum[:])
            g_s.wait(GP)
            ccs.inc(GP.collective_compute(
                "AllGather", ALU.bypass,
                replica_groups=[[0, 1], [2, 3], [4, 5], [6, 7]],
                ins=[ysi.ap().opt()], outs=[yso.ap().opt()]))
            ccs.wait(SY)
            s_g.put(ysg[:, 0:c.NI], yso.ap()[0:128, :])
            s_g.put(ysg[:, c.NI:2 * c.NI], yso.ap()[128:256, :], rot=False)
            s_g.wait_put(VE)
            vstep(VE.tensor_add(ysg[:, 0:c.NI], ysg[:, 0:c.NI], ysg[:, c.NI:2 * c.NI]))
            vv = vms.inc(VE.tensor_scalar(ysb[:], ysg[:, 0:c.NI], 1.0 / c.SEQ, None, ALU.mult))
            tms.wait(GP)
            g_dpb.put(dpb[:], i_wqf[:])
            g_dpb.wait_put(TE)
            TE.wait_ge(vms.sem, vv)
            for i in range(c.NI):
                last = TE.matmul(PS[6][:, 0:1], dpb[:, i, :], ysb[:, i:i + 1],
                                 start=(i == 0), stop=(i == c.NI - 1))
            tt = tms.inc(last)
            SC.wait_ge(tms.sem, tt)
            aa = ams.inc(SC.copy(sb_tiny[:, 2:3], PS[6][:, 0:1]))
            VE.wait_ge(ams.sem, aa)
            vv = vms.inc(VE.tensor_copy(sb_qcol[:], sb_tiny[:, 2:3]))
            TE.wait_ge(vms.sem, vv)
            g_misc.wait(TE)
            tt = tms.inc(TE.matmul(PS[7][0:1, 0:c.E], sb_qcol[:], sb_keys[:], start=True, stop=True))
            SC.wait_ge(tms.sem, tt)
            aa = ams.inc(SC.activation(sb_tiny[0:1, 8:8 + c.E], PS[7][0:1, 0:c.E], AF.Copy, scale=ISQ_LK))
            VE.wait_ge(ams.sem, aa)
            vstep(VE.tensor_reduce(sb_tiny[0:1, 16:17], sb_tiny[0:1, 8:8 + c.E], AX.X, ALU.max))
            vv = vms.inc(VE.tensor_scalar(sb_tiny[0:1, 8:8 + c.E], sb_tiny[0:1, 8:8 + c.E],
                                          sb_tiny[0:1, 16:17], None, ALU.subtract))
            SC.wait_ge(vms.sem, vv)
            aa = ams.inc(SC.activation(sb_tiny[0:1, 8:8 + c.E], sb_tiny[0:1, 8:8 + c.E], AF.Exp))
            VE.wait_ge(ams.sem, aa)
            vstep(VE.tensor_reduce(sb_tiny[0:1, 17:18], sb_tiny[0:1, 8:8 + c.E], AX.X, ALU.add))
            vstep(VE.reciprocal(sb_tiny[0:1, 18:19], sb_tiny[0:1, 17:18]))
            vstep(VE.tensor_scalar(sb_tiny[0:1, 8:8 + c.E], sb_tiny[0:1, 8:8 + c.E],
                             sb_tiny[0:1, 18:19], None, ALU.mult))
            vv = vms.inc(VE.tensor_copy(sb_rowb[0:1, 0:c.E], sb_tiny[0:1, 8:8 + c.E]))
            TE.wait_ge(vms.sem, vv)
            for p in range(c.NP):
                TE.matmul(PS[6][0:64, 2 + p:3 + p], sb_ones_r[0:1, 0:64],
                          sb_rowb[0:1, 2 * p:2 * p + 1], start=True, stop=True)
                tt = tms.inc(TE.matmul(PS[6][64:128, 2 + p:3 + p], sb_ones_r[0:1, 0:64],
                                       sb_rowb[0:1, 2 * p + 1:2 * p + 2], start=True, stop=True))
            SC.wait_ge(tms.sem, tt)
            acoef = ams.inc(SC.copy(sb_coefv[:, 0:c.NP], PS[6][:, 2:2 + c.NP]))

            for p in range(c.NP):
                tms.wait(GP)
                g_dpb.put(dpb[:], i_down[p])
                g_dpb.wait_put(TE)
                for tb in range(c.TBN):
                    t0 = tb * c.TBT
                    vms.wait(TE)
                    for i in range(c.NI):
                        for hh in range(c.NH):
                            last = TE.matmul(PS[0 + hh][:, :], dpb[:, i, :],
                                             y1[:, i, t0 + hh * 512:t0 + (hh + 1) * 512],
                                             start=(i == 0), stop=(i == c.NI - 1))
                    tt = tms.inc(last)
                    VE.wait_ge(tms.sem, tt)
                    VE.wait_ge(ams.sem, acoef)
                    for hh in range(c.NH):
                        vv = vms.inc(VE.tensor_scalar(
                            hsb[:, p, t0 + hh * 512:t0 + (hh + 1) * 512],
                            PS[0 + hh][:, :], sb_coefv[:, p:p + 1], None, ALU.mult))

            grp_cnt = {}
            stt_cnt = {}
            st_snap = {}
            vms.wait(SY)
            for tb in range(c.TBN):
                t0 = tb * c.TBT
                for o in range(c.NI):
                    sl = o % 2
                    it = tb * c.NI + o
                    if it >= 2:
                        GP.wait_ge(tms.sem, grp_cnt[it - 2])
                    g_wob.put(wob[sl][:], i_worg[o])
                    g_upb.put(upb[sl][:], i_up[o])
                    if it >= 2:
                        SY.wait_ge(vms.sem, stt_cnt[it - 2])
                    s_r.put(xb[sl][:], x_src[o * 128:(o + 1) * 128, t0:t0 + c.TBT])
                    g_wob.wait_put(TE)
                    g_upb.wait_put(TE)
                    for hh in range(c.NH):
                        pb = 4 + (it % 2) * c.NH + hh
                        if it >= 2:
                            TE.wait_ge(vms.sem, stt_cnt[it - 2])
                        else:
                            vms.wait(TE)
                        for i in range(c.NI):
                            TE.matmul(PS[pb][:, :], wob[sl][:, i, :],
                                      y1[:, i, t0 + hh * 512:t0 + (hh + 1) * 512],
                                      start=(i == 0), stop=False)
                        for p in range(c.NP):
                            last = TE.matmul(PS[pb][:, :], upb[sl][:, p, :],
                                             hsb[:, p, t0 + hh * 512:t0 + (hh + 1) * 512],
                                             start=False, stop=(p == c.NP - 1))
                        tt = tms.inc(last)
                        VE.wait_ge(tms.sem, tt)
                        s_r.wait_put(VE)
                        if it >= 1 and hh == 0:
                            c_s.wait_at(VE, st_snap[it - 1])
                        vv = vms.inc(VE.scalar_tensor_tensor(
                            res[0][:, hh * 512:(hh + 1) * 512], PS[pb][:, :], 0.5,
                            xb[sl][:, hh * 512:(hh + 1) * 512], ALU.mult, ALU.add))
                    grp_cnt[it] = tt
                    stt_cnt[it] = vv
                    SC.wait_ge(vms.sem, vv)
                    st_snap[it] = c_s.put(x_dst[o * 128:(o + 1) * 128, t0:t0 + c.TBT], res[0][:])
            c_s.wait(SC)
        return

    def attn_phase():
        es = contextlib.ExitStack()
        with es:
            y2b = es.enter_context(nc.sbuf_tensor("a_y2b", [128, c.NI, 512], BF16))
            es2 = contextlib.ExitStack()
            with es2:
                xb = [es2.enter_context(nc.sbuf_tensor(f"a_xb{i}", [128, 512], F32)) for i in range(2)]
                xc2 = [es2.enter_context(nc.sbuf_tensor(f"a_xc{i}", [128, 512], BF16)) for i in range(2)]
                sq2 = [es2.enter_context(nc.sbuf_tensor("a_sq0", [128, 512], BF16))]
                rstdb = es2.enter_context(nc.sbuf_tensor("a_rstdb", [128, 512], F32))
                mrb = es2.enter_context(nc.sbuf_tensor("a_mrb", [128, 512], F32))
                wkb = es2.enter_context(nc.sbuf_tensor("a_wkb", [128, c.NI, c.KVDIM], BF16))
                wvb = es2.enter_context(nc.sbuf_tensor("a_wvb", [128, c.NI, c.KVDIM], BF16))
                kev = [es2.enter_context(nc.sbuf_tensor(f"a_kev{i}", [128, 512], BF16)) for i in range(2)]
                vev = [es2.enter_context(nc.sbuf_tensor(f"a_vev{i}", [128, c.VW], BF16)) for i in range(2)]
                ksq = es2.enter_context(nc.sbuf_tensor("a_ksq", [128, 512], BF16))
                knt = es2.enter_context(nc.sbuf_tensor("a_knt", [1, 512], BF16))
                kngbuf = es2.enter_context(nc.sbuf_tensor("a_kngbuf", [1, c.NTOK], BF16))

                c_s.wait(SY)
                c_s.wait(GP)
                g_misc.put(wkb[:], i_wk[:])
                g_misc.put(wvb[:], i_wv[:])
                nb = c.NTOK // 512
                kv_snap = {}
                ps1_reader = [0]
                stat_mm = {}
                for tb in range(nb):
                    t0 = tb * 512
                    for i in range(c.NI):
                        sl = i % 2
                        it = tb * c.NI + i
                        if it >= 2:
                            GP.wait_ge(tms.sem, stat_mm[it - 2])
                        g_x.put(xc2[sl][:], d_x2[i * 128:(i + 1) * 128, t0:t0 + 512])
                        g_x.wait_put(VE)
                        if it >= 1:
                            VE.wait_ge(tms.sem, stat_mm[it - 1])
                        v1 = vms.inc(VE.tensor_mul(sq2[0][:], xc2[sl][:], xc2[sl][:]))
                        g_x.wait_put(TE)
                        TE.wait_ge(vms.sem, v1)
                        TE.matmul(PS[0][0:1, :], sb_ones_c[:], xc2[sl][:], start=(i == 0), stop=(i == c.NI - 1))
                        stat_mm[it] = tms.inc(TE.matmul(PS[2][0:1, :], sb_ones_c[:], sq2[0][:], start=(i == 0), stop=(i == c.NI - 1)))
                    aa = ln_stats_512(rstdb, mrb, PS[0], PS[2], 0)
                    for i in range(c.NI):
                        sl = i % 2
                        vms.wait(SY)
                        s_x.put(xb[sl][:], d_x2[i * 128:(i + 1) * 128, t0:t0 + 512])
                        s_x.wait_put(VE)
                        VE.wait_ge(ams.sem, aa)
                        if tb > 0 and i < 2:
                            s_y2w.wait(VE)
                        tms.wait(VE) if i < 2 else None
                        vstep(VE.tensor_mul(xb[sl][:], xb[sl][:], rstdb[:]))
                        vv = vms.inc(VE.tensor_sub(y2b[:, i, :], xb[sl][:], mrb[:]))
                    SC.wait_ge(vms.sem, vv)
                    np4 = max(1, c.NI // 4)
                    for part in range(cdiv(c.NI, np4)):
                        i0, i1 = part * np4, min((part + 1) * np4, c.NI)
                        s_y2w.put(d_y2[i0:i1, :, t0:t0 + 512].rearrange("a p t -> p a t"),
                                  y2b[:, i0:i1, :], rot=(part == 0))
                    g_misc.wait(TE)
                    TE.wait_ge(vms.sem, vv)
                    for h in range(c.KVH):
                        sl = h % 2
                        if ps1_reader[0]:
                            TE.wait_ge(ams.sem, ps1_reader[0])
                        for i in range(c.NI):
                            last = TE.matmul(PS[1][:, :], wkb[:, i, h * 128:(h + 1) * 128], y2b[:, i, :],
                                             start=(i == 0), stop=(i == c.NI - 1))
                        tt = tms.inc(last)
                        SC.wait_ge(tms.sem, tt)
                        if ("k", sl) in kv_snap:
                            g_ks.wait_at(SC, kv_snap[("k", sl)])
                        if ("ksq", sl) in kv_snap:
                            SC.wait_ge(vms.sem, kv_snap[("ksq", sl)])
                        aa2 = ams.inc(SC.copy(kev[sl][:], PS[1][:, :]))
                        ps1_reader[0] = aa2
                        GP.wait_ge(ams.sem, aa2)
                        off_k = h * 128 * c.NTOK
                        kv_snap[("k", sl)] = g_ks.put(
                            kvin_flat[off_k:off_k + 128 * c.NTOK].rearrange("(p t) -> p t", p=128)[:, t0:t0 + 512],
                            kev[sl][:])
                        VE.wait_ge(ams.sem, aa2)
                        if h >= 1:
                            VE.wait_ge(tms.sem, kv_snap[("knmm", 0)])
                        v1 = vms.inc(VE.tensor_mul(ksq[:], kev[sl][:], kev[sl][:]))
                        kv_snap[("ksq", sl)] = v1
                        TE.wait_ge(vms.sem, v1)
                        tt = tms.inc(TE.matmul(PS[3][0:1, :], sb_ones_c[:], ksq[:], start=True, stop=True))
                        kv_snap[("knmm", 0)] = tt
                        SC.wait_ge(tms.sem, tt)
                        if "kn" in kv_snap:
                            g_kn.wait_at(SC, kv_snap["kn"])
                        aa_kn = ams.inc(SC.copy(knt[:], PS[3][0:1, :]))
                        GP.wait_ge(ams.sem, aa_kn)
                        off_kn = 2 * kblk + h * c.NTOK + t0
                        kv_snap["kn"] = g_kn.put(
                            kvin_flat[off_kn:off_kn + 512].rearrange("(a b) -> a b", a=1), knt[:])
                    for u in range(4):
                        for hf in range(c.VH):
                            sl = (u * c.VH + hf) % 2
                            if ps1_reader[0]:
                                TE.wait_ge(ams.sem, ps1_reader[0])
                            for i in range(c.NI):
                                last = TE.matmul(PS[1][:, 0:c.VW], y2b[:, i, u * 128:(u + 1) * 128],
                                                 wvb[:, i, hf * c.VW:(hf + 1) * c.VW],
                                                 start=(i == 0), stop=(i == c.NI - 1))
                            tt = tms.inc(last)
                            SC.wait_ge(tms.sem, tt)
                            if ("v", sl) in kv_snap:
                                g_vs.wait_at(SC, kv_snap[("v", sl)])
                            aa2 = ams.inc(SC.copy(vev[sl][:], PS[1][:, 0:c.VW]))
                            ps1_reader[0] = aa2
                            GP.wait_ge(ams.sem, aa2)
                            for hh in range(c.VW // 128):
                                h = hf * (c.VW // 128) + hh
                                off_v = kblk + h * c.NTOK * 128 + (tb * 4 + u) * 128 * 128
                                kv_snap[("v", sl)] = g_vs.put(
                                    kvin_flat[off_v:off_v + 16384].rearrange("(p q) -> p q", p=128),
                                    vev[sl][:, hh * 128:(hh + 1) * 128], rot=(hh == 0))
                g_ks.wait(GP)
                g_vs.wait(GP)
                g_kn.wait(GP)
                ccs.inc(GP.collective_compute(
                    "AllGather", ALU.bypass,
                    replica_groups=[[0, 1], [2, 3], [4, 5], [6, 7]],
                    ins=[t_kvin.ap().opt()], outs=[t_kvout.ap().opt()]))
                ccs.wait(SY)
                vstep(VE.memset(sb_kmax2[:], 0.0))
                for rank in range(2):
                    for h in range(c.KVH):
                        vms.wait(SY)
                        off_kn = rank * cc_elems + 2 * kblk + h * c.NTOK
                        s_g.put(kngbuf[:], kvout_flat[off_kn:off_kn + c.NTOK].rearrange("(a b) -> a b", a=1))
                        s_g.wait_put(VE)
                        vstep(VE.tensor_reduce(sb_tiny[0:1, 20:21], kngbuf[:], AX.X, ALU.max))
                        vstep(VE.tensor_tensor(sb_kmax2[0:1, h:h + 1], sb_kmax2[0:1, h:h + 1],
                                               sb_tiny[0:1, 20:21], ALU.max))
                        VE.sem_inc(vms.sem, 1)
                        vms.n += 1
            barrier()

            wqb = [es.enter_context(nc.sbuf_tensor(f"a_wq{i}", [128, c.NI, 128], BF16)) for i in range(2)]
            qsb = [es.enter_context(nc.sbuf_tensor(f"a_q{i}", [128, 512], BF16)) for i in range(2)]
            qsq = es.enter_context(nc.sbuf_tensor("a_qsq", [128, 512], BF16))
            kch = [[es.enter_context(nc.sbuf_tensor(f"a_k{p}{i}", [128, 512], BF16)) for i in range(2)]
                   for p in range(2)]
            vch = [[es.enter_context(nc.sbuf_tensor(f"a_v{p}{i}", [128, 4, 128], BF16)) for i in range(2)]
                   for p in range(2)]
            pbuf = [es.enter_context(nc.sbuf_tensor(f"a_p{i}", [128, 512], BF16)) for i in range(3)]
            negm = [es.enter_context(nc.sbuf_tensor(f"a_negm{i}", [128, 1], F32)) for i in range(2)]
            recb = es.enter_context(nc.sbuf_tensor("a_recb", [128, 512], F32))
            ost = es.enter_context(nc.sbuf_tensor("a_ost", [128, c.NQH, 512], BF16))
            wwo = [es.enter_context(nc.sbuf_tensor(f"a_wwo{i}", [128, c.NQH, 128], BF16)) for i in range(2)]
            xrb = [es.enter_context(nc.sbuf_tensor(f"a_xr{i}", [128, 512], F32)) for i in range(2)]
            res = [es.enter_context(nc.sbuf_tensor(f"a_res{i}", [128, 512], F32)) for i in range(2)]

            def k_chunk_ap(rank, h, cq):
                base = rank * cc_elems + h * 128 * c.NTOK
                return kvout_flat[base:base + 128 * c.NTOK].rearrange(
                    "(p t) -> p t", p=128)[:, cq * 512:(cq + 1) * 512]

            def v_chunk_ap(rank, h, cq):
                base = rank * cc_elems + kblk + h * c.NTOK * 128 + cq * 512 * 128
                return kvout_flat[base:base + 512 * 128].rearrange(
                    "(a p q) -> p a q", p=128, a=4)

            s_y2w.wait(SY)
            c_s.wait(SY)
            wo_last_cnt = 0
            wo_st = []
            qp_done = {}
            qn_read = {}
            qev_map = {}
            tqn_map = {}
            lastS_map = {}
            lastRS_map = {}
            vrec_map = {}
            onorm_map = {}
            obank = lambda hg: 5 if hg % 2 == 0 else 7
            for qb in range(c.QBN):
                i0 = qb * c.QBTILE
                E = 2 * (i0 + c.QBTILE)
                NCH = cdiv(E, 8)
                tms.wait(SY)
                np4 = max(1, c.NI // 4)
                for part in range(cdiv(c.NI, np4)):
                    p0, p1 = part * np4, min((part + 1) * np4, c.NI)
                    s_y.put(y2b[:, p0:p1, :],
                            d_y2[p0:p1, :, qb * 512:(qb + 1) * 512].rearrange("a p t -> p a t"),
                            rot=(part == 0))

                def bound_stages(hg2):
                    h2 = hg2 // c.G
                    sl2 = hg2 % 2
                    key = (qb, hg2)

                    def s0():
                        pk = (qb, hg2 - 2) if hg2 >= 2 else (qb - 1, c.NQH - 2 + hg2)
                        if pk in qp_done:
                            GP.wait_ge(tms.sem, qp_done[pk])
                        g_wq.put(wqb[sl2][:], i_wq[hg2])
                        g_wq.wait_put(TE)
                        ams.wait(TE)
                        pk2 = (qb, hg2 - 1) if hg2 >= 1 else (qb - 1, c.NQH - 1)
                        TE.wait_ge(vms.sem, qn_read.get(pk2, 0))
                        sk = (qb, hg2 - 2) if hg2 >= 2 else (qb - 1, c.NQH - 2 + hg2)
                        if sk in lastS_map:
                            TE.wait_ge(tms.sem, lastS_map[sk])
                        s_y.wait_put(TE)
                        for i in range(c.NI):
                            last = TE.matmul(PS[0][:, :], wqb[sl2][:, i, :], y2b[:, i, :],
                                             start=(i == 0), stop=(i == c.NI - 1))
                        qp_done[key] = tms.inc(last)

                    def s1():
                        SC.wait_ge(tms.sem, qp_done[key])
                        qev_map[key] = ams.inc(SC.activation(qsb[sl2][:], PS[0][:, :], AF.Copy, scale=ISQ_HD))

                    def s2():
                        VE.wait_ge(ams.sem, qev_map[key])
                        pk2 = (qb, hg2 - 1) if hg2 >= 1 else (qb - 1, c.NQH - 1)
                        if pk2 in tqn_map:
                            VE.wait_ge(tms.sem, tqn_map[pk2])
                        qn_read[("v1", key)] = vms.inc(VE.tensor_mul(qsq[:], qsb[sl2][:], qsb[sl2][:]))

                    def s3():
                        TE.wait_ge(vms.sem, qn_read[("v1", key)])
                        tqn_map[key] = tms.inc(TE.matmul(PS[0][0:1, :], sb_ones_c[:], qsq[:],
                                                         start=True, stop=True))

                    def s4():
                        VE.wait_ge(tms.sem, tqn_map[key])
                        vstep(VE.tensor_reduce(sb_tiny[0:1, 24:25], PS[0][0:1, :], AX.X, ALU.max))
                        vms.inc(VE.tensor_mul(sb_tiny[0:1, 25:26], sb_tiny[0:1, 24:25],
                                              sb_kmax2[0:1, h2:h2 + 1]))
                        qn_read[key] = vms.n

                    def s5():
                        SC.wait_ge(vms.sem, qn_read[key])
                        astep(SC.activation(sb_tiny[0:1, 26:27], sb_tiny[0:1, 25:26], AF.Ln))
                        qn_read[("a2", key)] = ams.inc(SC.activation(sb_tiny[0:1, 27:28],
                                                                     sb_tiny[0:1, 26:27], AF.Exp, scale=0.5))

                    def s6():
                        VE.wait_ge(ams.sem, qn_read[("a2", key)])
                        qn_read[("v3", key)] = vms.inc(VE.tensor_scalar(
                            sb_rowb[0:1, 2047:2048], sb_tiny[0:1, 27:28], -1.0, None, ALU.mult))

                    def s7():
                        TE.wait_ge(vms.sem, qn_read[("v3", key)])
                        qn_read[("t3", key)] = tms.inc(TE.matmul(
                            PS[0][:, 0:1], sb_ones_r[:], sb_rowb[0:1, 2047:2048], start=True, stop=True))

                    def s8():
                        SC.wait_ge(tms.sem, qn_read[("t3", key)])
                        astep(SC.copy(negm[sl2][:], PS[0][:, 0:1]))
                        SC.sem_inc(ams.sem, 1)
                        ams.n += 1

                    return [s0, s1, s2, s3, s4, s5, s6, s7, s8]

                def tail_stages(hgp):
                    key = (qb, hgp)
                    ob = obank(hgp)

                    def t0():
                        VE.wait_ge(tms.sem, lastRS_map[key])
                        vstep(VE.reciprocal(sb_row[0:1, 1024:1536], PS[4][0:1, :]))
                        VE.sem_inc(vms.sem, 1)
                        vms.n += 1
                        vrec_map[key] = vms.n
                        vstep(VE.tensor_copy(sb_rowb[0:1, 0:512], sb_row[0:1, 1024:1536]))
                        qn_read[("v4", key)] = vms.inc(VE.tensor_sub(
                            sb_rowb[0:1, 512:1024], sb_row[0:1, 1024:1536], sb_rowb[0:1, 0:512]))

                    def t1():
                        TE.wait_ge(vms.sem, qn_read[("v4", key)])
                        TE.matmul(PS[6][:, :], sb_ones_r[:], sb_rowb[0:1, 0:512], start=True, stop=False)
                        qn_read[("t5", key)] = tms.inc(TE.matmul(
                            PS[6][:, :], sb_ones_r[:], sb_rowb[0:1, 512:1024], start=False, stop=True))

                    def t2():
                        SC.wait_ge(tms.sem, qn_read[("t5", key)])
                        pk = (qb, hgp - 1) if hgp >= 1 else (qb - 1, c.NQH - 1)
                        SC.wait_ge(vms.sem, onorm_map.get(pk, 0))
                        qn_read[("a5", key)] = ams.inc(SC.copy(recb[:], PS[6][:, :]))

                    def t3s():
                        VE.wait_ge(ams.sem, qn_read[("a5", key)])
                        if hgp == 0 and qb > 0:
                            VE.wait_ge(tms.sem, wo_last_cnt)
                        onorm_map[key] = vms.inc(VE.tensor_tensor(
                            ost[:, hgp, :], PS[ob][:, :], recb[:], ALU.mult))

                    return [t0, t1, t2, t3s]

                for st in bound_stages(0):
                    st()
                for hg in range(c.NQH):
                    h = hg // c.G
                    sl = hg % 2
                    ob = obank(hg)
                    pending = []
                    if hg >= 1:
                        pending += tail_stages(hg - 1)
                    if hg + 1 < c.NQH:
                        pending += bound_stages(hg + 1)
                    last_S = {}
                    last_exp = {}
                    last_rs = {}
                    ch_lastS = {}
                    ch_lastRS = {}
                    ksnap = {}
                    vsnap = {}
                    s_k.align(4)
                    s_v.align(4)
                    prev_key = (qb, hg - 1) if hg >= 1 else (qb - 1, c.NQH - 1)
                    prevRS = lastRS_map.get(prev_key, 0)

                    def load_kv_chunk(p, cq):
                        cb = cq % 2
                        if (p, cq - 2) in ch_lastRS:
                            SY.wait_ge(tms.sem, ch_lastRS[(p, cq - 2)])
                        elif prevRS:
                            SY.wait_ge(tms.sem, prevRS)
                        ksnap[(p, cq)] = s_k.put(kch[p][cb][:], k_chunk_ap(p, h, cq))
                        vsnap[(p, cq)] = s_v.put(vch[p][cb][:], v_chunk_ap(p, h, cq))

                    def emit_S(j):
                        sfx = max(0, cdiv(j - 1, 2) - i0) if j >= 1 else 0
                        p, jj = j % 2, j // 2
                        cq, a = jj // 4, jj % 4
                        bank = 1 + j % 3
                        if j >= 3:
                            TE.wait_ge(ams.sem, last_exp[j - 3])
                        if a == 0:
                            s_k.wait_at(TE, ksnap[(p, cq)])
                        last = TE.matmul(PS[bank][:, 128 * sfx:512], kch[p][cq % 2][:, a * 128:(a + 1) * 128],
                                         qsb[sl][:, 128 * sfx:512], start=True, stop=True)
                        last_S[j] = tms.inc(last)
                        ch_lastS[(p, cq)] = last_S[j]
                        return sfx

                    load_kv_chunk(0, 0)
                    load_kv_chunk(1, 0)
                    if NCH > 1:
                        load_kv_chunk(0, 1)
                        load_kv_chunk(1, 1)
                    geom = {0: emit_S(0)}
                    if E > 1:
                        geom[1] = emit_S(1)
                    for j in range(E):
                        p, jj = j % 2, j // 2
                        bank = 1 + j % 3
                        sfx = geom[j]
                        if j + 2 < E:
                            geom[j + 2] = emit_S(j + 2)
                        if pending:
                            pending.pop(0)()
                        masked = (j >= 2 * i0)
                        if masked:
                            VE.wait_ge(tms.sem, last_S[j])
                            mk = 0 if (j % 2 == 0) else 1
                            mv = vms.inc(VE.tensor_tensor(PS[bank][:, 128 * sfx:128 * sfx + 128],
                                                          PS[bank][:, 128 * sfx:128 * sfx + 128],
                                                          sb_masks[:, mk, :], ALU.add))
                            SC.wait_ge(vms.sem, mv)
                        else:
                            SC.wait_ge(tms.sem, last_S[j])
                        if j >= 3:
                            SC.wait_ge(tms.sem, last_rs[j - 3])
                        elif prevRS:
                            SC.wait_ge(tms.sem, prevRS)
                        last_exp[j] = ams.inc(SC.activation(pbuf[j % 3][:, 128 * sfx:512],
                                                            PS[bank][:, 128 * sfx:512],
                                                            AF.Exp, bias=negm[sl][:]))
                        cq, a = jj // 4, jj % 4
                        if a == 0:
                            s_v.wait_at(TE, vsnap[(p, cq)])
                        TE.wait_ge(ams.sem, last_exp[j])
                        if j == 0:
                            pk2 = (qb, hg - 2) if hg >= 2 else (qb - 1, c.NQH - 2 + hg)
                            TE.wait_ge(vms.sem, onorm_map.get(pk2, 0))
                            TE.wait_ge(vms.sem, vrec_map.get(prev_key, 0))
                        TE.matmul(PS[ob][:, 128 * sfx:512], vch[p][cq % 2][:, a, :],
                                  pbuf[j % 3][:, 128 * sfx:512], start=(j == 0), stop=(j == E - 1))
                        rs = TE.matmul(PS[4][0:1, 128 * sfx:512], sb_ones_c[:],
                                       pbuf[j % 3][:, 128 * sfx:512], start=(j == 0), stop=(j == E - 1))
                        last_rs[j] = tms.inc(rs)
                        ch_lastRS[(p, jj // 4)] = last_rs[j]
                        if jj % 4 == 3 and p == 1 and (jj // 4) + 2 < NCH:
                            load_kv_chunk(0, jj // 4 + 2)
                            load_kv_chunk(1, jj // 4 + 2)
                    for st in pending:
                        st()
                    lastS_map[(qb, hg)] = last_S[E - 1]
                    lastRS_map[(qb, hg)] = last_rs[E - 1]
                for st in tail_stages(c.NQH - 1):
                    st()
                onorm = onorm_map[(qb, c.NQH - 1)]
                stt_hist = []
                wo_hist = []
                for o in range(c.NI):
                    sl = o % 2
                    if len(wo_hist) >= 2:
                        GP.wait_ge(tms.sem, wo_hist[-2])
                    elif wo_last_cnt:
                        GP.wait_ge(tms.sem, wo_last_cnt)
                    g_wwo.put(wwo[sl][:], i_wo[o])
                    if o >= 2:
                        SY.wait_ge(vms.sem, stt_hist[-2])
                    elif qb > 0:
                        vms.wait(SY)
                    s_r.put(xrb[sl][:], d_x2[o * 128:(o + 1) * 128, qb * 512:(qb + 1) * 512])
                    g_wwo.wait_put(TE)
                    if o == 0:
                        TE.wait_ge(vms.sem, onorm)
                    pb = 6 + (o % 2)
                    if o >= 2:
                        TE.wait_ge(vms.sem, stt_hist[-2])
                    for hd in range(c.NQH):
                        last = TE.matmul(PS[pb][:, :], wwo[sl][:, hd, :], ost[:, hd, :],
                                         start=(hd == 0), stop=(hd == c.NQH - 1))
                    wo_last_cnt = tms.inc(last)
                    wo_hist.append(wo_last_cnt)
                    VE.wait_ge(tms.sem, wo_last_cnt)
                    s_r.wait_put(VE)
                    if len(wo_st) >= 2:
                        c_s.wait_at(VE, wo_st[-2])
                    vv = vms.inc(VE.scalar_tensor_tensor(res[sl][:], PS[pb][:, :], 1.0,
                                                         xrb[sl][:], ALU.mult, ALU.add))
                    stt_hist.append(vv)
                    SC.wait_ge(vms.sem, vv)
                    wo_st.append(c_s.put(d_x3[o * 128:(o + 1) * 128, qb * 512:(qb + 1) * 512], res[sl][:]))
            c_s.wait(SC)
        return

    ffn_phase("f1", i_x, d_x2, t_ys[0], t_yso[0])
    barrier()
    attn_phase()
    barrier()
    ffn_phase("f3", d_x3, o_out, t_ys[1], t_yso[1])
    c_s.wait(SC)
    g_s.wait(GP)
    s_y2w.wait(SC)
    barrier()
    return nc


def _host_prep(inputs, cfg: CFG):
    c = cfg
    x = np.asarray(inputs["L0_attn_x"], dtype=np.float32)
    Wq = np.asarray(inputs["L0_attn_Wq"], dtype=np.float32)
    Wk = np.asarray(inputs["L0_attn_Wk"], dtype=np.float32)
    Wv = np.asarray(inputs["L0_attn_Wv"], dtype=np.float32)
    Wo = np.asarray(inputs["L0_attn_Wo"], dtype=np.float32)
    Worg = np.asarray(inputs["L0_ffn_W_org"], dtype=np.float32)
    down = np.asarray(inputs["L0_ffn_lora_down"], dtype=np.float32)
    up = np.asarray(inputs["L0_ffn_lora_up"], dtype=np.float32)
    keys = np.asarray(inputs["L0_ffn_lora_keys"], dtype=np.float32)
    Wqf = np.asarray(inputs["L0_ffn_Wq"], dtype=np.float32)

    NI = c.NI

    def blk_lhsT(wT):
        K, M = wT.shape
        return np.ascontiguousarray(
            wT.reshape(K // 128, 128, M // 128, 128).transpose(2, 1, 0, 3))

    def blk_rhs(wT):
        K, N = wT.shape
        return np.ascontiguousarray(wT.reshape(K // 128, 128, N).transpose(1, 0, 2))

    worg4 = blk_lhsT(Worg.T)
    wq4 = blk_lhsT(Wq.T)
    wo4 = blk_lhsT(Wo.T)
    wk4 = blk_rhs(Wk.T)
    wv4 = blk_rhs(Wv.T)
    downp = np.concatenate([down[0::2], down[1::2]], axis=2)
    down4 = np.ascontiguousarray(
        downp.reshape(c.NP, NI, 128, 128).transpose(0, 2, 1, 3))
    uppair = np.stack([np.concatenate([up[2 * p], up[2 * p + 1]], axis=0)
                       for p in range(c.NP)])
    up4 = np.ascontiguousarray(
        uppair.reshape(c.NP, 128, NI, 128).transpose(2, 1, 0, 3))
    wqf4 = blk_rhs(Wqf.T)
    wqf4 = np.ascontiguousarray(Wqf.T.reshape(NI, 128, 128).transpose(1, 0, 2))
    keysT = np.ascontiguousarray(keys.T)

    import ml_dtypes
    kl = np.arange(128)[:, None]
    ql = np.arange(128)[None, :]
    bf = ml_dtypes.bfloat16
    tri = np.where(kl <= ql, 0.0, NEG).astype(bf)
    masks = [np.stack([tri, np.full((128, 128), NEG, bf)]),
             np.stack([np.zeros((128, 128), bf), tri])]

    in_maps = []
    for core in range(8):
        b, par = core // 2, core % 2
        sel = x[b].reshape(2 * c.NTILE, 128, c.D)[par::2].reshape(c.NTOK, c.D)
        xT = np.ascontiguousarray(sel.T)
        in_maps.append({
            "x": xT, "masks": masks[par], "worg4": worg4, "wq4": wq4,
            "wo4": wo4, "wk4": wk4, "wv4": wv4, "down4": down4, "up4": up4,
            "wqf4": wqf4, "keysT": keysT,
        })
    return in_maps


def _host_assemble(results, cfg: CFG):
    c = cfg
    out = np.empty((4, c.SEQ, c.D), dtype=np.float32)
    for core in range(8):
        b, par = core // 2, core % 2
        oT = np.asarray(results[core]["out"])
        rows = np.ascontiguousarray(oT.T)
        out[b].reshape(2 * c.NTILE, 128, c.D)[par::2] = rows.reshape(c.NTILE, 128, c.D)
    return out


def kernel(**inputs):
    cfg = CFG(mini=False)
    in_maps = _host_prep(inputs, cfg)
    nc = build(cfg)
    res = run_bass_kernel_spmd(nc, in_maps, core_ids=list(range(8)))
    return _host_assemble(res.results, cfg)


# version 08
# speedup vs baseline: 1.1672x; 1.1672x over previous
import math
import contextlib
import numpy as np

import concourse.bass as bass
import concourse.mybir as mybir
from concourse.bass_utils import run_bass_kernel_spmd

F32 = mybir.dt.float32
BF16 = mybir.dt.bfloat16
AF = mybir.ActivationFunctionType
ALU = mybir.AluOpType
AX = mybir.AxisListType

NEG = -1.0e30


class CFG:
    def __init__(self, mini=False, midi=False):
        if midi:
            self.D = 1024
            self.NTOK = 2048
            self.KVH = 8
            self.G = 1
        elif mini:
            self.D = 512
            self.NTOK = 512
            self.KVH = 2
            self.G = 2
        else:
            self.D = 4096
            self.NTOK = 2048
            self.KVH = 8
            self.G = 4
        self.HD = 128
        self.NI = self.D // 128
        self.NTILE = self.NTOK // 128
        self.QBN = max(1, self.NTOK // 512)
        self.QBT = self.NTOK // self.QBN
        self.QBTILE = self.QBT // 128
        self.TBN = max(1, self.NTOK // 1024)
        self.TBT = self.NTOK // self.TBN
        self.NH = self.TBT // 512
        self.E = 8
        self.R = 64
        self.LK = 128
        self.NP = 4
        self.KVDIM = self.KVH * self.HD
        self.NQH = self.KVH * self.G
        self.VH = max(1, self.KVDIM // 512)
        self.VW = self.KVDIM // self.VH
        self.SEQ = 2 * self.NTOK


class Ctr:
    def __init__(self, nc, name):
        self.sem = nc.alloc_semaphore(name)
        self.n = 0

    def inc(self, instr, k=1):
        instr.then_inc(self.sem, k)
        self.n += k
        return self.n

    def wait(self, eng, val=None):
        eng.wait_ge(self.sem, self.n if val is None else val)


class Stream:

    def __init__(self, nc, name, eng, nsem=2):
        self.sems = [nc.alloc_semaphore(f"{name}{i}") for i in range(nsem)]
        self.cnts = [0] * nsem
        self.eng = eng
        self.nsem = nsem
        self.ng = 0

    def put(self, out_ap, in_ap, rot=True, **kw):
        if rot:
            self.ng += 1
        k = self.ng % self.nsem
        self.eng.dma_start(out_ap, in_ap, **kw).then_inc(self.sems[k], 16)
        self.cnts[k] += 16
        return (k, self.cnts[k])

    def snap(self):
        k = self.ng % self.nsem
        return (k, self.cnts[k])

    def wait(self, eng):
        for k in range(self.nsem):
            if self.cnts[k]:
                eng.wait_ge(self.sems[k], self.cnts[k])

    def wait_put(self, eng):
        k = self.ng % self.nsem
        eng.wait_ge(self.sems[k], self.cnts[k])

    def wait_at(self, eng, sn):
        eng.wait_ge(self.sems[sn[0]], sn[1])

    def align(self, k):
        self.ng += (-self.ng) % k


def cdiv(a, b):
    return (a + b - 1) // b


def build(cfg: CFG):
    c = cfg
    nc = bass.Bass("TRN2", target_bir_lowering=False, debug=False, num_devices=8)

    i_x = nc.dram_tensor("x", [c.D, c.NTOK], F32, kind="ExternalInput").ap()
    i_masks = nc.dram_tensor("masks", [2, 128, 128], BF16, kind="ExternalInput").ap()
    i_worg = nc.dram_tensor("worg4", [c.NI, 128, c.NI, 128], F32, kind="ExternalInput").ap()
    i_wq = nc.dram_tensor("wq4", [c.NQH, 128, c.NI, 128], F32, kind="ExternalInput").ap()
    i_wo = nc.dram_tensor("wo4", [c.NI, 128, c.NQH, 128], F32, kind="ExternalInput").ap()
    i_wk = nc.dram_tensor("wk4", [128, c.NI, c.KVDIM], F32, kind="ExternalInput").ap()
    i_wv = nc.dram_tensor("wv4", [128, c.NI, c.KVDIM], F32, kind="ExternalInput").ap()
    i_down = nc.dram_tensor("down4", [c.NP, 128, c.NI, 128], F32, kind="ExternalInput").ap()
    i_up = nc.dram_tensor("up4", [c.NI, 128, c.NP, 128], F32, kind="ExternalInput").ap()
    i_wqf = nc.dram_tensor("wqf4", [128, c.NI, 128], F32, kind="ExternalInput").ap()
    i_keys = nc.dram_tensor("keysT", [c.LK, c.E], F32, kind="ExternalInput").ap()
    o_out = nc.dram_tensor("out", [c.D, c.NTOK], F32, kind="ExternalOutput").ap()

    d_x2 = nc.dram_tensor("x2d", [c.D, c.NTOK], F32, kind="Internal").ap()
    d_x3 = nc.dram_tensor("x3d", [c.D, c.NTOK], F32, kind="Internal").ap()
    d_y2 = nc.dram_tensor("y2d", [c.NI, 128, c.NTOK], BF16, kind="Internal").ap()
    kblk = c.KVH * c.NTILE * 128 * 128
    knorm_elems = c.KVH * c.NTOK
    cc_rows = cdiv(2 * kblk + knorm_elems, 2048)
    cc_elems = cc_rows * 2048
    t_kvin = nc.dram_tensor("kvin", [cc_rows, 2048], BF16, kind="Internal")
    t_kvout = nc.dram_tensor("kvout", [2 * cc_rows, 2048], BF16, kind="Internal")
    kvin_flat = t_kvin.ap().rearrange("a b -> (a b)")
    kvout_flat = t_kvout.ap().rearrange("a b -> (a b)")
    t_ys = [nc.dram_tensor(f"ysi{i}", [128, c.NI], F32, kind="Internal") for i in range(2)]
    t_yso = [nc.dram_tensor(f"yso{i}", [256, c.NI], F32, kind="Internal") for i in range(2)]

    def k_tile_in(h, j_loc):
        off = (h * c.NTILE + j_loc) * 16384
        return kvin_flat[off:off + 16384].rearrange("(p q) -> p q", p=128)

    def v_tile_in(h, j_loc):
        off = kblk + (j_loc * c.KVH + h) * 16384
        return kvin_flat[off:off + 16384].rearrange("(p q) -> p q", p=128)

    def knorm_in():
        off = 2 * kblk
        return kvin_flat[off:off + knorm_elems].rearrange("(a b) -> a b", a=1)

    def k_tile_out(rank, h, j_loc):
        off = rank * cc_elems + (h * c.NTILE + j_loc) * 16384
        return kvout_flat[off:off + 16384].rearrange("(p q) -> p q", p=128)

    def v_tile_out(rank, h, j_loc):
        off = rank * cc_elems + kblk + (j_loc * c.KVH + h) * 16384
        return kvout_flat[off:off + 16384].rearrange("(p q) -> p q", p=128)

    def knorm_out(rank):
        off = rank * cc_elems + 2 * kblk
        return kvout_flat[off:off + knorm_elems].rearrange("(a b) -> a b", a=1)

    SY, GP, SC, VE, TE = nc.sync, nc.gpsimd, nc.scalar, nc.vector, nc.tensor
    tms = Ctr(nc, "tms")
    vms = Ctr(nc, "vms")
    ams = Ctr(nc, "ams")
    vch = Ctr(nc, "vch")
    ach = Ctr(nc, "ach")
    gch = Ctr(nc, "gch")
    gms = Ctr(nc, "gms")

    def vstep(instr):
        vch.inc(instr)
        VE.wait_ge(vch.sem, vch.n)
        return instr

    def astep(instr):
        ach.inc(instr)
        SC.wait_ge(ach.sem, ach.n)
        return instr

    def gstep(instr):
        gch.inc(instr)
        GP.wait_ge(gch.sem, gch.n)
        return instr
    ccs = Ctr(nc, "ccs")
    bar = Ctr(nc, "bar")

    s_x = Stream(nc, "s_x", SY)
    s_n = Stream(nc, "s_n", SY, nsem=3)
    s_r = Stream(nc, "s_r", SY)
    s_y = Stream(nc, "s_y", SY)
    s_k = Stream(nc, "s_k", SY, nsem=4)
    s_v = Stream(nc, "s_v", SY, nsem=4)
    s_g = Stream(nc, "s_g", SY)
    g_x = Stream(nc, "g_x", GP)
    g_dpb = Stream(nc, "g_dpb", GP, nsem=1)
    g_wob = Stream(nc, "g_wob", GP)
    g_upb = Stream(nc, "g_upb", GP)
    g_wq = Stream(nc, "g_wq", GP)
    g_wwo = Stream(nc, "g_wwo", GP)
    g_misc = Stream(nc, "g_misc", GP, nsem=1)
    g_s = Stream(nc, "g_s", GP, nsem=1)
    g_kn = Stream(nc, "g_kn", GP, nsem=1)
    g_ks = Stream(nc, "g_ks", GP)
    g_vs = Stream(nc, "g_vs", GP)
    c_s = Stream(nc, "c_s", SC)
    s_y2w = Stream(nc, "s_y2w", SC)

    def barrier():
        for e in (TE, VE, SC, GP, SY):
            e.sem_inc(bar.sem, 1)
        bar.n += 5
        for e in (TE, VE, SC, GP, SY):
            e.wait_ge(bar.sem, bar.n)

    sb_keys = nc.alloc_sbuf_tensor("sb_keys", [c.LK, c.E], BF16)
    sb_masks = nc.alloc_sbuf_tensor("sb_masks", [128, 2, 128], BF16)
    sb_ones_r = nc.alloc_sbuf_tensor("sb_ones_r", [1, 128], BF16)
    sb_ones_c = nc.alloc_sbuf_tensor("sb_ones_c", [128, 1], BF16)
    sb_coefv = nc.alloc_sbuf_tensor("sb_coefv", [128, c.NP], F32)
    sb_kmax2 = nc.alloc_sbuf_tensor("sb_kmax2", [1, c.KVH], F32)
    sb_tiny = nc.alloc_sbuf_tensor("sb_tiny", [128, 32], F32)
    sb_row = nc.alloc_sbuf_tensor("sb_row", [1, 1536], F32)
    sb_rowb = nc.alloc_sbuf_tensor("sb_rowb", [1, 2048], BF16)
    sb_qcol = nc.alloc_sbuf_tensor("sb_qcol", [128, 1], BF16)

    PS = [nc.alloc_psum_tensor(f"ps{i}", [128, 512], F32) for i in range(8)]

    VE.memset(sb_ones_r[:], 1.0)
    VE.memset(sb_ones_c[:], 1.0)
    vms.inc(VE.memset(sb_coefv[:], 0.0))
    s_g.put(sb_masks[:, 0, :], i_masks[0])
    s_g.put(sb_masks[:, 1, :], i_masks[1], rot=False)
    g_misc.put(sb_keys[:], i_keys[:])

    ISQ_HD = 1.0 / math.sqrt(c.HD)
    ISQ_LK = 1.0 / math.sqrt(c.LK)

    R_M = 0

    def ln_stats_512(x_dst_rstdb, x_dst_mrb, ps_sum, ps_sq, col0):
        m_row = sb_row[0:1, 0:512]
        v_row = sb_row[0:1, 512:1024]
        t_row = sb_row[0:1, 1024:1536]
        tms.wait(VE)
        VE.tensor_scalar(m_row, ps_sum[0:1, :], 1.0 / c.D, None, ALU.mult)
        vstep(VE.tensor_scalar(v_row, ps_sq[0:1, :], 1.0 / c.D, None, ALU.mult))
        vstep(VE.tensor_mul(t_row, m_row, m_row))
        vv = vms.inc(VE.scalar_tensor_tensor(v_row, v_row, 1e-5, t_row, ALU.add, ALU.subtract))
        SC.wait_ge(vms.sem, vv)
        astep(SC.activation(t_row, v_row, AF.Ln))
        aa = ams.inc(SC.activation(v_row, t_row, AF.Exp, scale=-0.5))
        VE.wait_ge(ams.sem, aa)
        vstep(VE.tensor_mul(m_row, m_row, v_row))
        vstep(VE.tensor_copy(sb_rowb[0:1, 0:512], v_row))
        VE.tensor_sub(sb_rowb[0:1, 512:1024], v_row, sb_rowb[0:1, 0:512])
        vstep(VE.tensor_copy(sb_rowb[0:1, 1024:1536], m_row))
        vv = vms.inc(VE.tensor_sub(sb_rowb[0:1, 1536:2048], m_row, sb_rowb[0:1, 1024:1536]))
        TE.wait_ge(vms.sem, vv)
        TE.matmul(PS[4][:, :], sb_ones_r[:], sb_rowb[0:1, 0:512], start=True, stop=False)
        TE.matmul(PS[4][:, :], sb_ones_r[:], sb_rowb[0:1, 512:1024], start=False, stop=True)
        TE.matmul(PS[5][:, :], sb_ones_r[:], sb_rowb[0:1, 1024:1536], start=True, stop=False)
        tt = tms.inc(TE.matmul(PS[5][:, :], sb_ones_r[:], sb_rowb[0:1, 1536:2048], start=False, stop=True))
        SC.wait_ge(tms.sem, tt)
        SC.copy(x_dst_rstdb[:, col0:col0 + 512], PS[4][:, :])
        return ams.inc(SC.copy(x_dst_mrb[:, col0:col0 + 512], PS[5][:, :]))

    def ffn_phase(tag, x_src, x_dst, ysi, yso):
        es = contextlib.ExitStack()
        with es:
            y1 = es.enter_context(nc.sbuf_tensor(f"{tag}_y1", [128, c.NI, c.NTOK], BF16))
            xb = [es.enter_context(nc.sbuf_tensor(f"{tag}_xb{i}", [128, c.TBT], F32)) for i in range(2)]
            xc2 = [es.enter_context(nc.sbuf_tensor(f"{tag}_xc{i}", [128, c.TBT], BF16)) for i in range(2)]
            sq2 = [es.enter_context(nc.sbuf_tensor(f"{tag}_sq0", [128, c.TBT], BF16))]
            rstdb = es.enter_context(nc.sbuf_tensor(f"{tag}_rstdb", [128, c.TBT], F32))
            mrb = es.enter_context(nc.sbuf_tensor(f"{tag}_mrb", [128, c.TBT], F32))
            ysum = es.enter_context(nc.sbuf_tensor(f"{tag}_ysum", [128, c.NI], F32))
            xn2 = es.enter_context(nc.sbuf_tensor(f"{tag}_xn2", [128, c.TBT], F32))
            ysg = es.enter_context(nc.sbuf_tensor(f"{tag}_ysg", [128, 2 * c.NI], F32))
            ysb = es.enter_context(nc.sbuf_tensor(f"{tag}_ysb", [128, c.NI], BF16))
            hsb = es.enter_context(nc.sbuf_tensor(f"{tag}_h", [128, c.NP, c.NTOK], BF16))
            dpb = es.enter_context(nc.sbuf_tensor(f"{tag}_dp", [128, c.NI, 128], BF16))
            wob = [es.enter_context(nc.sbuf_tensor(f"{tag}_wo{i}", [128, c.NI, 128], BF16)) for i in range(2)]
            upb = [es.enter_context(nc.sbuf_tensor(f"{tag}_up{i}", [128, c.NP, 128], BF16)) for i in range(2)]
            res = [xn2]

            c_s.wait(SY)
            c_s.wait(GP)
            nrm = {}
            nrm_sem = {}
            s_n.align(3)
            ys0 = vms.inc(VE.memset(ysum[:], 0.0))

            stat_mm = {}
            for tb in range(c.TBN):
                t0 = tb * c.TBT
                for i in range(c.NI):
                    sl = i % 2
                    it = tb * c.NI + i
                    if it >= 2:
                        GP.wait_ge(tms.sem, stat_mm[it - 2])
                    g_x.put(xc2[sl][:], x_src[i * 128:(i + 1) * 128, t0:t0 + c.TBT])
                    g_x.wait_put(VE)
                    if it >= 1:
                        VE.wait_ge(tms.sem, stat_mm[it - 1])
                    v1 = vms.inc(VE.tensor_mul(sq2[0][:], xc2[sl][:], xc2[sl][:]))
                    g_x.wait_put(TE)
                    TE.wait_ge(vms.sem, v1)
                    for hh in range(c.NH):
                        TE.matmul(PS[0 + hh][0:1, :], sb_ones_c[:], xc2[sl][:, hh * 512:(hh + 1) * 512],
                                  start=(i == 0), stop=(i == c.NI - 1))
                        last = TE.matmul(PS[2 + hh][0:1, :], sb_ones_c[:], sq2[0][:, hh * 512:(hh + 1) * 512],
                                         start=(i == 0), stop=(i == c.NI - 1))
                    stat_mm[it] = tms.inc(last)
                for hh in range(c.NH):
                    aa = ln_stats_512(rstdb, mrb, PS[0 + hh], PS[2 + hh], hh * 512)
                xn = [xb[0], xb[1], xn2]
                for i in range(c.NI):
                    it = tb * c.NI + i
                    sl = it % 3
                    on_gp = (sl == 2)
                    EN = GP if on_gp else VE
                    mych = gstep if on_gp else vstep
                    myms = gms if on_gp else vms
                    if it >= 3:
                        SY.wait_ge(nrm_sem[it - 3], nrm[it - 3])
                    s_n.put(xn[sl][:], x_src[i * 128:(i + 1) * 128, t0:t0 + c.TBT])
                    s_n.wait_put(EN)
                    EN.wait_ge(ams.sem, aa)
                    if on_gp and tb == 0 and i == 2:
                        GP.wait_ge(vms.sem, ys0)
                    mych(EN.tensor_mul(xn[sl][:], xn[sl][:], rstdb[:]))
                    if on_gp:
                        mych(EN.tensor_sub(y1[:, i, t0:t0 + c.TBT], xn[sl][:], mrb[:]))
                        EN.sem_inc(myms.sem, 1)
                        myms.n += 1
                        VE.wait_ge(gms.sem, myms.n)
                        vstep(VE.tensor_reduce(sb_tiny[:, 1:2], y1[:, i, t0:t0 + c.TBT], AX.X, ALU.add))
                        vstep(VE.tensor_add(ysum[:, i:i + 1], ysum[:, i:i + 1], sb_tiny[:, 1:2]))
                        VE.sem_inc(vms.sem, 1)
                        vms.n += 1
                    else:
                        col = sb_tiny[:, 0:1]
                        mych(EN.scalar_tensor_tensor(y1[:, i, t0:t0 + c.TBT], xn[sl][:], 1.0,
                                                     mrb[:], ALU.mult, ALU.subtract,
                                                     accum_out=col))
                        mych(EN.tensor_add(ysum[:, i:i + 1], ysum[:, i:i + 1], col))
                        EN.sem_inc(myms.sem, 1)
                        myms.n += 1
                    nrm[it] = myms.n
                    nrm_sem[it] = myms.sem

            vms.wait(GP)
            gms.wait(GP)
            g_s.put(ysi.ap()[:], ysum[:])
            g_s.wait(GP)
            ccs.inc(GP.collective_compute(
                "AllGather", ALU.bypass,
                replica_groups=[[0, 1], [2, 3], [4, 5], [6, 7]],
                ins=[ysi.ap().opt()], outs=[yso.ap().opt()]))
            ccs.wait(SY)
            s_g.put(ysg[:, 0:c.NI], yso.ap()[0:128, :])
            s_g.put(ysg[:, c.NI:2 * c.NI], yso.ap()[128:256, :], rot=False)
            s_g.wait_put(VE)
            vstep(VE.tensor_add(ysg[:, 0:c.NI], ysg[:, 0:c.NI], ysg[:, c.NI:2 * c.NI]))
            vv = vms.inc(VE.tensor_scalar(ysb[:], ysg[:, 0:c.NI], 1.0 / c.SEQ, None, ALU.mult))
            tms.wait(GP)
            g_dpb.put(dpb[:], i_wqf[:])
            g_dpb.wait_put(TE)
            TE.wait_ge(vms.sem, vv)
            for i in range(c.NI):
                last = TE.matmul(PS[6][:, 0:1], dpb[:, i, :], ysb[:, i:i + 1],
                                 start=(i == 0), stop=(i == c.NI - 1))
            tt = tms.inc(last)
            SC.wait_ge(tms.sem, tt)
            aa = ams.inc(SC.copy(sb_tiny[:, 2:3], PS[6][:, 0:1]))
            VE.wait_ge(ams.sem, aa)
            vv = vms.inc(VE.tensor_copy(sb_qcol[:], sb_tiny[:, 2:3]))
            TE.wait_ge(vms.sem, vv)
            g_misc.wait(TE)
            tt = tms.inc(TE.matmul(PS[7][0:1, 0:c.E], sb_qcol[:], sb_keys[:], start=True, stop=True))
            SC.wait_ge(tms.sem, tt)
            aa = ams.inc(SC.activation(sb_tiny[0:1, 8:8 + c.E], PS[7][0:1, 0:c.E], AF.Copy, scale=ISQ_LK))
            VE.wait_ge(ams.sem, aa)
            vstep(VE.tensor_reduce(sb_tiny[0:1, 16:17], sb_tiny[0:1, 8:8 + c.E], AX.X, ALU.max))
            vv = vms.inc(VE.tensor_scalar(sb_tiny[0:1, 8:8 + c.E], sb_tiny[0:1, 8:8 + c.E],
                                          sb_tiny[0:1, 16:17], None, ALU.subtract))
            SC.wait_ge(vms.sem, vv)
            aa = ams.inc(SC.activation(sb_tiny[0:1, 8:8 + c.E], sb_tiny[0:1, 8:8 + c.E], AF.Exp))
            VE.wait_ge(ams.sem, aa)
            vstep(VE.tensor_reduce(sb_tiny[0:1, 17:18], sb_tiny[0:1, 8:8 + c.E], AX.X, ALU.add))
            vstep(VE.reciprocal(sb_tiny[0:1, 18:19], sb_tiny[0:1, 17:18]))
            vstep(VE.tensor_scalar(sb_tiny[0:1, 8:8 + c.E], sb_tiny[0:1, 8:8 + c.E],
                             sb_tiny[0:1, 18:19], None, ALU.mult))
            vv = vms.inc(VE.tensor_copy(sb_rowb[0:1, 0:c.E], sb_tiny[0:1, 8:8 + c.E]))
            TE.wait_ge(vms.sem, vv)
            for p in range(c.NP):
                TE.matmul(PS[6][0:64, 2 + p:3 + p], sb_ones_r[0:1, 0:64],
                          sb_rowb[0:1, 2 * p:2 * p + 1], start=True, stop=True)
                tt = tms.inc(TE.matmul(PS[6][64:128, 2 + p:3 + p], sb_ones_r[0:1, 0:64],
                                       sb_rowb[0:1, 2 * p + 1:2 * p + 2], start=True, stop=True))
            SC.wait_ge(tms.sem, tt)
            acoef = ams.inc(SC.copy(sb_coefv[:, 0:c.NP], PS[6][:, 2:2 + c.NP]))

            gms.wait(TE)
            for p in range(c.NP):
                tms.wait(GP)
                g_dpb.put(dpb[:], i_down[p])
                g_dpb.wait_put(TE)
                for tb in range(c.TBN):
                    t0 = tb * c.TBT
                    vms.wait(TE)
                    for i in range(c.NI):
                        for hh in range(c.NH):
                            last = TE.matmul(PS[0 + hh][:, :], dpb[:, i, :],
                                             y1[:, i, t0 + hh * 512:t0 + (hh + 1) * 512],
                                             start=(i == 0), stop=(i == c.NI - 1))
                    tt = tms.inc(last)
                    VE.wait_ge(tms.sem, tt)
                    VE.wait_ge(ams.sem, acoef)
                    for hh in range(c.NH):
                        vv = vms.inc(VE.tensor_scalar(
                            hsb[:, p, t0 + hh * 512:t0 + (hh + 1) * 512],
                            PS[0 + hh][:, :], sb_coefv[:, p:p + 1], None, ALU.mult))

            grp_cnt = {}
            stt_cnt = {}
            st_snap = {}
            vms.wait(SY)
            gms.wait(SY)
            VE.wait_ge(gms.sem, gms.n)
            for tb in range(c.TBN):
                t0 = tb * c.TBT
                for o in range(c.NI):
                    sl = o % 2
                    it = tb * c.NI + o
                    if it >= 2:
                        GP.wait_ge(tms.sem, grp_cnt[it - 2])
                    g_wob.put(wob[sl][:], i_worg[o])
                    g_upb.put(upb[sl][:], i_up[o])
                    if it >= 2:
                        SY.wait_ge(vms.sem, stt_cnt[it - 2])
                    s_r.put(xb[sl][:], x_src[o * 128:(o + 1) * 128, t0:t0 + c.TBT])
                    g_wob.wait_put(TE)
                    g_upb.wait_put(TE)
                    for hh in range(c.NH):
                        pb = 4 + (it % 2) * c.NH + hh
                        if it >= 2:
                            TE.wait_ge(vms.sem, stt_cnt[it - 2])
                        else:
                            vms.wait(TE)
                        for i in range(c.NI):
                            TE.matmul(PS[pb][:, :], wob[sl][:, i, :],
                                      y1[:, i, t0 + hh * 512:t0 + (hh + 1) * 512],
                                      start=(i == 0), stop=False)
                        for p in range(c.NP):
                            last = TE.matmul(PS[pb][:, :], upb[sl][:, p, :],
                                             hsb[:, p, t0 + hh * 512:t0 + (hh + 1) * 512],
                                             start=False, stop=(p == c.NP - 1))
                        tt = tms.inc(last)
                        VE.wait_ge(tms.sem, tt)
                        s_r.wait_put(VE)
                        if it >= 1 and hh == 0:
                            c_s.wait_at(VE, st_snap[it - 1])
                        vv = vms.inc(VE.scalar_tensor_tensor(
                            res[0][:, hh * 512:(hh + 1) * 512], PS[pb][:, :], 0.5,
                            xb[sl][:, hh * 512:(hh + 1) * 512], ALU.mult, ALU.add))
                    grp_cnt[it] = tt
                    stt_cnt[it] = vv
                    SC.wait_ge(vms.sem, vv)
                    st_snap[it] = c_s.put(x_dst[o * 128:(o + 1) * 128, t0:t0 + c.TBT], res[0][:])
            c_s.wait(SC)
        return

    def attn_phase():
        es = contextlib.ExitStack()
        with es:
            y2b = es.enter_context(nc.sbuf_tensor("a_y2b", [128, c.NI, 512], BF16))
            es2 = contextlib.ExitStack()
            with es2:
                xb = [es2.enter_context(nc.sbuf_tensor(f"a_xb{i}", [128, 512], F32)) for i in range(2)]
                xc2 = [es2.enter_context(nc.sbuf_tensor(f"a_xc{i}", [128, 512], BF16)) for i in range(2)]
                sq2 = [es2.enter_context(nc.sbuf_tensor("a_sq0", [128, 512], BF16))]
                rstdb = [es2.enter_context(nc.sbuf_tensor(f"a_rstdb{i}", [128, 512], F32)) for i in range(2)]
                mrb = [es2.enter_context(nc.sbuf_tensor(f"a_mrb{i}", [128, 512], F32)) for i in range(2)]
                xn2 = es2.enter_context(nc.sbuf_tensor("a_xn2", [128, 512], F32))
                wkb = es2.enter_context(nc.sbuf_tensor("a_wkb", [128, c.NI, c.KVDIM], BF16))
                wvb = es2.enter_context(nc.sbuf_tensor("a_wvb", [128, c.NI, c.KVDIM], BF16))
                kev = [es2.enter_context(nc.sbuf_tensor(f"a_kev{i}", [128, 512], BF16)) for i in range(2)]
                vev = [es2.enter_context(nc.sbuf_tensor(f"a_vev{i}", [128, c.VW], BF16)) for i in range(2)]
                ksq = es2.enter_context(nc.sbuf_tensor("a_ksq", [128, 512], BF16))
                knt = es2.enter_context(nc.sbuf_tensor("a_knt", [1, 512], BF16))
                kngbuf = es2.enter_context(nc.sbuf_tensor("a_kngbuf", [1, c.NTOK], BF16))

                c_s.wait(SY)
                c_s.wait(GP)
                g_misc.put(wkb[:], i_wk[:])
                g_misc.put(wvb[:], i_wv[:])
                nb = c.NTOK // 512
                kv_snap = {}
                ps1_reader = [0]
                nrm_v = [0]
                nrm_g = [0]
                aa_ln = [0]

                def stats_stage(tb2, i):
                    sl = i % 2
                    it = tb2 * c.NI + i
                    if it >= 2:
                        GP.wait_ge(tms.sem, stat_mm[it - 2])
                    g_x.put(xc2[sl][:], d_x2[i * 128:(i + 1) * 128, tb2 * 512:tb2 * 512 + 512])
                    g_x.wait_put(VE)
                    if it >= 1:
                        VE.wait_ge(tms.sem, stat_mm[it - 1])
                    v1 = vms.inc(VE.tensor_mul(sq2[0][:], xc2[sl][:], xc2[sl][:]))
                    g_x.wait_put(TE)
                    TE.wait_ge(vms.sem, v1)
                    TE.matmul(PS[0][0:1, :], sb_ones_c[:], xc2[sl][:], start=(i == 0), stop=(i == c.NI - 1))
                    stat_mm[it] = tms.inc(TE.matmul(PS[2][0:1, :], sb_ones_c[:], sq2[0][:],
                                                    start=(i == 0), stop=(i == c.NI - 1)))

                stat_mm = {}
                for i in range(c.NI):
                    stats_stage(0, i)
                aa_ln[0] = ln_stats_512(rstdb[0], mrb[0], PS[0], PS[2], 0)
                nrm = {}
                nrm_sem = {}
                s_n.align(3)
                for tb in range(nb):
                    t0 = tb * 512
                    xn = [xb[0], xb[1], xn2]
                    aa = aa_ln[0]
                    for i in range(c.NI):
                        it = tb * c.NI + i
                        sl = it % 3
                        on_gp = (sl == 2)
                        EN = GP if on_gp else VE
                        mych = gstep if on_gp else vstep
                        myms = gms if on_gp else vms
                        if it >= 3:
                            SY.wait_ge(nrm_sem[it - 3], nrm[it - 3])
                        s_n.put(xn[sl][:], d_x2[i * 128:(i + 1) * 128, t0:t0 + 512])
                        s_n.wait_put(EN)
                        EN.wait_ge(ams.sem, aa)
                        if tb > 0 and i < 3:
                            s_y2w.wait(EN)
                        if i < 3:
                            tms.wait(EN)
                        mych(EN.tensor_mul(xn[sl][:], xn[sl][:], rstdb[tb % 2][:]))
                        mych(EN.tensor_sub(y2b[:, i, :], xn[sl][:], mrb[tb % 2][:]))
                        EN.sem_inc(myms.sem, 1)
                        myms.n += 1
                        nrm[it] = myms.n
                        nrm_sem[it] = myms.sem
                    nrm_v[0] = vms.n
                    nrm_g[0] = gms.n
                    SC.wait_ge(vms.sem, nrm_v[0])
                    SC.wait_ge(gms.sem, nrm_g[0])
                    np4 = max(1, c.NI // 4)
                    for part in range(cdiv(c.NI, np4)):
                        i0, i1 = part * np4, min((part + 1) * np4, c.NI)
                        s_y2w.put(d_y2[i0:i1, :, t0:t0 + 512].rearrange("a p t -> p a t"),
                                  y2b[:, i0:i1, :], rot=(part == 0))
                    st_stages = []
                    if tb + 1 < nb:
                        for i in range(c.NI):
                            st_stages.append((lambda tb2, ii: lambda: stats_stage(tb2, ii))(tb + 1, i))
                    g_misc.wait(TE)
                    TE.wait_ge(vms.sem, nrm_v[0])
                    TE.wait_ge(gms.sem, nrm_g[0])
                    for h in range(c.KVH):
                        sl = h % 2
                        if ps1_reader[0]:
                            TE.wait_ge(ams.sem, ps1_reader[0])
                        for i in range(c.NI):
                            last = TE.matmul(PS[1][:, :], wkb[:, i, h * 128:(h + 1) * 128], y2b[:, i, :],
                                             start=(i == 0), stop=(i == c.NI - 1))
                        tt = tms.inc(last)
                        SC.wait_ge(tms.sem, tt)
                        if ("k", sl) in kv_snap:
                            g_ks.wait_at(SC, kv_snap[("k", sl)])
                        if ("ksq", sl) in kv_snap:
                            SC.wait_ge(vms.sem, kv_snap[("ksq", sl)])
                        aa2 = ams.inc(SC.copy(kev[sl][:], PS[1][:, :]))
                        ps1_reader[0] = aa2
                        GP.wait_ge(ams.sem, aa2)
                        off_k = h * 128 * c.NTOK
                        kv_snap[("k", sl)] = g_ks.put(
                            kvin_flat[off_k:off_k + 128 * c.NTOK].rearrange("(p t) -> p t", p=128)[:, t0:t0 + 512],
                            kev[sl][:])
                        VE.wait_ge(ams.sem, aa2)
                        if h >= 1 or tb >= 1:
                            VE.wait_ge(tms.sem, kv_snap.get(("knmm", 0), 0))
                        v1 = vms.inc(VE.tensor_mul(ksq[:], kev[sl][:], kev[sl][:]))
                        kv_snap[("ksq", sl)] = v1
                        TE.wait_ge(vms.sem, v1)
                        tt = tms.inc(TE.matmul(PS[3][0:1, :], sb_ones_c[:], ksq[:], start=True, stop=True))
                        kv_snap[("knmm", 0)] = tt
                        SC.wait_ge(tms.sem, tt)
                        if "kn" in kv_snap:
                            g_kn.wait_at(SC, kv_snap["kn"])
                        aa_kn = ams.inc(SC.copy(knt[:], PS[3][0:1, :]))
                        GP.wait_ge(ams.sem, aa_kn)
                        off_kn = 2 * kblk + h * c.NTOK + t0
                        kv_snap["kn"] = g_kn.put(
                            kvin_flat[off_kn:off_kn + 512].rearrange("(a b) -> a b", a=1), knt[:])
                        for _ in range(2):
                            if st_stages:
                                st_stages.pop(0)()
                    for u in range(4):
                        for hf in range(c.VH):
                            sl = (u * c.VH + hf) % 2
                            if ps1_reader[0]:
                                TE.wait_ge(ams.sem, ps1_reader[0])
                            for i in range(c.NI):
                                last = TE.matmul(PS[1][:, 0:c.VW], y2b[:, i, u * 128:(u + 1) * 128],
                                                 wvb[:, i, hf * c.VW:(hf + 1) * c.VW],
                                                 start=(i == 0), stop=(i == c.NI - 1))
                            tt = tms.inc(last)
                            SC.wait_ge(tms.sem, tt)
                            if ("v", sl) in kv_snap:
                                g_vs.wait_at(SC, kv_snap[("v", sl)])
                            aa2 = ams.inc(SC.copy(vev[sl][:], PS[1][:, 0:c.VW]))
                            ps1_reader[0] = aa2
                            GP.wait_ge(ams.sem, aa2)
                            for hh in range(c.VW // 128):
                                h = hf * (c.VW // 128) + hh
                                off_v = kblk + h * c.NTOK * 128 + (tb * 4 + u) * 128 * 128
                                kv_snap[("v", sl)] = g_vs.put(
                                    kvin_flat[off_v:off_v + 16384].rearrange("(p q) -> p q", p=128),
                                    vev[sl][:, hh * 128:(hh + 1) * 128], rot=(hh == 0))
                            for _ in range(2):
                                if st_stages:
                                    st_stages.pop(0)()
                    for st in st_stages:
                        st()
                    if tb + 1 < nb:
                        aa_ln[0] = ln_stats_512(rstdb[(tb + 1) % 2], mrb[(tb + 1) % 2], PS[0], PS[2], 0)
                g_ks.wait(GP)
                g_vs.wait(GP)
                g_kn.wait(GP)
                ccs.inc(GP.collective_compute(
                    "AllGather", ALU.bypass,
                    replica_groups=[[0, 1], [2, 3], [4, 5], [6, 7]],
                    ins=[t_kvin.ap().opt()], outs=[t_kvout.ap().opt()]))
                ccs.wait(SY)
                vstep(VE.memset(sb_kmax2[:], 0.0))
                for rank in range(2):
                    for h in range(c.KVH):
                        vms.wait(SY)
                        off_kn = rank * cc_elems + 2 * kblk + h * c.NTOK
                        s_g.put(kngbuf[:], kvout_flat[off_kn:off_kn + c.NTOK].rearrange("(a b) -> a b", a=1))
                        s_g.wait_put(VE)
                        vstep(VE.tensor_reduce(sb_tiny[0:1, 20:21], kngbuf[:], AX.X, ALU.max))
                        vstep(VE.tensor_tensor(sb_kmax2[0:1, h:h + 1], sb_kmax2[0:1, h:h + 1],
                                               sb_tiny[0:1, 20:21], ALU.max))
                        VE.sem_inc(vms.sem, 1)
                        vms.n += 1
            barrier()

            wqb = [es.enter_context(nc.sbuf_tensor(f"a_wq{i}", [128, c.NI, 128], BF16)) for i in range(2)]
            qsb = [es.enter_context(nc.sbuf_tensor(f"a_q{i}", [128, 512], BF16)) for i in range(2)]
            qsq = es.enter_context(nc.sbuf_tensor("a_qsq", [128, 512], BF16))
            kch = [[es.enter_context(nc.sbuf_tensor(f"a_k{p}{i}", [128, 512], BF16)) for i in range(2)]
                   for p in range(2)]
            vch = [[es.enter_context(nc.sbuf_tensor(f"a_v{p}{i}", [128, 4, 128], BF16)) for i in range(2)]
                   for p in range(2)]
            pbuf = [es.enter_context(nc.sbuf_tensor(f"a_p{i}", [128, 512], BF16)) for i in range(3)]
            negm = [es.enter_context(nc.sbuf_tensor(f"a_negm{i}", [128, 1], F32)) for i in range(2)]
            recb = es.enter_context(nc.sbuf_tensor("a_recb", [128, 512], F32))
            ost = es.enter_context(nc.sbuf_tensor("a_ost", [128, c.NQH, 512], BF16))
            wwo = [es.enter_context(nc.sbuf_tensor(f"a_wwo{i}", [128, c.NQH, 128], BF16)) for i in range(2)]
            xrb = [es.enter_context(nc.sbuf_tensor(f"a_xr{i}", [128, 512], F32)) for i in range(2)]
            res = [es.enter_context(nc.sbuf_tensor(f"a_res{i}", [128, 512], F32)) for i in range(2)]

            def k_chunk_ap(rank, h, cq):
                base = rank * cc_elems + h * 128 * c.NTOK
                return kvout_flat[base:base + 128 * c.NTOK].rearrange(
                    "(p t) -> p t", p=128)[:, cq * 512:(cq + 1) * 512]

            def v_chunk_ap(rank, h, cq):
                base = rank * cc_elems + kblk + h * c.NTOK * 128 + cq * 512 * 128
                return kvout_flat[base:base + 512 * 128].rearrange(
                    "(a p q) -> p a q", p=128, a=4)

            s_y2w.wait(SY)
            c_s.wait(SY)
            wo_last_cnt = 0
            wo_st = []
            qp_done = {}
            qn_read = {}
            qev_map = {}
            tqn_map = {}
            lastS_map = {}
            lastRS_map = {}
            vrec_map = {}
            onorm_map = {}
            obank = lambda hg: 5 if hg % 2 == 0 else 7
            for qb in range(c.QBN):
                i0 = qb * c.QBTILE
                E = 2 * (i0 + c.QBTILE)
                NCH = cdiv(E, 8)
                tms.wait(SY)
                np4 = max(1, c.NI // 4)
                for part in range(cdiv(c.NI, np4)):
                    p0, p1 = part * np4, min((part + 1) * np4, c.NI)
                    s_y.put(y2b[:, p0:p1, :],
                            d_y2[p0:p1, :, qb * 512:(qb + 1) * 512].rearrange("a p t -> p a t"),
                            rot=(part == 0))

                def bound_stages(hg2):
                    h2 = hg2 // c.G
                    sl2 = hg2 % 2
                    key = (qb, hg2)

                    def s0():
                        pk = (qb, hg2 - 2) if hg2 >= 2 else (qb - 1, c.NQH - 2 + hg2)
                        if pk in qp_done:
                            GP.wait_ge(tms.sem, qp_done[pk])
                        g_wq.put(wqb[sl2][:], i_wq[hg2])
                        g_wq.wait_put(TE)
                        ams.wait(TE)
                        pk2 = (qb, hg2 - 1) if hg2 >= 1 else (qb - 1, c.NQH - 1)
                        TE.wait_ge(vms.sem, qn_read.get(pk2, 0))
                        sk = (qb, hg2 - 2) if hg2 >= 2 else (qb - 1, c.NQH - 2 + hg2)
                        if sk in lastS_map:
                            TE.wait_ge(tms.sem, lastS_map[sk])
                        s_y.wait_put(TE)
                        for i in range(c.NI):
                            last = TE.matmul(PS[0][:, :], wqb[sl2][:, i, :], y2b[:, i, :],
                                             start=(i == 0), stop=(i == c.NI - 1))
                        qp_done[key] = tms.inc(last)

                    def s1():
                        SC.wait_ge(tms.sem, qp_done[key])
                        qev_map[key] = ams.inc(SC.activation(qsb[sl2][:], PS[0][:, :], AF.Copy, scale=ISQ_HD))

                    def s2():
                        VE.wait_ge(ams.sem, qev_map[key])
                        pk2 = (qb, hg2 - 1) if hg2 >= 1 else (qb - 1, c.NQH - 1)
                        if pk2 in tqn_map:
                            VE.wait_ge(tms.sem, tqn_map[pk2])
                        qn_read[("v1", key)] = vms.inc(VE.tensor_mul(qsq[:], qsb[sl2][:], qsb[sl2][:]))

                    def s3():
                        TE.wait_ge(vms.sem, qn_read[("v1", key)])
                        tqn_map[key] = tms.inc(TE.matmul(PS[0][0:1, :], sb_ones_c[:], qsq[:],
                                                         start=True, stop=True))

                    def s4():
                        VE.wait_ge(tms.sem, tqn_map[key])
                        vstep(VE.tensor_reduce(sb_tiny[0:1, 24:25], PS[0][0:1, :], AX.X, ALU.max))
                        vms.inc(VE.tensor_mul(sb_tiny[0:1, 25:26], sb_tiny[0:1, 24:25],
                                              sb_kmax2[0:1, h2:h2 + 1]))
                        qn_read[key] = vms.n

                    def s5():
                        SC.wait_ge(vms.sem, qn_read[key])
                        astep(SC.activation(sb_tiny[0:1, 26:27], sb_tiny[0:1, 25:26], AF.Ln))
                        qn_read[("a2", key)] = ams.inc(SC.activation(sb_tiny[0:1, 27:28],
                                                                     sb_tiny[0:1, 26:27], AF.Exp, scale=0.5))

                    def s6():
                        VE.wait_ge(ams.sem, qn_read[("a2", key)])
                        qn_read[("v3", key)] = vms.inc(VE.tensor_scalar(
                            sb_rowb[0:1, 2047:2048], sb_tiny[0:1, 27:28], -1.0, None, ALU.mult))

                    def s7():
                        TE.wait_ge(vms.sem, qn_read[("v3", key)])
                        qn_read[("t3", key)] = tms.inc(TE.matmul(
                            PS[0][:, 0:1], sb_ones_r[:], sb_rowb[0:1, 2047:2048], start=True, stop=True))

                    def s8():
                        SC.wait_ge(tms.sem, qn_read[("t3", key)])
                        astep(SC.copy(negm[sl2][:], PS[0][:, 0:1]))
                        SC.sem_inc(ams.sem, 1)
                        ams.n += 1

                    return [s0, s1, s2, s3, s4, s5, s6, s7, s8]

                def tail_stages(hgp):
                    key = (qb, hgp)
                    ob = obank(hgp)

                    def t0():
                        VE.wait_ge(tms.sem, lastRS_map[key])
                        vstep(VE.reciprocal(sb_row[0:1, 1024:1536], PS[4][0:1, :]))
                        VE.sem_inc(vms.sem, 1)
                        vms.n += 1
                        vrec_map[key] = vms.n
                        vstep(VE.tensor_copy(sb_rowb[0:1, 0:512], sb_row[0:1, 1024:1536]))
                        qn_read[("v4", key)] = vms.inc(VE.tensor_sub(
                            sb_rowb[0:1, 512:1024], sb_row[0:1, 1024:1536], sb_rowb[0:1, 0:512]))

                    def t1():
                        TE.wait_ge(vms.sem, qn_read[("v4", key)])
                        TE.matmul(PS[6][:, :], sb_ones_r[:], sb_rowb[0:1, 0:512], start=True, stop=False)
                        qn_read[("t5", key)] = tms.inc(TE.matmul(
                            PS[6][:, :], sb_ones_r[:], sb_rowb[0:1, 512:1024], start=False, stop=True))

                    def t2():
                        SC.wait_ge(tms.sem, qn_read[("t5", key)])
                        pk = (qb, hgp - 1) if hgp >= 1 else (qb - 1, c.NQH - 1)
                        SC.wait_ge(vms.sem, onorm_map.get(pk, 0))
                        qn_read[("a5", key)] = ams.inc(SC.copy(recb[:], PS[6][:, :]))

                    def t3s():
                        VE.wait_ge(ams.sem, qn_read[("a5", key)])
                        if hgp == 0 and qb > 0:
                            VE.wait_ge(tms.sem, wo_last_cnt)
                        onorm_map[key] = vms.inc(VE.tensor_tensor(
                            ost[:, hgp, :], PS[ob][:, :], recb[:], ALU.mult))

                    return [t0, t1, t2, t3s]

                for st in bound_stages(0):
                    st()
                for hg in range(c.NQH):
                    h = hg // c.G
                    sl = hg % 2
                    ob = obank(hg)
                    pending = []
                    if hg >= 1:
                        pending += tail_stages(hg - 1)
                    if hg + 1 < c.NQH:
                        pending += bound_stages(hg + 1)
                    last_S = {}
                    last_exp = {}
                    last_rs = {}
                    ch_lastS = {}
                    ch_lastRS = {}
                    ksnap = {}
                    vsnap = {}
                    s_k.align(4)
                    s_v.align(4)
                    prev_key = (qb, hg - 1) if hg >= 1 else (qb - 1, c.NQH - 1)
                    prevRS = lastRS_map.get(prev_key, 0)

                    def load_kv_chunk(p, cq):
                        cb = cq % 2
                        if (p, cq - 2) in ch_lastRS:
                            SY.wait_ge(tms.sem, ch_lastRS[(p, cq - 2)])
                        elif prevRS:
                            SY.wait_ge(tms.sem, prevRS)
                        ksnap[(p, cq)] = s_k.put(kch[p][cb][:], k_chunk_ap(p, h, cq))
                        vsnap[(p, cq)] = s_v.put(vch[p][cb][:], v_chunk_ap(p, h, cq))

                    def emit_S(j):
                        sfx = max(0, cdiv(j - 1, 2) - i0) if j >= 1 else 0
                        p, jj = j % 2, j // 2
                        cq, a = jj // 4, jj % 4
                        bank = 1 + j % 3
                        if j >= 3:
                            TE.wait_ge(ams.sem, last_exp[j - 3])
                        if a == 0:
                            s_k.wait_at(TE, ksnap[(p, cq)])
                        last = TE.matmul(PS[bank][:, 128 * sfx:512], kch[p][cq % 2][:, a * 128:(a + 1) * 128],
                                         qsb[sl][:, 128 * sfx:512], start=True, stop=True)
                        last_S[j] = tms.inc(last)
                        ch_lastS[(p, cq)] = last_S[j]
                        return sfx

                    load_kv_chunk(0, 0)
                    load_kv_chunk(1, 0)
                    if NCH > 1:
                        load_kv_chunk(0, 1)
                        load_kv_chunk(1, 1)
                    geom = {0: emit_S(0)}
                    if E > 1:
                        geom[1] = emit_S(1)
                    for j in range(E):
                        p, jj = j % 2, j // 2
                        bank = 1 + j % 3
                        sfx = geom[j]
                        if j + 2 < E:
                            geom[j + 2] = emit_S(j + 2)
                        if pending:
                            pending.pop(0)()
                        masked = (j >= 2 * i0)
                        if masked:
                            VE.wait_ge(tms.sem, last_S[j])
                            mk = 0 if (j % 2 == 0) else 1
                            mv = vms.inc(VE.tensor_tensor(PS[bank][:, 128 * sfx:128 * sfx + 128],
                                                          PS[bank][:, 128 * sfx:128 * sfx + 128],
                                                          sb_masks[:, mk, :], ALU.add))
                            SC.wait_ge(vms.sem, mv)
                        else:
                            SC.wait_ge(tms.sem, last_S[j])
                        if j >= 3:
                            SC.wait_ge(tms.sem, last_rs[j - 3])
                        elif prevRS:
                            SC.wait_ge(tms.sem, prevRS)
                        last_exp[j] = ams.inc(SC.activation(pbuf[j % 3][:, 128 * sfx:512],
                                                            PS[bank][:, 128 * sfx:512],
                                                            AF.Exp, bias=negm[sl][:]))
                        cq, a = jj // 4, jj % 4
                        if a == 0:
                            s_v.wait_at(TE, vsnap[(p, cq)])
                        TE.wait_ge(ams.sem, last_exp[j])
                        if j == 0:
                            pk2 = (qb, hg - 2) if hg >= 2 else (qb - 1, c.NQH - 2 + hg)
                            TE.wait_ge(vms.sem, onorm_map.get(pk2, 0))
                            TE.wait_ge(vms.sem, vrec_map.get(prev_key, 0))
                        TE.matmul(PS[ob][:, 128 * sfx:512], vch[p][cq % 2][:, a, :],
                                  pbuf[j % 3][:, 128 * sfx:512], start=(j == 0), stop=(j == E - 1))
                        rs = TE.matmul(PS[4][0:1, 128 * sfx:512], sb_ones_c[:],
                                       pbuf[j % 3][:, 128 * sfx:512], start=(j == 0), stop=(j == E - 1))
                        last_rs[j] = tms.inc(rs)
                        ch_lastRS[(p, jj // 4)] = last_rs[j]
                        if jj % 4 == 3 and p == 1 and (jj // 4) + 2 < NCH:
                            load_kv_chunk(0, jj // 4 + 2)
                            load_kv_chunk(1, jj // 4 + 2)
                    for st in pending:
                        st()
                    lastS_map[(qb, hg)] = last_S[E - 1]
                    lastRS_map[(qb, hg)] = last_rs[E - 1]
                for st in tail_stages(c.NQH - 1):
                    st()
                onorm = onorm_map[(qb, c.NQH - 1)]
                stt_hist = []
                wo_hist = []
                for o in range(c.NI):
                    sl = o % 2
                    if len(wo_hist) >= 2:
                        GP.wait_ge(tms.sem, wo_hist[-2])
                    elif wo_last_cnt:
                        GP.wait_ge(tms.sem, wo_last_cnt)
                    g_wwo.put(wwo[sl][:], i_wo[o])
                    if o >= 2:
                        SY.wait_ge(vms.sem, stt_hist[-2])
                    elif qb > 0:
                        vms.wait(SY)
                    s_r.put(xrb[sl][:], d_x2[o * 128:(o + 1) * 128, qb * 512:(qb + 1) * 512])
                    g_wwo.wait_put(TE)
                    if o == 0:
                        TE.wait_ge(vms.sem, onorm)
                    pb = 6 + (o % 2)
                    if o >= 2:
                        TE.wait_ge(vms.sem, stt_hist[-2])
                    for hd in range(c.NQH):
                        last = TE.matmul(PS[pb][:, :], wwo[sl][:, hd, :], ost[:, hd, :],
                                         start=(hd == 0), stop=(hd == c.NQH - 1))
                    wo_last_cnt = tms.inc(last)
                    wo_hist.append(wo_last_cnt)
                    VE.wait_ge(tms.sem, wo_last_cnt)
                    s_r.wait_put(VE)
                    if len(wo_st) >= 2:
                        c_s.wait_at(VE, wo_st[-2])
                    vv = vms.inc(VE.scalar_tensor_tensor(res[sl][:], PS[pb][:, :], 1.0,
                                                         xrb[sl][:], ALU.mult, ALU.add))
                    stt_hist.append(vv)
                    SC.wait_ge(vms.sem, vv)
                    wo_st.append(c_s.put(d_x3[o * 128:(o + 1) * 128, qb * 512:(qb + 1) * 512], res[sl][:]))
            c_s.wait(SC)
        return

    ffn_phase("f1", i_x, d_x2, t_ys[0], t_yso[0])
    barrier()
    attn_phase()
    barrier()
    ffn_phase("f3", d_x3, o_out, t_ys[1], t_yso[1])
    c_s.wait(SC)
    g_s.wait(GP)
    s_y2w.wait(SC)
    barrier()
    return nc


def _host_prep(inputs, cfg: CFG):
    c = cfg
    x = np.asarray(inputs["L0_attn_x"], dtype=np.float32)
    Wq = np.asarray(inputs["L0_attn_Wq"], dtype=np.float32)
    Wk = np.asarray(inputs["L0_attn_Wk"], dtype=np.float32)
    Wv = np.asarray(inputs["L0_attn_Wv"], dtype=np.float32)
    Wo = np.asarray(inputs["L0_attn_Wo"], dtype=np.float32)
    Worg = np.asarray(inputs["L0_ffn_W_org"], dtype=np.float32)
    down = np.asarray(inputs["L0_ffn_lora_down"], dtype=np.float32)
    up = np.asarray(inputs["L0_ffn_lora_up"], dtype=np.float32)
    keys = np.asarray(inputs["L0_ffn_lora_keys"], dtype=np.float32)
    Wqf = np.asarray(inputs["L0_ffn_Wq"], dtype=np.float32)

    NI = c.NI

    def blk_lhsT(wT):
        K, M = wT.shape
        return np.ascontiguousarray(
            wT.reshape(K // 128, 128, M // 128, 128).transpose(2, 1, 0, 3))

    def blk_rhs(wT):
        K, N = wT.shape
        return np.ascontiguousarray(wT.reshape(K // 128, 128, N).transpose(1, 0, 2))

    worg4 = blk_lhsT(Worg.T)
    wq4 = blk_lhsT(Wq.T)
    wo4 = blk_lhsT(Wo.T)
    wk4 = blk_rhs(Wk.T)
    wv4 = blk_rhs(Wv.T)
    downp = np.concatenate([down[0::2], down[1::2]], axis=2)
    down4 = np.ascontiguousarray(
        downp.reshape(c.NP, NI, 128, 128).transpose(0, 2, 1, 3))
    uppair = np.stack([np.concatenate([up[2 * p], up[2 * p + 1]], axis=0)
                       for p in range(c.NP)])
    up4 = np.ascontiguousarray(
        uppair.reshape(c.NP, 128, NI, 128).transpose(2, 1, 0, 3))
    wqf4 = blk_rhs(Wqf.T)
    wqf4 = np.ascontiguousarray(Wqf.T.reshape(NI, 128, 128).transpose(1, 0, 2))
    keysT = np.ascontiguousarray(keys.T)

    import ml_dtypes
    kl = np.arange(128)[:, None]
    ql = np.arange(128)[None, :]
    bf = ml_dtypes.bfloat16
    tri = np.where(kl <= ql, 0.0, NEG).astype(bf)
    masks = [np.stack([tri, np.full((128, 128), NEG, bf)]),
             np.stack([np.zeros((128, 128), bf), tri])]

    in_maps = []
    for core in range(8):
        b, par = core // 2, core % 2
        sel = x[b].reshape(2 * c.NTILE, 128, c.D)[par::2].reshape(c.NTOK, c.D)
        xT = np.ascontiguousarray(sel.T)
        in_maps.append({
            "x": xT, "masks": masks[par], "worg4": worg4, "wq4": wq4,
            "wo4": wo4, "wk4": wk4, "wv4": wv4, "down4": down4, "up4": up4,
            "wqf4": wqf4, "keysT": keysT,
        })
    return in_maps


def _host_assemble(results, cfg: CFG):
    c = cfg
    out = np.empty((4, c.SEQ, c.D), dtype=np.float32)
    for core in range(8):
        b, par = core // 2, core % 2
        oT = np.asarray(results[core]["out"])
        rows = np.ascontiguousarray(oT.T)
        out[b].reshape(2 * c.NTILE, 128, c.D)[par::2] = rows.reshape(c.NTILE, 128, c.D)
    return out


def kernel(**inputs):
    cfg = CFG(mini=False)
    in_maps = _host_prep(inputs, cfg)
    nc = build(cfg)
    res = run_bass_kernel_spmd(nc, in_maps, core_ids=list(range(8)))
    return _host_assemble(res.results, cfg)


# version 09
# speedup vs baseline: 1.1954x; 1.0242x over previous
import math
import contextlib
import numpy as np

import concourse.bass as bass
import concourse.mybir as mybir
from concourse.bass_utils import run_bass_kernel_spmd

F32 = mybir.dt.float32
BF16 = mybir.dt.bfloat16
AF = mybir.ActivationFunctionType
ALU = mybir.AluOpType
AX = mybir.AxisListType

NEG = -1.0e30


class CFG:
    def __init__(self, mini=False, midi=False):
        if midi:
            self.D = 1024
            self.NTOK = 2048
            self.KVH = 8
            self.G = 1
        elif mini:
            self.D = 512
            self.NTOK = 512
            self.KVH = 2
            self.G = 2
        else:
            self.D = 4096
            self.NTOK = 2048
            self.KVH = 8
            self.G = 4
        self.HD = 128
        self.NI = self.D // 128
        self.NTILE = self.NTOK // 128
        self.QBN = max(1, self.NTOK // 512)
        self.QBT = self.NTOK // self.QBN
        self.QBTILE = self.QBT // 128
        self.TBN = max(1, self.NTOK // 1024)
        self.TBT = self.NTOK // self.TBN
        self.NH = self.TBT // 512
        self.E = 8
        self.R = 64
        self.LK = 128
        self.NP = 4
        self.KVDIM = self.KVH * self.HD
        self.NQH = self.KVH * self.G
        self.VH = max(1, self.KVDIM // 512)
        self.VW = self.KVDIM // self.VH
        self.SEQ = 2 * self.NTOK


class Ctr:
    def __init__(self, nc, name):
        self.sem = nc.alloc_semaphore(name)
        self.n = 0

    def inc(self, instr, k=1):
        instr.then_inc(self.sem, k)
        self.n += k
        return self.n

    def wait(self, eng, val=None):
        eng.wait_ge(self.sem, self.n if val is None else val)


class Stream:

    def __init__(self, nc, name, eng, nsem=2):
        self.sems = [nc.alloc_semaphore(f"{name}{i}") for i in range(nsem)]
        self.cnts = [0] * nsem
        self.eng = eng
        self.nsem = nsem
        self.ng = 0

    def put(self, out_ap, in_ap, rot=True, **kw):
        if rot:
            self.ng += 1
        k = self.ng % self.nsem
        self.eng.dma_start(out_ap, in_ap, **kw).then_inc(self.sems[k], 16)
        self.cnts[k] += 16
        return (k, self.cnts[k])

    def snap(self):
        k = self.ng % self.nsem
        return (k, self.cnts[k])

    def wait(self, eng):
        for k in range(self.nsem):
            if self.cnts[k]:
                eng.wait_ge(self.sems[k], self.cnts[k])

    def wait_put(self, eng):
        k = self.ng % self.nsem
        eng.wait_ge(self.sems[k], self.cnts[k])

    def wait_at(self, eng, sn):
        eng.wait_ge(self.sems[sn[0]], sn[1])

    def align(self, k):
        self.ng += (-self.ng) % k


def cdiv(a, b):
    return (a + b - 1) // b


def build(cfg: CFG):
    c = cfg
    nc = bass.Bass("TRN2", target_bir_lowering=False, debug=False, num_devices=8)

    i_x = nc.dram_tensor("x", [c.D, c.NTOK], F32, kind="ExternalInput").ap()
    i_masks = nc.dram_tensor("masks", [2, 128, 128], BF16, kind="ExternalInput").ap()
    i_worg = nc.dram_tensor("worg4", [c.NI, 128, c.NI, 128], F32, kind="ExternalInput").ap()
    i_wq = nc.dram_tensor("wq4", [c.NQH, 128, c.NI, 128], F32, kind="ExternalInput").ap()
    i_wo = nc.dram_tensor("wo4", [c.NI, 128, c.NQH, 128], F32, kind="ExternalInput").ap()
    i_wk = nc.dram_tensor("wk4", [128, c.NI, c.KVDIM], F32, kind="ExternalInput").ap()
    i_wv = nc.dram_tensor("wv4", [128, c.NI, c.KVDIM], F32, kind="ExternalInput").ap()
    i_down = nc.dram_tensor("down4", [c.NP, 128, c.NI, 128], F32, kind="ExternalInput").ap()
    i_up = nc.dram_tensor("up4", [c.NI, 128, c.NP, 128], F32, kind="ExternalInput").ap()
    i_wqf = nc.dram_tensor("wqf4", [128, c.NI, 128], F32, kind="ExternalInput").ap()
    i_keys = nc.dram_tensor("keysT", [c.LK, c.E], F32, kind="ExternalInput").ap()
    o_out = nc.dram_tensor("out", [c.D, c.NTOK], F32, kind="ExternalOutput").ap()

    d_x2 = nc.dram_tensor("x2d", [c.D, c.NTOK], F32, kind="Internal").ap()
    d_x3 = nc.dram_tensor("x3d", [c.D, c.NTOK], F32, kind="Internal").ap()
    d_y2 = nc.dram_tensor("y2d", [c.NI, 128, c.NTOK], BF16, kind="Internal").ap()
    kblk = c.KVH * c.NTILE * 128 * 128
    knorm_elems = c.KVH * c.NTOK
    cc_rows = cdiv(2 * kblk + knorm_elems, 2048)
    cc_elems = cc_rows * 2048
    t_kvin = nc.dram_tensor("kvin", [cc_rows, 2048], BF16, kind="Internal")
    t_kvout = nc.dram_tensor("kvout", [2 * cc_rows, 2048], BF16, kind="Internal")
    kvin_flat = t_kvin.ap().rearrange("a b -> (a b)")
    kvout_flat = t_kvout.ap().rearrange("a b -> (a b)")
    t_ys = [nc.dram_tensor(f"ysi{i}", [128, c.NI], F32, kind="Internal") for i in range(2)]
    t_yso = [nc.dram_tensor(f"yso{i}", [256, c.NI], F32, kind="Internal") for i in range(2)]

    def k_tile_in(h, j_loc):
        off = (h * c.NTILE + j_loc) * 16384
        return kvin_flat[off:off + 16384].rearrange("(p q) -> p q", p=128)

    def v_tile_in(h, j_loc):
        off = kblk + (j_loc * c.KVH + h) * 16384
        return kvin_flat[off:off + 16384].rearrange("(p q) -> p q", p=128)

    def knorm_in():
        off = 2 * kblk
        return kvin_flat[off:off + knorm_elems].rearrange("(a b) -> a b", a=1)

    def k_tile_out(rank, h, j_loc):
        off = rank * cc_elems + (h * c.NTILE + j_loc) * 16384
        return kvout_flat[off:off + 16384].rearrange("(p q) -> p q", p=128)

    def v_tile_out(rank, h, j_loc):
        off = rank * cc_elems + kblk + (j_loc * c.KVH + h) * 16384
        return kvout_flat[off:off + 16384].rearrange("(p q) -> p q", p=128)

    def knorm_out(rank):
        off = rank * cc_elems + 2 * kblk
        return kvout_flat[off:off + knorm_elems].rearrange("(a b) -> a b", a=1)

    SY, GP, SC, VE, TE = nc.sync, nc.gpsimd, nc.scalar, nc.vector, nc.tensor
    tms = Ctr(nc, "tms")
    vms = Ctr(nc, "vms")
    ams = Ctr(nc, "ams")
    vch = Ctr(nc, "vch")
    ach = Ctr(nc, "ach")
    gch = Ctr(nc, "gch")
    gms = Ctr(nc, "gms")

    def vstep(instr):
        vch.inc(instr)
        VE.wait_ge(vch.sem, vch.n)
        return instr

    def astep(instr):
        ach.inc(instr)
        SC.wait_ge(ach.sem, ach.n)
        return instr

    def gstep(instr):
        gch.inc(instr)
        GP.wait_ge(gch.sem, gch.n)
        return instr
    ccs = Ctr(nc, "ccs")
    bar = Ctr(nc, "bar")

    s_x = Stream(nc, "s_x", SY)
    s_n = Stream(nc, "s_n", SY, nsem=3)
    s_r = Stream(nc, "s_r", SY)
    s_y = Stream(nc, "s_y", SY)
    s_k = Stream(nc, "s_k", SY, nsem=4)
    s_v = Stream(nc, "s_v", SY, nsem=4)
    s_g = Stream(nc, "s_g", SY)
    g_x = Stream(nc, "g_x", GP)
    g_dpb = Stream(nc, "g_dpb", GP, nsem=1)
    g_wob = Stream(nc, "g_wob", GP)
    g_upb = Stream(nc, "g_upb", GP)
    g_wq = Stream(nc, "g_wq", GP)
    g_wwo = Stream(nc, "g_wwo", GP)
    g_misc = Stream(nc, "g_misc", GP, nsem=1)
    g_s = Stream(nc, "g_s", GP, nsem=1)
    g_kn = Stream(nc, "g_kn", GP, nsem=1)
    g_ks = Stream(nc, "g_ks", GP)
    g_vs = Stream(nc, "g_vs", GP)
    c_s = Stream(nc, "c_s", SC)
    s_y2w = Stream(nc, "s_y2w", SC)

    def barrier():
        for e in (TE, VE, SC, GP, SY):
            e.sem_inc(bar.sem, 1)
        bar.n += 5
        for e in (TE, VE, SC, GP, SY):
            e.wait_ge(bar.sem, bar.n)

    sb_keys = nc.alloc_sbuf_tensor("sb_keys", [c.LK, c.E], BF16)
    sb_masks = nc.alloc_sbuf_tensor("sb_masks", [128, 2, 128], BF16)
    sb_ones_r = nc.alloc_sbuf_tensor("sb_ones_r", [1, 128], BF16)
    sb_ones_c = nc.alloc_sbuf_tensor("sb_ones_c", [128, 1], BF16)
    sb_coefv = nc.alloc_sbuf_tensor("sb_coefv", [128, c.NP], F32)
    sb_kmax2 = nc.alloc_sbuf_tensor("sb_kmax2", [1, c.KVH], F32)
    sb_tiny = nc.alloc_sbuf_tensor("sb_tiny", [128, 32], F32)
    sb_row = nc.alloc_sbuf_tensor("sb_row", [1, 1536], F32)
    sb_rowb = nc.alloc_sbuf_tensor("sb_rowb", [1, 2048], BF16)
    sb_qcol = nc.alloc_sbuf_tensor("sb_qcol", [128, 1], BF16)

    PS = [nc.alloc_psum_tensor(f"ps{i}", [128, 512], F32) for i in range(8)]

    VE.memset(sb_ones_r[:], 1.0)
    VE.memset(sb_ones_c[:], 1.0)
    vms.inc(VE.memset(sb_coefv[:], 0.0))
    s_g.put(sb_masks[:, 0, :], i_masks[0])
    s_g.put(sb_masks[:, 1, :], i_masks[1], rot=False)
    g_misc.put(sb_keys[:], i_keys[:])

    ISQ_HD = 1.0 / math.sqrt(c.HD)
    ISQ_LK = 1.0 / math.sqrt(c.LK)

    R_M = 0

    def ln_stats_512(x_dst_rstdb, x_dst_mrb, ps_sum, ps_sq, col0):
        m_row = sb_row[0:1, 0:512]
        v_row = sb_row[0:1, 512:1024]
        t_row = sb_row[0:1, 1024:1536]
        tms.wait(VE)
        VE.tensor_scalar(m_row, ps_sum[0:1, :], 1.0 / c.D, None, ALU.mult)
        vstep(VE.tensor_scalar(v_row, ps_sq[0:1, :], 1.0 / c.D, None, ALU.mult))
        vstep(VE.tensor_mul(t_row, m_row, m_row))
        vv = vms.inc(VE.scalar_tensor_tensor(v_row, v_row, 1e-5, t_row, ALU.add, ALU.subtract))
        SC.wait_ge(vms.sem, vv)
        astep(SC.activation(t_row, v_row, AF.Ln))
        aa = ams.inc(SC.activation(v_row, t_row, AF.Exp, scale=-0.5))
        VE.wait_ge(ams.sem, aa)
        vstep(VE.tensor_mul(m_row, m_row, v_row))
        vstep(VE.tensor_copy(sb_rowb[0:1, 0:512], v_row))
        VE.tensor_sub(sb_rowb[0:1, 512:1024], v_row, sb_rowb[0:1, 0:512])
        vstep(VE.tensor_copy(sb_rowb[0:1, 1024:1536], m_row))
        vv = vms.inc(VE.tensor_sub(sb_rowb[0:1, 1536:2048], m_row, sb_rowb[0:1, 1024:1536]))
        TE.wait_ge(vms.sem, vv)
        TE.matmul(PS[4][:, :], sb_ones_r[:], sb_rowb[0:1, 0:512], start=True, stop=False)
        TE.matmul(PS[4][:, :], sb_ones_r[:], sb_rowb[0:1, 512:1024], start=False, stop=True)
        TE.matmul(PS[5][:, :], sb_ones_r[:], sb_rowb[0:1, 1024:1536], start=True, stop=False)
        tt = tms.inc(TE.matmul(PS[5][:, :], sb_ones_r[:], sb_rowb[0:1, 1536:2048], start=False, stop=True))
        SC.wait_ge(tms.sem, tt)
        SC.copy(x_dst_rstdb[:, col0:col0 + 512], PS[4][:, :])
        return ams.inc(SC.copy(x_dst_mrb[:, col0:col0 + 512], PS[5][:, :]))

    def ffn_phase(tag, x_src, x_dst, ysi, yso):
        es = contextlib.ExitStack()
        with es:
            y1 = es.enter_context(nc.sbuf_tensor(f"{tag}_y1", [128, c.NI, c.NTOK], BF16))
            xb = [es.enter_context(nc.sbuf_tensor(f"{tag}_xb{i}", [128, c.TBT], F32)) for i in range(2)]
            xc2 = [es.enter_context(nc.sbuf_tensor(f"{tag}_xc{i}", [128, c.TBT], BF16)) for i in range(2)]
            sq2 = [es.enter_context(nc.sbuf_tensor(f"{tag}_sq0", [128, c.TBT], BF16))]
            rstdb = es.enter_context(nc.sbuf_tensor(f"{tag}_rstdb", [128, c.TBT], F32))
            mrb = es.enter_context(nc.sbuf_tensor(f"{tag}_mrb", [128, c.TBT], F32))
            ysum = es.enter_context(nc.sbuf_tensor(f"{tag}_ysum", [128, c.NI], F32))
            xn2 = es.enter_context(nc.sbuf_tensor(f"{tag}_xn2", [128, c.TBT], F32))
            ysg = es.enter_context(nc.sbuf_tensor(f"{tag}_ysg", [128, 2 * c.NI], F32))
            ysb = es.enter_context(nc.sbuf_tensor(f"{tag}_ysb", [128, c.NI], BF16))
            hsb = es.enter_context(nc.sbuf_tensor(f"{tag}_h", [128, c.NP, c.NTOK], BF16))
            dpb = es.enter_context(nc.sbuf_tensor(f"{tag}_dp", [128, c.NI, 128], BF16))
            wob = [es.enter_context(nc.sbuf_tensor(f"{tag}_wo{i}", [128, c.NI, 128], BF16)) for i in range(2)]
            upb = [es.enter_context(nc.sbuf_tensor(f"{tag}_up{i}", [128, c.NP, 128], BF16)) for i in range(2)]
            res = [xn2]

            c_s.wait(SY)
            c_s.wait(GP)
            nrm = {}
            nrm_sem = {}
            s_n.align(3)
            ys0 = vms.inc(VE.memset(ysum[:], 0.0))

            stat_mm = {}
            for tb in range(c.TBN):
                t0 = tb * c.TBT
                for i in range(c.NI):
                    sl = i % 2
                    it = tb * c.NI + i
                    if it >= 2:
                        GP.wait_ge(tms.sem, stat_mm[it - 2])
                    g_x.put(xc2[sl][:], x_src[i * 128:(i + 1) * 128, t0:t0 + c.TBT])
                    g_x.wait_put(VE)
                    if it >= 1:
                        VE.wait_ge(tms.sem, stat_mm[it - 1])
                    v1 = vms.inc(VE.tensor_mul(sq2[0][:], xc2[sl][:], xc2[sl][:]))
                    g_x.wait_put(TE)
                    TE.wait_ge(vms.sem, v1)
                    for hh in range(c.NH):
                        TE.matmul(PS[0 + hh][0:1, :], sb_ones_c[:], xc2[sl][:, hh * 512:(hh + 1) * 512],
                                  start=(i == 0), stop=(i == c.NI - 1))
                        last = TE.matmul(PS[2 + hh][0:1, :], sb_ones_c[:], sq2[0][:, hh * 512:(hh + 1) * 512],
                                         start=(i == 0), stop=(i == c.NI - 1))
                    stat_mm[it] = tms.inc(last)
                for hh in range(c.NH):
                    aa = ln_stats_512(rstdb, mrb, PS[0 + hh], PS[2 + hh], hh * 512)
                xn = [xb[0], xb[1], xn2]
                for i in range(c.NI):
                    it = tb * c.NI + i
                    sl = it % 3
                    on_gp = (sl == 2)
                    EN = GP if on_gp else VE
                    mych = gstep if on_gp else vstep
                    myms = gms if on_gp else vms
                    if it >= 3:
                        SY.wait_ge(nrm_sem[it - 3], nrm[it - 3])
                    s_n.put(xn[sl][:], x_src[i * 128:(i + 1) * 128, t0:t0 + c.TBT])
                    s_n.wait_put(EN)
                    EN.wait_ge(ams.sem, aa)
                    if on_gp and tb == 0 and i == 2:
                        GP.wait_ge(vms.sem, ys0)
                    mych(EN.tensor_mul(xn[sl][:], xn[sl][:], rstdb[:]))
                    if on_gp:
                        mych(EN.tensor_sub(y1[:, i, t0:t0 + c.TBT], xn[sl][:], mrb[:]))
                        EN.sem_inc(myms.sem, 1)
                        myms.n += 1
                        VE.wait_ge(gms.sem, myms.n)
                        vstep(VE.tensor_reduce(sb_tiny[:, 1:2], y1[:, i, t0:t0 + c.TBT], AX.X, ALU.add))
                        vstep(VE.tensor_add(ysum[:, i:i + 1], ysum[:, i:i + 1], sb_tiny[:, 1:2]))
                        VE.sem_inc(vms.sem, 1)
                        vms.n += 1
                    else:
                        col = sb_tiny[:, 0:1]
                        mych(EN.scalar_tensor_tensor(y1[:, i, t0:t0 + c.TBT], xn[sl][:], 1.0,
                                                     mrb[:], ALU.mult, ALU.subtract,
                                                     accum_out=col))
                        mych(EN.tensor_add(ysum[:, i:i + 1], ysum[:, i:i + 1], col))
                        EN.sem_inc(myms.sem, 1)
                        myms.n += 1
                    nrm[it] = myms.n
                    nrm_sem[it] = myms.sem

            vms.wait(GP)
            gms.wait(GP)
            g_s.put(ysi.ap()[:], ysum[:])
            g_s.wait(GP)
            ccs.inc(GP.collective_compute(
                "AllGather", ALU.bypass,
                replica_groups=[[0, 1], [2, 3], [4, 5], [6, 7]],
                ins=[ysi.ap().opt()], outs=[yso.ap().opt()]))
            ccs.wait(SY)
            s_g.put(ysg[:, 0:c.NI], yso.ap()[0:128, :])
            s_g.put(ysg[:, c.NI:2 * c.NI], yso.ap()[128:256, :], rot=False)
            s_g.wait_put(VE)
            vstep(VE.tensor_add(ysg[:, 0:c.NI], ysg[:, 0:c.NI], ysg[:, c.NI:2 * c.NI]))
            vv = vms.inc(VE.tensor_scalar(ysb[:], ysg[:, 0:c.NI], 1.0 / c.SEQ, None, ALU.mult))
            tms.wait(GP)
            g_dpb.put(dpb[:], i_wqf[:])
            g_dpb.wait_put(TE)
            TE.wait_ge(vms.sem, vv)
            for i in range(c.NI):
                last = TE.matmul(PS[6][:, 0:1], dpb[:, i, :], ysb[:, i:i + 1],
                                 start=(i == 0), stop=(i == c.NI - 1))
            tt = tms.inc(last)
            SC.wait_ge(tms.sem, tt)
            aa = ams.inc(SC.copy(sb_tiny[:, 2:3], PS[6][:, 0:1]))
            VE.wait_ge(ams.sem, aa)
            vv = vms.inc(VE.tensor_copy(sb_qcol[:], sb_tiny[:, 2:3]))
            TE.wait_ge(vms.sem, vv)
            g_misc.wait(TE)
            tt = tms.inc(TE.matmul(PS[7][0:1, 0:c.E], sb_qcol[:], sb_keys[:], start=True, stop=True))
            SC.wait_ge(tms.sem, tt)
            aa = ams.inc(SC.activation(sb_tiny[0:1, 8:8 + c.E], PS[7][0:1, 0:c.E], AF.Copy, scale=ISQ_LK))
            VE.wait_ge(ams.sem, aa)
            vstep(VE.tensor_reduce(sb_tiny[0:1, 16:17], sb_tiny[0:1, 8:8 + c.E], AX.X, ALU.max))
            vv = vms.inc(VE.tensor_scalar(sb_tiny[0:1, 8:8 + c.E], sb_tiny[0:1, 8:8 + c.E],
                                          sb_tiny[0:1, 16:17], None, ALU.subtract))
            SC.wait_ge(vms.sem, vv)
            aa = ams.inc(SC.activation(sb_tiny[0:1, 8:8 + c.E], sb_tiny[0:1, 8:8 + c.E], AF.Exp))
            VE.wait_ge(ams.sem, aa)
            vstep(VE.tensor_reduce(sb_tiny[0:1, 17:18], sb_tiny[0:1, 8:8 + c.E], AX.X, ALU.add))
            vstep(VE.reciprocal(sb_tiny[0:1, 18:19], sb_tiny[0:1, 17:18]))
            vstep(VE.tensor_scalar(sb_tiny[0:1, 8:8 + c.E], sb_tiny[0:1, 8:8 + c.E],
                             sb_tiny[0:1, 18:19], None, ALU.mult))
            vv = vms.inc(VE.tensor_copy(sb_rowb[0:1, 0:c.E], sb_tiny[0:1, 8:8 + c.E]))
            TE.wait_ge(vms.sem, vv)
            for p in range(c.NP):
                TE.matmul(PS[6][0:64, 2 + p:3 + p], sb_ones_r[0:1, 0:64],
                          sb_rowb[0:1, 2 * p:2 * p + 1], start=True, stop=True)
                tt = tms.inc(TE.matmul(PS[6][64:128, 2 + p:3 + p], sb_ones_r[0:1, 0:64],
                                       sb_rowb[0:1, 2 * p + 1:2 * p + 2], start=True, stop=True))
            SC.wait_ge(tms.sem, tt)
            acoef = ams.inc(SC.copy(sb_coefv[:, 0:c.NP], PS[6][:, 2:2 + c.NP]))

            gms.wait(TE)
            for p in range(c.NP):
                tms.wait(GP)
                g_dpb.put(dpb[:], i_down[p])
                g_dpb.wait_put(TE)
                for tb in range(c.TBN):
                    t0 = tb * c.TBT
                    vms.wait(TE)
                    for i in range(c.NI):
                        for hh in range(c.NH):
                            last = TE.matmul(PS[0 + hh][:, :], dpb[:, i, :],
                                             y1[:, i, t0 + hh * 512:t0 + (hh + 1) * 512],
                                             start=(i == 0), stop=(i == c.NI - 1))
                    tt = tms.inc(last)
                    VE.wait_ge(tms.sem, tt)
                    VE.wait_ge(ams.sem, acoef)
                    for hh in range(c.NH):
                        vv = vms.inc(VE.tensor_scalar(
                            hsb[:, p, t0 + hh * 512:t0 + (hh + 1) * 512],
                            PS[0 + hh][:, :], sb_coefv[:, p:p + 1], None, ALU.mult))

            grp_cnt = {}
            stt_cnt = {}
            st_snap = {}
            vms.wait(SY)
            gms.wait(SY)
            VE.wait_ge(gms.sem, gms.n)
            for tb in range(c.TBN):
                t0 = tb * c.TBT
                for o in range(c.NI):
                    sl = o % 2
                    it = tb * c.NI + o
                    if it >= 2:
                        GP.wait_ge(tms.sem, grp_cnt[it - 2])
                    g_wob.put(wob[sl][:], i_worg[o])
                    g_upb.put(upb[sl][:], i_up[o])
                    if it >= 2:
                        SY.wait_ge(vms.sem, stt_cnt[it - 2])
                    s_r.put(xb[sl][:], x_src[o * 128:(o + 1) * 128, t0:t0 + c.TBT])
                    g_wob.wait_put(TE)
                    g_upb.wait_put(TE)
                    for hh in range(c.NH):
                        pb = 4 + (it % 2) * c.NH + hh
                        if it >= 2:
                            TE.wait_ge(vms.sem, stt_cnt[it - 2])
                        else:
                            vms.wait(TE)
                        for i in range(c.NI):
                            TE.matmul(PS[pb][:, :], wob[sl][:, i, :],
                                      y1[:, i, t0 + hh * 512:t0 + (hh + 1) * 512],
                                      start=(i == 0), stop=False)
                        for p in range(c.NP):
                            last = TE.matmul(PS[pb][:, :], upb[sl][:, p, :],
                                             hsb[:, p, t0 + hh * 512:t0 + (hh + 1) * 512],
                                             start=False, stop=(p == c.NP - 1))
                        tt = tms.inc(last)
                        VE.wait_ge(tms.sem, tt)
                        s_r.wait_put(VE)
                        if it >= 1 and hh == 0:
                            c_s.wait_at(VE, st_snap[it - 1])
                        vv = vms.inc(VE.scalar_tensor_tensor(
                            res[0][:, hh * 512:(hh + 1) * 512], PS[pb][:, :], 0.5,
                            xb[sl][:, hh * 512:(hh + 1) * 512], ALU.mult, ALU.add))
                    grp_cnt[it] = tt
                    stt_cnt[it] = vv
                    SC.wait_ge(vms.sem, vv)
                    st_snap[it] = c_s.put(x_dst[o * 128:(o + 1) * 128, t0:t0 + c.TBT], res[0][:])
            c_s.wait(SC)
        return

    def attn_phase():
        es = contextlib.ExitStack()
        with es:
            y2b = es.enter_context(nc.sbuf_tensor("a_y2b", [128, c.NI, 512], BF16))
            es2 = contextlib.ExitStack()
            with es2:
                xb = [es2.enter_context(nc.sbuf_tensor(f"a_xb{i}", [128, 512], F32)) for i in range(2)]
                xc2 = [es2.enter_context(nc.sbuf_tensor(f"a_xc{i}", [128, 512], BF16)) for i in range(2)]
                sq2 = [es2.enter_context(nc.sbuf_tensor("a_sq0", [128, 512], BF16))]
                rstdb = [es2.enter_context(nc.sbuf_tensor(f"a_rstdb{i}", [128, 512], F32)) for i in range(2)]
                mrb = [es2.enter_context(nc.sbuf_tensor(f"a_mrb{i}", [128, 512], F32)) for i in range(2)]
                xn2 = es2.enter_context(nc.sbuf_tensor("a_xn2", [128, 512], F32))
                wkb = es2.enter_context(nc.sbuf_tensor("a_wkb", [128, c.NI, c.KVDIM], BF16))
                wvb = es2.enter_context(nc.sbuf_tensor("a_wvb", [128, c.NI, c.KVDIM], BF16))
                kev = [es2.enter_context(nc.sbuf_tensor(f"a_kev{i}", [128, 512], BF16)) for i in range(2)]
                vev = [es2.enter_context(nc.sbuf_tensor(f"a_vev{i}", [128, c.VW], BF16)) for i in range(2)]
                ksq = es2.enter_context(nc.sbuf_tensor("a_ksq", [128, 512], BF16))
                knt = es2.enter_context(nc.sbuf_tensor("a_knt", [1, 512], BF16))
                kngbuf = es2.enter_context(nc.sbuf_tensor("a_kngbuf", [1, c.NTOK], BF16))

                c_s.wait(SY)
                c_s.wait(GP)
                g_misc.put(wkb[:], i_wk[:])
                g_misc.put(wvb[:], i_wv[:])
                nb = c.NTOK // 512
                kv_snap = {}
                ps1_reader = [0]
                nrm_v = [0]
                nrm_g = [0]
                aa_ln = [0]

                def stats_stage(tb2, i):
                    sl = i % 2
                    it = tb2 * c.NI + i
                    if it >= 2:
                        GP.wait_ge(tms.sem, stat_mm[it - 2])
                    g_x.put(xc2[sl][:], d_x2[i * 128:(i + 1) * 128, tb2 * 512:tb2 * 512 + 512])
                    g_x.wait_put(VE)
                    if it >= 1:
                        VE.wait_ge(tms.sem, stat_mm[it - 1])
                    v1 = vms.inc(VE.tensor_mul(sq2[0][:], xc2[sl][:], xc2[sl][:]))
                    g_x.wait_put(TE)
                    TE.wait_ge(vms.sem, v1)
                    TE.matmul(PS[0][0:1, :], sb_ones_c[:], xc2[sl][:], start=(i == 0), stop=(i == c.NI - 1))
                    stat_mm[it] = tms.inc(TE.matmul(PS[2][0:1, :], sb_ones_c[:], sq2[0][:],
                                                    start=(i == 0), stop=(i == c.NI - 1)))

                stat_mm = {}
                for i in range(c.NI):
                    stats_stage(0, i)
                aa_ln[0] = ln_stats_512(rstdb[0], mrb[0], PS[0], PS[2], 0)
                nrm = {}
                nrm_sem = {}
                s_n.align(3)
                for tb in range(nb):
                    t0 = tb * 512
                    xn = [xb[0], xb[1], xn2]
                    aa = aa_ln[0]
                    for i in range(c.NI):
                        it = tb * c.NI + i
                        sl = it % 3
                        on_gp = (sl == 2)
                        EN = GP if on_gp else VE
                        mych = gstep if on_gp else vstep
                        myms = gms if on_gp else vms
                        if it >= 3:
                            SY.wait_ge(nrm_sem[it - 3], nrm[it - 3])
                        s_n.put(xn[sl][:], d_x2[i * 128:(i + 1) * 128, t0:t0 + 512])
                        s_n.wait_put(EN)
                        EN.wait_ge(ams.sem, aa)
                        if tb > 0 and i < 3:
                            s_y2w.wait(EN)
                        if i < 3:
                            tms.wait(EN)
                        mych(EN.tensor_mul(xn[sl][:], xn[sl][:], rstdb[tb % 2][:]))
                        mych(EN.tensor_sub(y2b[:, i, :], xn[sl][:], mrb[tb % 2][:]))
                        EN.sem_inc(myms.sem, 1)
                        myms.n += 1
                        nrm[it] = myms.n
                        nrm_sem[it] = myms.sem
                    nrm_v[0] = vms.n
                    nrm_g[0] = gms.n
                    SC.wait_ge(vms.sem, nrm_v[0])
                    SC.wait_ge(gms.sem, nrm_g[0])
                    np4 = max(1, c.NI // 4)
                    for part in range(cdiv(c.NI, np4)):
                        i0, i1 = part * np4, min((part + 1) * np4, c.NI)
                        s_y2w.put(d_y2[i0:i1, :, t0:t0 + 512].rearrange("a p t -> p a t"),
                                  y2b[:, i0:i1, :], rot=(part == 0))
                    st_stages = []
                    if tb + 1 < nb:
                        for i in range(c.NI):
                            st_stages.append((lambda tb2, ii: lambda: stats_stage(tb2, ii))(tb + 1, i))
                    g_misc.wait(TE)
                    TE.wait_ge(vms.sem, nrm_v[0])
                    TE.wait_ge(gms.sem, nrm_g[0])
                    for h in range(c.KVH):
                        sl = h % 2
                        if ps1_reader[0]:
                            TE.wait_ge(ams.sem, ps1_reader[0])
                        for i in range(c.NI):
                            last = TE.matmul(PS[1][:, :], wkb[:, i, h * 128:(h + 1) * 128], y2b[:, i, :],
                                             start=(i == 0), stop=(i == c.NI - 1))
                        tt = tms.inc(last)
                        SC.wait_ge(tms.sem, tt)
                        if ("k", sl) in kv_snap:
                            g_ks.wait_at(SC, kv_snap[("k", sl)])
                        if ("ksq", sl) in kv_snap:
                            SC.wait_ge(vms.sem, kv_snap[("ksq", sl)])
                        aa2 = ams.inc(SC.copy(kev[sl][:], PS[1][:, :]))
                        ps1_reader[0] = aa2
                        GP.wait_ge(ams.sem, aa2)
                        off_k = h * 128 * c.NTOK
                        kv_snap[("k", sl)] = g_ks.put(
                            kvin_flat[off_k:off_k + 128 * c.NTOK].rearrange("(p t) -> p t", p=128)[:, t0:t0 + 512],
                            kev[sl][:])
                        VE.wait_ge(ams.sem, aa2)
                        if h >= 1 or tb >= 1:
                            VE.wait_ge(tms.sem, kv_snap.get(("knmm", 0), 0))
                        v1 = vms.inc(VE.tensor_mul(ksq[:], kev[sl][:], kev[sl][:]))
                        kv_snap[("ksq", sl)] = v1
                        TE.wait_ge(vms.sem, v1)
                        tt = tms.inc(TE.matmul(PS[3][0:1, :], sb_ones_c[:], ksq[:], start=True, stop=True))
                        kv_snap[("knmm", 0)] = tt
                        SC.wait_ge(tms.sem, tt)
                        if "kn" in kv_snap:
                            g_kn.wait_at(SC, kv_snap["kn"])
                        aa_kn = ams.inc(SC.copy(knt[:], PS[3][0:1, :]))
                        GP.wait_ge(ams.sem, aa_kn)
                        off_kn = 2 * kblk + h * c.NTOK + t0
                        kv_snap["kn"] = g_kn.put(
                            kvin_flat[off_kn:off_kn + 512].rearrange("(a b) -> a b", a=1), knt[:])
                        for _ in range(2):
                            if st_stages:
                                st_stages.pop(0)()
                    for u in range(4):
                        for hf in range(c.VH):
                            sl = (u * c.VH + hf) % 2
                            if ps1_reader[0]:
                                TE.wait_ge(ams.sem, ps1_reader[0])
                            for i in range(c.NI):
                                last = TE.matmul(PS[1][:, 0:c.VW], y2b[:, i, u * 128:(u + 1) * 128],
                                                 wvb[:, i, hf * c.VW:(hf + 1) * c.VW],
                                                 start=(i == 0), stop=(i == c.NI - 1))
                            tt = tms.inc(last)
                            SC.wait_ge(tms.sem, tt)
                            if ("v", sl) in kv_snap:
                                g_vs.wait_at(SC, kv_snap[("v", sl)])
                            aa2 = ams.inc(SC.copy(vev[sl][:], PS[1][:, 0:c.VW]))
                            ps1_reader[0] = aa2
                            GP.wait_ge(ams.sem, aa2)
                            for hh in range(c.VW // 128):
                                h = hf * (c.VW // 128) + hh
                                off_v = kblk + h * c.NTOK * 128 + (tb * 4 + u) * 128 * 128
                                kv_snap[("v", sl)] = g_vs.put(
                                    kvin_flat[off_v:off_v + 16384].rearrange("(p q) -> p q", p=128),
                                    vev[sl][:, hh * 128:(hh + 1) * 128], rot=(hh == 0))
                            for _ in range(2):
                                if st_stages:
                                    st_stages.pop(0)()
                    for st in st_stages:
                        st()
                    if tb + 1 < nb:
                        aa_ln[0] = ln_stats_512(rstdb[(tb + 1) % 2], mrb[(tb + 1) % 2], PS[0], PS[2], 0)
                g_ks.wait(GP)
                g_vs.wait(GP)
                g_kn.wait(GP)
                ccs.inc(GP.collective_compute(
                    "AllGather", ALU.bypass,
                    replica_groups=[[0, 1], [2, 3], [4, 5], [6, 7]],
                    ins=[t_kvin.ap().opt()], outs=[t_kvout.ap().opt()]))
                ccs.wait(SY)
                vstep(VE.memset(sb_kmax2[:], 0.0))
                for rank in range(2):
                    for h in range(c.KVH):
                        vms.wait(SY)
                        off_kn = rank * cc_elems + 2 * kblk + h * c.NTOK
                        s_g.put(kngbuf[:], kvout_flat[off_kn:off_kn + c.NTOK].rearrange("(a b) -> a b", a=1))
                        s_g.wait_put(VE)
                        vstep(VE.tensor_reduce(sb_tiny[0:1, 20:21], kngbuf[:], AX.X, ALU.max))
                        vstep(VE.tensor_tensor(sb_kmax2[0:1, h:h + 1], sb_kmax2[0:1, h:h + 1],
                                               sb_tiny[0:1, 20:21], ALU.max))
                        VE.sem_inc(vms.sem, 1)
                        vms.n += 1
            barrier()

            wqb = [es.enter_context(nc.sbuf_tensor(f"a_wq{i}", [128, c.NI, 128], BF16)) for i in range(2)]
            qsb = [es.enter_context(nc.sbuf_tensor(f"a_q{i}", [128, 512], BF16)) for i in range(2)]
            qsq = es.enter_context(nc.sbuf_tensor("a_qsq", [128, 512], BF16))
            kch = [[es.enter_context(nc.sbuf_tensor(f"a_k{p}{i}", [128, 512], BF16)) for i in range(2)]
                   for p in range(2)]
            vch = [[es.enter_context(nc.sbuf_tensor(f"a_v{p}{i}", [128, 4, 128], BF16)) for i in range(2)]
                   for p in range(2)]
            pbuf = [es.enter_context(nc.sbuf_tensor(f"a_p{i}", [128, 512], BF16)) for i in range(3)]
            negm = [es.enter_context(nc.sbuf_tensor(f"a_negm{i}", [128, 1], F32)) for i in range(2)]
            recb = es.enter_context(nc.sbuf_tensor("a_recb", [128, 512], F32))
            ost = es.enter_context(nc.sbuf_tensor("a_ost", [128, c.NQH, 512], BF16))
            wwo = [es.enter_context(nc.sbuf_tensor(f"a_wwo{i}", [128, c.NQH, 128], BF16)) for i in range(2)]
            xrb = [es.enter_context(nc.sbuf_tensor(f"a_xr{i}", [128, 512], F32)) for i in range(2)]
            res = [es.enter_context(nc.sbuf_tensor(f"a_res{i}", [128, 512], F32)) for i in range(2)]

            def k_chunk_ap(rank, h, cq):
                base = rank * cc_elems + h * 128 * c.NTOK
                return kvout_flat[base:base + 128 * c.NTOK].rearrange(
                    "(p t) -> p t", p=128)[:, cq * 512:(cq + 1) * 512]

            def v_chunk_ap(rank, h, cq):
                base = rank * cc_elems + kblk + h * c.NTOK * 128 + cq * 512 * 128
                return kvout_flat[base:base + 512 * 128].rearrange(
                    "(a p q) -> p a q", p=128, a=4)

            s_y2w.wait(SY)
            c_s.wait(SY)
            wo_last_cnt = 0
            wo_st = []
            prev_buf_rs = {}
            qp_done = {}
            qn_read = {}
            qev_map = {}
            tqn_map = {}
            lastS_map = {}
            lastRS_map = {}
            vrec_map = {}
            onorm_map = {}
            obank = lambda hg: 5 if hg % 2 == 0 else 7
            for qb in range(c.QBN):
                i0 = qb * c.QBTILE
                E = 2 * (i0 + c.QBTILE)
                NCH = cdiv(E, 8)
                tms.wait(SY)
                np4 = max(1, c.NI // 4)
                for part in range(cdiv(c.NI, np4)):
                    p0, p1 = part * np4, min((part + 1) * np4, c.NI)
                    s_y.put(y2b[:, p0:p1, :],
                            d_y2[p0:p1, :, qb * 512:(qb + 1) * 512].rearrange("a p t -> p a t"),
                            rot=(part == 0))

                def bound_stages(hg2):
                    h2 = hg2 // c.G
                    sl2 = hg2 % 2
                    key = (qb, hg2)

                    def s0():
                        pk = (qb, hg2 - 2) if hg2 >= 2 else (qb - 1, c.NQH - 2 + hg2)
                        if pk in qp_done:
                            GP.wait_ge(tms.sem, qp_done[pk])
                        g_wq.put(wqb[sl2][:], i_wq[hg2])
                        g_wq.wait_put(TE)
                        ams.wait(TE)
                        pk2 = (qb, hg2 - 1) if hg2 >= 1 else (qb - 1, c.NQH - 1)
                        TE.wait_ge(vms.sem, qn_read.get(pk2, 0))
                        sk = (qb, hg2 - 2) if hg2 >= 2 else (qb - 1, c.NQH - 2 + hg2)
                        if sk in lastS_map:
                            TE.wait_ge(tms.sem, lastS_map[sk])
                        s_y.wait_put(TE)
                        for i in range(c.NI):
                            last = TE.matmul(PS[0][:, :], wqb[sl2][:, i, :], y2b[:, i, :],
                                             start=(i == 0), stop=(i == c.NI - 1))
                        qp_done[key] = tms.inc(last)

                    def s1():
                        SC.wait_ge(tms.sem, qp_done[key])
                        qev_map[key] = ams.inc(SC.activation(qsb[sl2][:], PS[0][:, :], AF.Copy, scale=ISQ_HD))

                    def s2():
                        VE.wait_ge(ams.sem, qev_map[key])
                        pk2 = (qb, hg2 - 1) if hg2 >= 1 else (qb - 1, c.NQH - 1)
                        if pk2 in tqn_map:
                            VE.wait_ge(tms.sem, tqn_map[pk2])
                        qn_read[("v1", key)] = vms.inc(VE.tensor_mul(qsq[:], qsb[sl2][:], qsb[sl2][:]))

                    def s3():
                        TE.wait_ge(vms.sem, qn_read[("v1", key)])
                        tqn_map[key] = tms.inc(TE.matmul(PS[0][0:1, :], sb_ones_c[:], qsq[:],
                                                         start=True, stop=True))

                    def s4():
                        VE.wait_ge(tms.sem, tqn_map[key])
                        vstep(VE.tensor_reduce(sb_tiny[0:1, 24:25], PS[0][0:1, :], AX.X, ALU.max))
                        vms.inc(VE.tensor_mul(sb_tiny[0:1, 25:26], sb_tiny[0:1, 24:25],
                                              sb_kmax2[0:1, h2:h2 + 1]))
                        qn_read[key] = vms.n

                    def s5():
                        SC.wait_ge(vms.sem, qn_read[key])
                        astep(SC.activation(sb_tiny[0:1, 26:27], sb_tiny[0:1, 25:26], AF.Ln))
                        qn_read[("a2", key)] = ams.inc(SC.activation(sb_tiny[0:1, 27:28],
                                                                     sb_tiny[0:1, 26:27], AF.Exp, scale=0.5))

                    def s6():
                        VE.wait_ge(ams.sem, qn_read[("a2", key)])
                        qn_read[("v3", key)] = vms.inc(VE.tensor_scalar(
                            sb_rowb[0:1, 2047:2048], sb_tiny[0:1, 27:28], -1.0, None, ALU.mult))

                    def s7():
                        TE.wait_ge(vms.sem, qn_read[("v3", key)])
                        qn_read[("t3", key)] = tms.inc(TE.matmul(
                            PS[0][:, 0:1], sb_ones_r[:], sb_rowb[0:1, 2047:2048], start=True, stop=True))

                    def s8():
                        SC.wait_ge(tms.sem, qn_read[("t3", key)])
                        astep(SC.copy(negm[sl2][:], PS[0][:, 0:1]))
                        SC.sem_inc(ams.sem, 1)
                        ams.n += 1

                    return [s0, s1, s2, s3, s4, s5, s6, s7, s8]

                def tail_stages(hgp):
                    key = (qb, hgp)
                    ob = obank(hgp)

                    def t0():
                        VE.wait_ge(tms.sem, lastRS_map[key])
                        vstep(VE.reciprocal(sb_row[0:1, 1024:1536], PS[4][0:1, :]))
                        VE.sem_inc(vms.sem, 1)
                        vms.n += 1
                        vrec_map[key] = vms.n
                        vstep(VE.tensor_copy(sb_rowb[0:1, 0:512], sb_row[0:1, 1024:1536]))
                        qn_read[("v4", key)] = vms.inc(VE.tensor_sub(
                            sb_rowb[0:1, 512:1024], sb_row[0:1, 1024:1536], sb_rowb[0:1, 0:512]))

                    def t1():
                        TE.wait_ge(vms.sem, qn_read[("v4", key)])
                        TE.matmul(PS[6][:, :], sb_ones_r[:], sb_rowb[0:1, 0:512], start=True, stop=False)
                        qn_read[("t5", key)] = tms.inc(TE.matmul(
                            PS[6][:, :], sb_ones_r[:], sb_rowb[0:1, 512:1024], start=False, stop=True))

                    def t2():
                        SC.wait_ge(tms.sem, qn_read[("t5", key)])
                        pk = (qb, hgp - 1) if hgp >= 1 else (qb - 1, c.NQH - 1)
                        SC.wait_ge(vms.sem, onorm_map.get(pk, 0))
                        qn_read[("a5", key)] = ams.inc(SC.copy(recb[:], PS[6][:, :]))

                    def t3s():
                        VE.wait_ge(ams.sem, qn_read[("a5", key)])
                        if hgp == 0 and qb > 0:
                            VE.wait_ge(tms.sem, wo_last_cnt)
                        onorm_map[key] = vms.inc(VE.tensor_tensor(
                            ost[:, hgp, :], PS[ob][:, :], recb[:], ALU.mult))

                    return [t0, t1, t2, t3s]

                for st in bound_stages(0):
                    st()
                for hg in range(c.NQH):
                    h = hg // c.G
                    sl = hg % 2
                    ob = obank(hg)
                    pending = []
                    tl = tail_stages(hg - 1) if hg >= 1 else []
                    bd = bound_stages(hg + 1) if hg + 1 < c.NQH else []
                    order = []
                    if tl:
                        order.append(tl[0])
                    order += bd[0:4]
                    if tl:
                        order.append(tl[1])
                    order += bd[4:6]
                    if tl:
                        order.append(tl[2])
                    order += bd[6:8]
                    if tl:
                        order.append(tl[3])
                    order += bd[8:]
                    pending = order
                    last_S = {}
                    last_exp = {}
                    last_rs = {}
                    ch_lastS = {}
                    ch_lastRS = {}
                    ksnap = {}
                    vsnap = {}
                    s_k.align(4)
                    s_v.align(4)
                    prev_key = (qb, hg - 1) if hg >= 1 else (qb - 1, c.NQH - 1)
                    prevRS = lastRS_map.get(prev_key, 0)

                    def load_kv_chunk(p, cq):
                        cb = cq % 2
                        if (p, cq - 2) in ch_lastRS:
                            SY.wait_ge(tms.sem, ch_lastRS[(p, cq - 2)])
                        else:
                            g = prev_buf_rs.get((p, cb), prevRS)
                            if g:
                                SY.wait_ge(tms.sem, g)
                        ksnap[(p, cq)] = s_k.put(kch[p][cb][:], k_chunk_ap(p, h, cq))
                        vsnap[(p, cq)] = s_v.put(vch[p][cb][:], v_chunk_ap(p, h, cq))

                    def emit_S(j):
                        sfx = max(0, cdiv(j - 1, 2) - i0) if j >= 1 else 0
                        p, jj = j % 2, j // 2
                        cq, a = jj // 4, jj % 4
                        bank = 1 + j % 3
                        if j >= 3:
                            TE.wait_ge(ams.sem, last_exp[j - 3])
                        if a == 0:
                            s_k.wait_at(TE, ksnap[(p, cq)])
                        last = TE.matmul(PS[bank][:, 128 * sfx:512], kch[p][cq % 2][:, a * 128:(a + 1) * 128],
                                         qsb[sl][:, 128 * sfx:512], start=True, stop=True)
                        last_S[j] = tms.inc(last)
                        ch_lastS[(p, cq)] = last_S[j]
                        return sfx

                    load_kv_chunk(0, 0)
                    load_kv_chunk(1, 0)
                    if NCH > 1:
                        load_kv_chunk(0, 1)
                        load_kv_chunk(1, 1)
                    geom = {0: emit_S(0)}
                    if E > 1:
                        geom[1] = emit_S(1)
                    for j in range(E):
                        p, jj = j % 2, j // 2
                        bank = 1 + j % 3
                        sfx = geom[j]
                        if j + 2 < E:
                            geom[j + 2] = emit_S(j + 2)
                        if pending:
                            pending.pop(0)()
                        masked = (j >= 2 * i0)
                        if masked:
                            VE.wait_ge(tms.sem, last_S[j])
                            mk = 0 if (j % 2 == 0) else 1
                            mv = vms.inc(VE.tensor_tensor(PS[bank][:, 128 * sfx:128 * sfx + 128],
                                                          PS[bank][:, 128 * sfx:128 * sfx + 128],
                                                          sb_masks[:, mk, :], ALU.add))
                            SC.wait_ge(vms.sem, mv)
                        else:
                            SC.wait_ge(tms.sem, last_S[j])
                        if j >= 3:
                            SC.wait_ge(tms.sem, last_rs[j - 3])
                        elif prevRS:
                            SC.wait_ge(tms.sem, prevRS)
                        last_exp[j] = ams.inc(SC.activation(pbuf[j % 3][:, 128 * sfx:512],
                                                            PS[bank][:, 128 * sfx:512],
                                                            AF.Exp, bias=negm[sl][:]))
                        cq, a = jj // 4, jj % 4
                        if a == 0:
                            s_v.wait_at(TE, vsnap[(p, cq)])
                        TE.wait_ge(ams.sem, last_exp[j])
                        if j == 0:
                            pk2 = (qb, hg - 2) if hg >= 2 else (qb - 1, c.NQH - 2 + hg)
                            TE.wait_ge(vms.sem, onorm_map.get(pk2, 0))
                            TE.wait_ge(vms.sem, vrec_map.get(prev_key, 0))
                        TE.matmul(PS[ob][:, 128 * sfx:512], vch[p][cq % 2][:, a, :],
                                  pbuf[j % 3][:, 128 * sfx:512], start=(j == 0), stop=(j == E - 1))
                        rs = TE.matmul(PS[4][0:1, 128 * sfx:512], sb_ones_c[:],
                                       pbuf[j % 3][:, 128 * sfx:512], start=(j == 0), stop=(j == E - 1))
                        last_rs[j] = tms.inc(rs)
                        ch_lastRS[(p, jj // 4)] = last_rs[j]
                        if jj % 4 == 3 and p == 1 and (jj // 4) + 2 < NCH:
                            load_kv_chunk(0, jj // 4 + 2)
                            load_kv_chunk(1, jj // 4 + 2)
                    for st in pending:
                        st()
                    lastS_map[(qb, hg)] = last_S[E - 1]
                    lastRS_map[(qb, hg)] = last_rs[E - 1]
                    prev_buf_rs = {}
                    for (p_, cq_), v_ in ch_lastRS.items():
                        k_ = (p_, cq_ % 2)
                        prev_buf_rs[k_] = max(prev_buf_rs.get(k_, 0), v_)
                for st in tail_stages(c.NQH - 1):
                    st()
                onorm = onorm_map[(qb, c.NQH - 1)]
                stt_hist = []
                wo_hist = []
                for o in range(c.NI):
                    sl = o % 2
                    if len(wo_hist) >= 2:
                        GP.wait_ge(tms.sem, wo_hist[-2])
                    elif wo_last_cnt:
                        GP.wait_ge(tms.sem, wo_last_cnt)
                    g_wwo.put(wwo[sl][:], i_wo[o])
                    if o >= 2:
                        SY.wait_ge(vms.sem, stt_hist[-2])
                    elif qb > 0:
                        vms.wait(SY)
                    s_r.put(xrb[sl][:], d_x2[o * 128:(o + 1) * 128, qb * 512:(qb + 1) * 512])
                    g_wwo.wait_put(TE)
                    if o == 0:
                        TE.wait_ge(vms.sem, onorm)
                    pb = 6 + (o % 2)
                    if o >= 2:
                        TE.wait_ge(vms.sem, stt_hist[-2])
                    for hd in range(c.NQH):
                        last = TE.matmul(PS[pb][:, :], wwo[sl][:, hd, :], ost[:, hd, :],
                                         start=(hd == 0), stop=(hd == c.NQH - 1))
                    wo_last_cnt = tms.inc(last)
                    wo_hist.append(wo_last_cnt)
                    VE.wait_ge(tms.sem, wo_last_cnt)
                    s_r.wait_put(VE)
                    if len(wo_st) >= 2:
                        c_s.wait_at(VE, wo_st[-2])
                    vv = vms.inc(VE.scalar_tensor_tensor(res[sl][:], PS[pb][:, :], 1.0,
                                                         xrb[sl][:], ALU.mult, ALU.add))
                    stt_hist.append(vv)
                    SC.wait_ge(vms.sem, vv)
                    wo_st.append(c_s.put(d_x3[o * 128:(o + 1) * 128, qb * 512:(qb + 1) * 512], res[sl][:]))
            c_s.wait(SC)
        return

    ffn_phase("f1", i_x, d_x2, t_ys[0], t_yso[0])
    barrier()
    attn_phase()
    barrier()
    ffn_phase("f3", d_x3, o_out, t_ys[1], t_yso[1])
    c_s.wait(SC)
    g_s.wait(GP)
    s_y2w.wait(SC)
    barrier()
    return nc


def _host_prep(inputs, cfg: CFG):
    c = cfg
    x = np.asarray(inputs["L0_attn_x"], dtype=np.float32)
    Wq = np.asarray(inputs["L0_attn_Wq"], dtype=np.float32)
    Wk = np.asarray(inputs["L0_attn_Wk"], dtype=np.float32)
    Wv = np.asarray(inputs["L0_attn_Wv"], dtype=np.float32)
    Wo = np.asarray(inputs["L0_attn_Wo"], dtype=np.float32)
    Worg = np.asarray(inputs["L0_ffn_W_org"], dtype=np.float32)
    down = np.asarray(inputs["L0_ffn_lora_down"], dtype=np.float32)
    up = np.asarray(inputs["L0_ffn_lora_up"], dtype=np.float32)
    keys = np.asarray(inputs["L0_ffn_lora_keys"], dtype=np.float32)
    Wqf = np.asarray(inputs["L0_ffn_Wq"], dtype=np.float32)

    NI = c.NI

    def blk_lhsT(wT):
        K, M = wT.shape
        return np.ascontiguousarray(
            wT.reshape(K // 128, 128, M // 128, 128).transpose(2, 1, 0, 3))

    def blk_rhs(wT):
        K, N = wT.shape
        return np.ascontiguousarray(wT.reshape(K // 128, 128, N).transpose(1, 0, 2))

    worg4 = blk_lhsT(Worg.T)
    wq4 = blk_lhsT(Wq.T)
    wo4 = blk_lhsT(Wo.T)
    wk4 = blk_rhs(Wk.T)
    wv4 = blk_rhs(Wv.T)
    downp = np.concatenate([down[0::2], down[1::2]], axis=2)
    down4 = np.ascontiguousarray(
        downp.reshape(c.NP, NI, 128, 128).transpose(0, 2, 1, 3))
    uppair = np.stack([np.concatenate([up[2 * p], up[2 * p + 1]], axis=0)
                       for p in range(c.NP)])
    up4 = np.ascontiguousarray(
        uppair.reshape(c.NP, 128, NI, 128).transpose(2, 1, 0, 3))
    wqf4 = blk_rhs(Wqf.T)
    wqf4 = np.ascontiguousarray(Wqf.T.reshape(NI, 128, 128).transpose(1, 0, 2))
    keysT = np.ascontiguousarray(keys.T)

    import ml_dtypes
    kl = np.arange(128)[:, None]
    ql = np.arange(128)[None, :]
    bf = ml_dtypes.bfloat16
    tri = np.where(kl <= ql, 0.0, NEG).astype(bf)
    masks = [np.stack([tri, np.full((128, 128), NEG, bf)]),
             np.stack([np.zeros((128, 128), bf), tri])]

    in_maps = []
    for core in range(8):
        b, par = core // 2, core % 2
        sel = x[b].reshape(2 * c.NTILE, 128, c.D)[par::2].reshape(c.NTOK, c.D)
        xT = np.ascontiguousarray(sel.T)
        in_maps.append({
            "x": xT, "masks": masks[par], "worg4": worg4, "wq4": wq4,
            "wo4": wo4, "wk4": wk4, "wv4": wv4, "down4": down4, "up4": up4,
            "wqf4": wqf4, "keysT": keysT,
        })
    return in_maps


def _host_assemble(results, cfg: CFG):
    c = cfg
    out = np.empty((4, c.SEQ, c.D), dtype=np.float32)
    for core in range(8):
        b, par = core // 2, core % 2
        oT = np.asarray(results[core]["out"])
        rows = np.ascontiguousarray(oT.T)
        out[b].reshape(2 * c.NTILE, 128, c.D)[par::2] = rows.reshape(c.NTILE, 128, c.D)
    return out


def kernel(**inputs):
    cfg = CFG(mini=False)
    in_maps = _host_prep(inputs, cfg)
    nc = build(cfg)
    res = run_bass_kernel_spmd(nc, in_maps, core_ids=list(range(8)))
    return _host_assemble(res.results, cfg)


# version 10
# speedup vs baseline: 1.2965x; 1.0846x over previous
import math
import contextlib
import numpy as np

import concourse.bass as bass
import concourse.mybir as mybir
from concourse.bass_utils import run_bass_kernel_spmd

F32 = mybir.dt.float32
BF16 = mybir.dt.bfloat16
AF = mybir.ActivationFunctionType
ALU = mybir.AluOpType
AX = mybir.AxisListType

NEG = -1.0e30


class CFG:
    def __init__(self, mini=False, midi=False):
        if midi:
            self.D = 1024
            self.NTOK = 2048
            self.KVH = 8
            self.G = 1
        elif mini:
            self.D = 512
            self.NTOK = 512
            self.KVH = 2
            self.G = 2
        else:
            self.D = 4096
            self.NTOK = 2048
            self.KVH = 8
            self.G = 4
        self.HD = 128
        self.NI = self.D // 128
        self.NTILE = self.NTOK // 128
        self.QBN = max(1, self.NTOK // 512)
        self.QBT = self.NTOK // self.QBN
        self.QBTILE = self.QBT // 128
        self.TBN = max(1, self.NTOK // 1024)
        self.TBT = self.NTOK // self.TBN
        self.NH = self.TBT // 512
        self.E = 8
        self.R = 64
        self.LK = 128
        self.NP = 4
        self.KVDIM = self.KVH * self.HD
        self.NQH = self.KVH * self.G
        self.VH = max(1, self.KVDIM // 512)
        self.VW = self.KVDIM // self.VH
        self.SEQ = 2 * self.NTOK


class Ctr:
    def __init__(self, nc, name):
        self.sem = nc.alloc_semaphore(name)
        self.n = 0

    def inc(self, instr, k=1):
        instr.then_inc(self.sem, k)
        self.n += k
        return self.n

    def wait(self, eng, val=None):
        eng.wait_ge(self.sem, self.n if val is None else val)


class Stream:

    def __init__(self, nc, name, eng, nsem=2):
        self.sems = [nc.alloc_semaphore(f"{name}{i}") for i in range(nsem)]
        self.cnts = [0] * nsem
        self.eng = eng
        self.nsem = nsem
        self.ng = 0

    def put(self, out_ap, in_ap, rot=True, **kw):
        if rot:
            self.ng += 1
        k = self.ng % self.nsem
        self.eng.dma_start(out_ap, in_ap, **kw).then_inc(self.sems[k], 16)
        self.cnts[k] += 16
        return (k, self.cnts[k])

    def snap(self):
        k = self.ng % self.nsem
        return (k, self.cnts[k])

    def wait(self, eng):
        for k in range(self.nsem):
            if self.cnts[k]:
                eng.wait_ge(self.sems[k], self.cnts[k])

    def wait_put(self, eng):
        k = self.ng % self.nsem
        eng.wait_ge(self.sems[k], self.cnts[k])

    def wait_at(self, eng, sn):
        eng.wait_ge(self.sems[sn[0]], sn[1])

    def align(self, k):
        self.ng += (-self.ng) % k


def cdiv(a, b):
    return (a + b - 1) // b


def build(cfg: CFG):
    c = cfg
    nc = bass.Bass("TRN2", target_bir_lowering=False, debug=False, num_devices=8)

    i_x = nc.dram_tensor("x", [c.D, c.NTOK], F32, kind="ExternalInput").ap()
    i_masks = nc.dram_tensor("masks", [2, 128, 128], BF16, kind="ExternalInput").ap()
    i_worg = nc.dram_tensor("worg4", [c.NI, 128, c.NI, 128], F32, kind="ExternalInput").ap()
    i_wq = nc.dram_tensor("wq4", [c.NQH, 128, c.NI, 128], F32, kind="ExternalInput").ap()
    i_wo = nc.dram_tensor("wo4", [c.NI, 128, c.NQH, 128], F32, kind="ExternalInput").ap()
    i_wk = nc.dram_tensor("wk4", [128, c.NI, c.KVDIM], F32, kind="ExternalInput").ap()
    i_wv = nc.dram_tensor("wv4", [128, c.NI, c.KVDIM], F32, kind="ExternalInput").ap()
    i_down = nc.dram_tensor("down4", [c.NP, 128, c.NI, 128], F32, kind="ExternalInput").ap()
    i_up = nc.dram_tensor("up4", [c.NI, 128, c.NP, 128], F32, kind="ExternalInput").ap()
    i_wqf = nc.dram_tensor("wqf4", [128, c.NI, 128], F32, kind="ExternalInput").ap()
    i_keys = nc.dram_tensor("keysT", [c.LK, c.E], F32, kind="ExternalInput").ap()
    o_out = nc.dram_tensor("out", [c.D, c.NTOK], F32, kind="ExternalOutput").ap()

    d_x2 = nc.dram_tensor("x2d", [c.D, c.NTOK], F32, kind="Internal").ap()
    d_x3 = nc.dram_tensor("x3d", [c.D, c.NTOK], F32, kind="Internal").ap()
    d_y2 = nc.dram_tensor("y2d", [c.NI, 128, c.NTOK], BF16, kind="Internal").ap()
    kblk = c.KVH * c.NTILE * 128 * 128
    knorm_elems = c.KVH * c.NTOK
    cc_rows = cdiv(2 * kblk + knorm_elems, 2048)
    cc_elems = cc_rows * 2048
    t_kvin = nc.dram_tensor("kvin", [cc_rows, 2048], BF16, kind="Internal")
    t_kvout = nc.dram_tensor("kvout", [2 * cc_rows, 2048], BF16, kind="Internal")
    kvin_flat = t_kvin.ap().rearrange("a b -> (a b)")
    kvout_flat = t_kvout.ap().rearrange("a b -> (a b)")
    t_ys = [nc.dram_tensor(f"ysi{i}", [128, c.NI], F32, kind="Internal") for i in range(2)]
    t_yso = [nc.dram_tensor(f"yso{i}", [256, c.NI], F32, kind="Internal") for i in range(2)]

    def k_tile_in(h, j_loc):
        off = (h * c.NTILE + j_loc) * 16384
        return kvin_flat[off:off + 16384].rearrange("(p q) -> p q", p=128)

    def v_tile_in(h, j_loc):
        off = kblk + (j_loc * c.KVH + h) * 16384
        return kvin_flat[off:off + 16384].rearrange("(p q) -> p q", p=128)

    def knorm_in():
        off = 2 * kblk
        return kvin_flat[off:off + knorm_elems].rearrange("(a b) -> a b", a=1)

    def k_tile_out(rank, h, j_loc):
        off = rank * cc_elems + (h * c.NTILE + j_loc) * 16384
        return kvout_flat[off:off + 16384].rearrange("(p q) -> p q", p=128)

    def v_tile_out(rank, h, j_loc):
        off = rank * cc_elems + kblk + (j_loc * c.KVH + h) * 16384
        return kvout_flat[off:off + 16384].rearrange("(p q) -> p q", p=128)

    def knorm_out(rank):
        off = rank * cc_elems + 2 * kblk
        return kvout_flat[off:off + knorm_elems].rearrange("(a b) -> a b", a=1)

    SY, GP, SC, VE, TE = nc.sync, nc.gpsimd, nc.scalar, nc.vector, nc.tensor
    tms = Ctr(nc, "tms")
    vms = Ctr(nc, "vms")
    ams = Ctr(nc, "ams")
    vch = Ctr(nc, "vch")
    ach = Ctr(nc, "ach")
    gch = Ctr(nc, "gch")
    gms = Ctr(nc, "gms")

    def vstep(instr):
        vch.inc(instr)
        VE.wait_ge(vch.sem, vch.n)
        return instr

    def astep(instr):
        ach.inc(instr)
        SC.wait_ge(ach.sem, ach.n)
        return instr

    def gstep(instr):
        gch.inc(instr)
        GP.wait_ge(gch.sem, gch.n)
        return instr
    ccs = Ctr(nc, "ccs")
    bar = Ctr(nc, "bar")

    s_x = Stream(nc, "s_x", SY)
    s_n = Stream(nc, "s_n", SY, nsem=3)
    s_r = Stream(nc, "s_r", SY)
    s_y = Stream(nc, "s_y", SY)
    s_k = Stream(nc, "s_k", SY, nsem=4)
    s_v = Stream(nc, "s_v", SY, nsem=4)
    s_g = Stream(nc, "s_g", SY)
    g_x = Stream(nc, "g_x", GP)
    g_dpb = Stream(nc, "g_dpb", GP, nsem=1)
    g_wob = Stream(nc, "g_wob", GP)
    g_upb = Stream(nc, "g_upb", GP)
    g_wq = Stream(nc, "g_wq", GP)
    g_wwo = Stream(nc, "g_wwo", GP, nsem=3)
    g_misc = Stream(nc, "g_misc", GP, nsem=1)
    g_s = Stream(nc, "g_s", GP, nsem=1)
    g_kn = Stream(nc, "g_kn", GP, nsem=1)
    g_ks = Stream(nc, "g_ks", GP)
    g_vs = Stream(nc, "g_vs", GP)
    c_s = Stream(nc, "c_s", SC)
    s_y2w = Stream(nc, "s_y2w", SC)

    def barrier():
        for e in (TE, VE, SC, GP, SY):
            e.sem_inc(bar.sem, 1)
        bar.n += 5
        for e in (TE, VE, SC, GP, SY):
            e.wait_ge(bar.sem, bar.n)

    sb_keys = nc.alloc_sbuf_tensor("sb_keys", [c.LK, c.E], BF16)
    sb_masks = nc.alloc_sbuf_tensor("sb_masks", [128, 2, 128], BF16)
    sb_ones_r = nc.alloc_sbuf_tensor("sb_ones_r", [1, 128], BF16)
    sb_ones_c = nc.alloc_sbuf_tensor("sb_ones_c", [128, 1], BF16)
    sb_coefv = nc.alloc_sbuf_tensor("sb_coefv", [128, c.NP], F32)
    sb_kmax2 = nc.alloc_sbuf_tensor("sb_kmax2", [1, c.KVH], F32)
    sb_tiny = nc.alloc_sbuf_tensor("sb_tiny", [128, 32], F32)
    sb_row = nc.alloc_sbuf_tensor("sb_row", [1, 1536], F32)
    sb_rowb = nc.alloc_sbuf_tensor("sb_rowb", [1, 2048], BF16)
    sb_qcol = nc.alloc_sbuf_tensor("sb_qcol", [128, 1], BF16)

    PS = [nc.alloc_psum_tensor(f"ps{i}", [128, 512], F32) for i in range(8)]

    VE.memset(sb_ones_r[:], 1.0)
    VE.memset(sb_ones_c[:], 1.0)
    vms.inc(VE.memset(sb_coefv[:], 0.0))
    s_g.put(sb_masks[:, 0, :], i_masks[0])
    s_g.put(sb_masks[:, 1, :], i_masks[1], rot=False)
    g_misc.put(sb_keys[:], i_keys[:])

    ISQ_HD = 1.0 / math.sqrt(c.HD)
    ISQ_LK = 1.0 / math.sqrt(c.LK)

    R_M = 0

    def ln_stats_512(x_dst_rstdb, x_dst_mrb, ps_sum, ps_sq, col0):
        m_row = sb_row[0:1, 0:512]
        v_row = sb_row[0:1, 512:1024]
        t_row = sb_row[0:1, 1024:1536]
        tms.wait(VE)
        VE.tensor_scalar(m_row, ps_sum[0:1, :], 1.0 / c.D, None, ALU.mult)
        vstep(VE.tensor_scalar(v_row, ps_sq[0:1, :], 1.0 / c.D, None, ALU.mult))
        vstep(VE.tensor_mul(t_row, m_row, m_row))
        vv = vms.inc(VE.scalar_tensor_tensor(v_row, v_row, 1e-5, t_row, ALU.add, ALU.subtract))
        SC.wait_ge(vms.sem, vv)
        astep(SC.activation(t_row, v_row, AF.Ln))
        aa = ams.inc(SC.activation(v_row, t_row, AF.Exp, scale=-0.5))
        VE.wait_ge(ams.sem, aa)
        vstep(VE.tensor_mul(m_row, m_row, v_row))
        vstep(VE.tensor_copy(sb_rowb[0:1, 0:512], v_row))
        VE.tensor_sub(sb_rowb[0:1, 512:1024], v_row, sb_rowb[0:1, 0:512])
        vstep(VE.tensor_copy(sb_rowb[0:1, 1024:1536], m_row))
        vv = vms.inc(VE.tensor_sub(sb_rowb[0:1, 1536:2048], m_row, sb_rowb[0:1, 1024:1536]))
        TE.wait_ge(vms.sem, vv)
        TE.matmul(PS[4][:, :], sb_ones_r[:], sb_rowb[0:1, 0:512], start=True, stop=False)
        TE.matmul(PS[4][:, :], sb_ones_r[:], sb_rowb[0:1, 512:1024], start=False, stop=True)
        TE.matmul(PS[5][:, :], sb_ones_r[:], sb_rowb[0:1, 1024:1536], start=True, stop=False)
        tt = tms.inc(TE.matmul(PS[5][:, :], sb_ones_r[:], sb_rowb[0:1, 1536:2048], start=False, stop=True))
        SC.wait_ge(tms.sem, tt)
        SC.copy(x_dst_rstdb[:, col0:col0 + 512], PS[4][:, :])
        return ams.inc(SC.copy(x_dst_mrb[:, col0:col0 + 512], PS[5][:, :]))

    def ffn_phase(tag, x_src, x_dst, ysi, yso):
        es = contextlib.ExitStack()
        with es:
            y1 = es.enter_context(nc.sbuf_tensor(f"{tag}_y1", [128, c.NI, c.NTOK], BF16))
            xb = [es.enter_context(nc.sbuf_tensor(f"{tag}_xb{i}", [128, c.TBT], F32)) for i in range(2)]
            xc2 = [es.enter_context(nc.sbuf_tensor(f"{tag}_xc{i}", [128, c.TBT], BF16)) for i in range(2)]
            sq2 = [es.enter_context(nc.sbuf_tensor(f"{tag}_sq0", [128, c.TBT], BF16))]
            rstdb = es.enter_context(nc.sbuf_tensor(f"{tag}_rstdb", [128, c.TBT], F32))
            mrb = es.enter_context(nc.sbuf_tensor(f"{tag}_mrb", [128, c.TBT], F32))
            ysum = es.enter_context(nc.sbuf_tensor(f"{tag}_ysum", [128, c.NI], F32))
            xn2 = es.enter_context(nc.sbuf_tensor(f"{tag}_xn2", [128, c.TBT], F32))
            ysg = es.enter_context(nc.sbuf_tensor(f"{tag}_ysg", [128, 2 * c.NI], F32))
            ysb = es.enter_context(nc.sbuf_tensor(f"{tag}_ysb", [128, c.NI], BF16))
            hsb = es.enter_context(nc.sbuf_tensor(f"{tag}_h", [128, c.NP, c.NTOK], BF16))
            dpb = es.enter_context(nc.sbuf_tensor(f"{tag}_dp", [128, c.NI, 128], BF16))
            wob = [es.enter_context(nc.sbuf_tensor(f"{tag}_wo{i}", [128, c.NI, 128], BF16)) for i in range(2)]
            upb = [es.enter_context(nc.sbuf_tensor(f"{tag}_up{i}", [128, c.NP, 128], BF16)) for i in range(2)]
            res = [xn2]

            c_s.wait(SY)
            c_s.wait(GP)
            nrm = {}
            nrm_sem = {}
            s_n.align(3)
            ys0 = vms.inc(VE.memset(ysum[:], 0.0))

            stat_mm = {}
            for tb in range(c.TBN):
                t0 = tb * c.TBT
                for i in range(c.NI):
                    sl = i % 2
                    it = tb * c.NI + i
                    if it >= 2:
                        GP.wait_ge(tms.sem, stat_mm[it - 2])
                    g_x.put(xc2[sl][:], x_src[i * 128:(i + 1) * 128, t0:t0 + c.TBT])
                    g_x.wait_put(VE)
                    if it >= 1:
                        VE.wait_ge(tms.sem, stat_mm[it - 1])
                    v1 = vms.inc(VE.tensor_mul(sq2[0][:], xc2[sl][:], xc2[sl][:]))
                    g_x.wait_put(TE)
                    TE.wait_ge(vms.sem, v1)
                    for hh in range(c.NH):
                        TE.matmul(PS[0 + hh][0:1, :], sb_ones_c[:], xc2[sl][:, hh * 512:(hh + 1) * 512],
                                  start=(i == 0), stop=(i == c.NI - 1))
                        last = TE.matmul(PS[2 + hh][0:1, :], sb_ones_c[:], sq2[0][:, hh * 512:(hh + 1) * 512],
                                         start=(i == 0), stop=(i == c.NI - 1))
                    stat_mm[it] = tms.inc(last)
                for hh in range(c.NH):
                    aa = ln_stats_512(rstdb, mrb, PS[0 + hh], PS[2 + hh], hh * 512)
                xn = [xb[0], xb[1], xn2]
                for i in range(c.NI):
                    it = tb * c.NI + i
                    sl = it % 3
                    on_gp = (sl == 2)
                    EN = GP if on_gp else VE
                    mych = gstep if on_gp else vstep
                    myms = gms if on_gp else vms
                    if it >= 3:
                        SY.wait_ge(nrm_sem[it - 3], nrm[it - 3])
                    s_n.put(xn[sl][:], x_src[i * 128:(i + 1) * 128, t0:t0 + c.TBT])
                    s_n.wait_put(EN)
                    EN.wait_ge(ams.sem, aa)
                    if on_gp and tb == 0 and i == 2:
                        GP.wait_ge(vms.sem, ys0)
                    mych(EN.tensor_mul(xn[sl][:], xn[sl][:], rstdb[:]))
                    if on_gp:
                        mych(EN.tensor_sub(y1[:, i, t0:t0 + c.TBT], xn[sl][:], mrb[:]))
                        EN.sem_inc(myms.sem, 1)
                        myms.n += 1
                        VE.wait_ge(gms.sem, myms.n)
                        vstep(VE.tensor_reduce(sb_tiny[:, 1:2], y1[:, i, t0:t0 + c.TBT], AX.X, ALU.add))
                        vstep(VE.tensor_add(ysum[:, i:i + 1], ysum[:, i:i + 1], sb_tiny[:, 1:2]))
                        VE.sem_inc(vms.sem, 1)
                        vms.n += 1
                    else:
                        col = sb_tiny[:, 0:1]
                        mych(EN.scalar_tensor_tensor(y1[:, i, t0:t0 + c.TBT], xn[sl][:], 1.0,
                                                     mrb[:], ALU.mult, ALU.subtract,
                                                     accum_out=col))
                        mych(EN.tensor_add(ysum[:, i:i + 1], ysum[:, i:i + 1], col))
                        EN.sem_inc(myms.sem, 1)
                        myms.n += 1
                    nrm[it] = myms.n
                    nrm_sem[it] = myms.sem

            vms.wait(GP)
            gms.wait(GP)
            g_s.put(ysi.ap()[:], ysum[:])
            g_s.wait(GP)
            ccs.inc(GP.collective_compute(
                "AllGather", ALU.bypass,
                replica_groups=[[0, 1], [2, 3], [4, 5], [6, 7]],
                ins=[ysi.ap().opt()], outs=[yso.ap().opt()]))
            ccs.wait(SY)
            s_g.put(ysg[:, 0:c.NI], yso.ap()[0:128, :])
            s_g.put(ysg[:, c.NI:2 * c.NI], yso.ap()[128:256, :], rot=False)
            s_g.wait_put(VE)
            vstep(VE.tensor_add(ysg[:, 0:c.NI], ysg[:, 0:c.NI], ysg[:, c.NI:2 * c.NI]))
            vv = vms.inc(VE.tensor_scalar(ysb[:], ysg[:, 0:c.NI], 1.0 / c.SEQ, None, ALU.mult))
            tms.wait(GP)
            g_dpb.put(dpb[:], i_wqf[:])
            g_dpb.wait_put(TE)
            TE.wait_ge(vms.sem, vv)
            for i in range(c.NI):
                last = TE.matmul(PS[6][:, 0:1], dpb[:, i, :], ysb[:, i:i + 1],
                                 start=(i == 0), stop=(i == c.NI - 1))
            tt = tms.inc(last)
            SC.wait_ge(tms.sem, tt)
            aa = ams.inc(SC.copy(sb_tiny[:, 2:3], PS[6][:, 0:1]))
            VE.wait_ge(ams.sem, aa)
            vv = vms.inc(VE.tensor_copy(sb_qcol[:], sb_tiny[:, 2:3]))
            TE.wait_ge(vms.sem, vv)
            g_misc.wait(TE)
            tt = tms.inc(TE.matmul(PS[7][0:1, 0:c.E], sb_qcol[:], sb_keys[:], start=True, stop=True))
            SC.wait_ge(tms.sem, tt)
            aa = ams.inc(SC.activation(sb_tiny[0:1, 8:8 + c.E], PS[7][0:1, 0:c.E], AF.Copy, scale=ISQ_LK))
            VE.wait_ge(ams.sem, aa)
            vstep(VE.tensor_reduce(sb_tiny[0:1, 16:17], sb_tiny[0:1, 8:8 + c.E], AX.X, ALU.max))
            vv = vms.inc(VE.tensor_scalar(sb_tiny[0:1, 8:8 + c.E], sb_tiny[0:1, 8:8 + c.E],
                                          sb_tiny[0:1, 16:17], None, ALU.subtract))
            SC.wait_ge(vms.sem, vv)
            aa = ams.inc(SC.activation(sb_tiny[0:1, 8:8 + c.E], sb_tiny[0:1, 8:8 + c.E], AF.Exp))
            VE.wait_ge(ams.sem, aa)
            vstep(VE.tensor_reduce(sb_tiny[0:1, 17:18], sb_tiny[0:1, 8:8 + c.E], AX.X, ALU.add))
            vstep(VE.reciprocal(sb_tiny[0:1, 18:19], sb_tiny[0:1, 17:18]))
            vstep(VE.tensor_scalar(sb_tiny[0:1, 8:8 + c.E], sb_tiny[0:1, 8:8 + c.E],
                             sb_tiny[0:1, 18:19], None, ALU.mult))
            vv = vms.inc(VE.tensor_copy(sb_rowb[0:1, 0:c.E], sb_tiny[0:1, 8:8 + c.E]))
            TE.wait_ge(vms.sem, vv)
            for p in range(c.NP):
                TE.matmul(PS[6][0:64, 2 + p:3 + p], sb_ones_r[0:1, 0:64],
                          sb_rowb[0:1, 2 * p:2 * p + 1], start=True, stop=True)
                tt = tms.inc(TE.matmul(PS[6][64:128, 2 + p:3 + p], sb_ones_r[0:1, 0:64],
                                       sb_rowb[0:1, 2 * p + 1:2 * p + 2], start=True, stop=True))
            SC.wait_ge(tms.sem, tt)
            acoef = ams.inc(SC.copy(sb_coefv[:, 0:c.NP], PS[6][:, 2:2 + c.NP]))

            gms.wait(TE)
            for p in range(c.NP):
                tms.wait(GP)
                g_dpb.put(dpb[:], i_down[p])
                g_dpb.wait_put(TE)
                for tb in range(c.TBN):
                    t0 = tb * c.TBT
                    vms.wait(TE)
                    for i in range(c.NI):
                        for hh in range(c.NH):
                            last = TE.matmul(PS[0 + hh][:, :], dpb[:, i, :],
                                             y1[:, i, t0 + hh * 512:t0 + (hh + 1) * 512],
                                             start=(i == 0), stop=(i == c.NI - 1))
                    tt = tms.inc(last)
                    VE.wait_ge(tms.sem, tt)
                    VE.wait_ge(ams.sem, acoef)
                    for hh in range(c.NH):
                        vv = vms.inc(VE.tensor_scalar(
                            hsb[:, p, t0 + hh * 512:t0 + (hh + 1) * 512],
                            PS[0 + hh][:, :], sb_coefv[:, p:p + 1], None, ALU.mult))

            grp_cnt = {}
            stt_cnt = {}
            st_snap = {}
            vms.wait(SY)
            gms.wait(SY)
            VE.wait_ge(gms.sem, gms.n)
            for tb in range(c.TBN):
                t0 = tb * c.TBT
                for o in range(c.NI):
                    sl = o % 2
                    it = tb * c.NI + o
                    if it >= 2:
                        GP.wait_ge(tms.sem, grp_cnt[it - 2])
                    g_wob.put(wob[sl][:], i_worg[o])
                    g_upb.put(upb[sl][:], i_up[o])
                    if it >= 2:
                        SY.wait_ge(vms.sem, stt_cnt[it - 2])
                    s_r.put(xb[sl][:], x_src[o * 128:(o + 1) * 128, t0:t0 + c.TBT])
                    g_wob.wait_put(TE)
                    g_upb.wait_put(TE)
                    for hh in range(c.NH):
                        pb = 4 + (it % 2) * c.NH + hh
                        if it >= 2:
                            TE.wait_ge(vms.sem, stt_cnt[it - 2])
                        else:
                            vms.wait(TE)
                        for i in range(c.NI):
                            TE.matmul(PS[pb][:, :], wob[sl][:, i, :],
                                      y1[:, i, t0 + hh * 512:t0 + (hh + 1) * 512],
                                      start=(i == 0), stop=False)
                        for p in range(c.NP):
                            last = TE.matmul(PS[pb][:, :], upb[sl][:, p, :],
                                             hsb[:, p, t0 + hh * 512:t0 + (hh + 1) * 512],
                                             start=False, stop=(p == c.NP - 1))
                        tt = tms.inc(last)
                        VE.wait_ge(tms.sem, tt)
                        s_r.wait_put(VE)
                        if it >= 1 and hh == 0:
                            c_s.wait_at(VE, st_snap[it - 1])
                        vv = vms.inc(VE.scalar_tensor_tensor(
                            res[0][:, hh * 512:(hh + 1) * 512], PS[pb][:, :], 0.5,
                            xb[sl][:, hh * 512:(hh + 1) * 512], ALU.mult, ALU.add))
                    grp_cnt[it] = tt
                    stt_cnt[it] = vv
                    SC.wait_ge(vms.sem, vv)
                    st_snap[it] = c_s.put(x_dst[o * 128:(o + 1) * 128, t0:t0 + c.TBT], res[0][:])
            c_s.wait(SC)
        return

    def attn_phase():
        es = contextlib.ExitStack()
        with es:
            y2b = es.enter_context(nc.sbuf_tensor("a_y2b", [128, c.NI, 512], BF16))
            es2 = contextlib.ExitStack()
            with es2:
                xb = [es2.enter_context(nc.sbuf_tensor(f"a_xb{i}", [128, 512], F32)) for i in range(2)]
                xc2 = [es2.enter_context(nc.sbuf_tensor(f"a_xc{i}", [128, 512], BF16)) for i in range(2)]
                sq2 = [es2.enter_context(nc.sbuf_tensor("a_sq0", [128, 512], BF16))]
                rstdb = [es2.enter_context(nc.sbuf_tensor(f"a_rstdb{i}", [128, 512], F32)) for i in range(2)]
                mrb = [es2.enter_context(nc.sbuf_tensor(f"a_mrb{i}", [128, 512], F32)) for i in range(2)]
                xn2 = es2.enter_context(nc.sbuf_tensor("a_xn2", [128, 512], F32))
                wkb = es2.enter_context(nc.sbuf_tensor("a_wkb", [128, c.NI, c.KVDIM], BF16))
                wvb = es2.enter_context(nc.sbuf_tensor("a_wvb", [128, c.NI, c.KVDIM], BF16))
                kev = [es2.enter_context(nc.sbuf_tensor(f"a_kev{i}", [128, 512], BF16)) for i in range(2)]
                vev = [es2.enter_context(nc.sbuf_tensor(f"a_vev{i}", [128, c.VW], BF16)) for i in range(2)]
                ksq = es2.enter_context(nc.sbuf_tensor("a_ksq", [128, 512], BF16))
                knt = es2.enter_context(nc.sbuf_tensor("a_knt", [1, 512], BF16))
                kngbuf = es2.enter_context(nc.sbuf_tensor("a_kngbuf", [1, c.NTOK], BF16))

                c_s.wait(SY)
                c_s.wait(GP)
                g_misc.put(wkb[:], i_wk[:])
                g_misc.put(wvb[:], i_wv[:])
                nb = c.NTOK // 512
                kv_snap = {}
                ps1_reader = [0]
                nrm_v = [0]
                nrm_g = [0]
                aa_ln = [0]

                def stats_stage(tb2, i):
                    sl = i % 2
                    it = tb2 * c.NI + i
                    if it >= 2:
                        GP.wait_ge(tms.sem, stat_mm[it - 2])
                    g_x.put(xc2[sl][:], d_x2[i * 128:(i + 1) * 128, tb2 * 512:tb2 * 512 + 512])
                    g_x.wait_put(VE)
                    if it >= 1:
                        VE.wait_ge(tms.sem, stat_mm[it - 1])
                    v1 = vms.inc(VE.tensor_mul(sq2[0][:], xc2[sl][:], xc2[sl][:]))
                    g_x.wait_put(TE)
                    TE.wait_ge(vms.sem, v1)
                    TE.matmul(PS[0][0:1, :], sb_ones_c[:], xc2[sl][:], start=(i == 0), stop=(i == c.NI - 1))
                    stat_mm[it] = tms.inc(TE.matmul(PS[2][0:1, :], sb_ones_c[:], sq2[0][:],
                                                    start=(i == 0), stop=(i == c.NI - 1)))

                stat_mm = {}
                for i in range(c.NI):
                    stats_stage(0, i)
                aa_ln[0] = ln_stats_512(rstdb[0], mrb[0], PS[0], PS[2], 0)
                nrm = {}
                nrm_sem = {}
                s_n.align(3)
                for tb in range(nb):
                    t0 = tb * 512
                    xn = [xb[0], xb[1], xn2]
                    aa = aa_ln[0]
                    for i in range(c.NI):
                        it = tb * c.NI + i
                        sl = it % 3
                        on_gp = (sl == 2)
                        EN = GP if on_gp else VE
                        mych = gstep if on_gp else vstep
                        myms = gms if on_gp else vms
                        if it >= 3:
                            SY.wait_ge(nrm_sem[it - 3], nrm[it - 3])
                        s_n.put(xn[sl][:], d_x2[i * 128:(i + 1) * 128, t0:t0 + 512])
                        s_n.wait_put(EN)
                        EN.wait_ge(ams.sem, aa)
                        if tb > 0 and i < 3:
                            s_y2w.wait(EN)
                        if i < 3:
                            tms.wait(EN)
                        mych(EN.tensor_mul(xn[sl][:], xn[sl][:], rstdb[tb % 2][:]))
                        mych(EN.tensor_sub(y2b[:, i, :], xn[sl][:], mrb[tb % 2][:]))
                        EN.sem_inc(myms.sem, 1)
                        myms.n += 1
                        nrm[it] = myms.n
                        nrm_sem[it] = myms.sem
                    nrm_v[0] = vms.n
                    nrm_g[0] = gms.n
                    SC.wait_ge(vms.sem, nrm_v[0])
                    SC.wait_ge(gms.sem, nrm_g[0])
                    np4 = max(1, c.NI // 4)
                    for part in range(cdiv(c.NI, np4)):
                        i0, i1 = part * np4, min((part + 1) * np4, c.NI)
                        s_y2w.put(d_y2[i0:i1, :, t0:t0 + 512].rearrange("a p t -> p a t"),
                                  y2b[:, i0:i1, :], rot=(part == 0))
                    st_stages = []
                    if tb + 1 < nb:
                        for i in range(c.NI):
                            st_stages.append((lambda tb2, ii: lambda: stats_stage(tb2, ii))(tb + 1, i))
                    g_misc.wait(TE)
                    TE.wait_ge(vms.sem, nrm_v[0])
                    TE.wait_ge(gms.sem, nrm_g[0])
                    for h in range(c.KVH):
                        sl = h % 2
                        if ps1_reader[0]:
                            TE.wait_ge(ams.sem, ps1_reader[0])
                        for i in range(c.NI):
                            last = TE.matmul(PS[1][:, :], wkb[:, i, h * 128:(h + 1) * 128], y2b[:, i, :],
                                             start=(i == 0), stop=(i == c.NI - 1))
                        tt = tms.inc(last)
                        SC.wait_ge(tms.sem, tt)
                        if ("k", sl) in kv_snap:
                            g_ks.wait_at(SC, kv_snap[("k", sl)])
                        if ("ksq", sl) in kv_snap:
                            SC.wait_ge(vms.sem, kv_snap[("ksq", sl)])
                        aa2 = ams.inc(SC.copy(kev[sl][:], PS[1][:, :]))
                        ps1_reader[0] = aa2
                        GP.wait_ge(ams.sem, aa2)
                        off_k = h * 128 * c.NTOK
                        kv_snap[("k", sl)] = g_ks.put(
                            kvin_flat[off_k:off_k + 128 * c.NTOK].rearrange("(p t) -> p t", p=128)[:, t0:t0 + 512],
                            kev[sl][:])
                        VE.wait_ge(ams.sem, aa2)
                        if h >= 1 or tb >= 1:
                            VE.wait_ge(tms.sem, kv_snap.get(("knmm", 0), 0))
                        v1 = vms.inc(VE.tensor_mul(ksq[:], kev[sl][:], kev[sl][:]))
                        kv_snap[("ksq", sl)] = v1
                        TE.wait_ge(vms.sem, v1)
                        tt = tms.inc(TE.matmul(PS[3][0:1, :], sb_ones_c[:], ksq[:], start=True, stop=True))
                        kv_snap[("knmm", 0)] = tt
                        SC.wait_ge(tms.sem, tt)
                        if "kn" in kv_snap:
                            g_kn.wait_at(SC, kv_snap["kn"])
                        aa_kn = ams.inc(SC.copy(knt[:], PS[3][0:1, :]))
                        GP.wait_ge(ams.sem, aa_kn)
                        off_kn = 2 * kblk + h * c.NTOK + t0
                        kv_snap["kn"] = g_kn.put(
                            kvin_flat[off_kn:off_kn + 512].rearrange("(a b) -> a b", a=1), knt[:])
                        for _ in range(2):
                            if st_stages:
                                st_stages.pop(0)()
                    for u in range(4):
                        for hf in range(c.VH):
                            sl = (u * c.VH + hf) % 2
                            if ps1_reader[0]:
                                TE.wait_ge(ams.sem, ps1_reader[0])
                            for i in range(c.NI):
                                last = TE.matmul(PS[1][:, 0:c.VW], y2b[:, i, u * 128:(u + 1) * 128],
                                                 wvb[:, i, hf * c.VW:(hf + 1) * c.VW],
                                                 start=(i == 0), stop=(i == c.NI - 1))
                            tt = tms.inc(last)
                            SC.wait_ge(tms.sem, tt)
                            if ("v", sl) in kv_snap:
                                g_vs.wait_at(SC, kv_snap[("v", sl)])
                            aa2 = ams.inc(SC.copy(vev[sl][:], PS[1][:, 0:c.VW]))
                            ps1_reader[0] = aa2
                            GP.wait_ge(ams.sem, aa2)
                            for hh in range(c.VW // 128):
                                h = hf * (c.VW // 128) + hh
                                off_v = kblk + h * c.NTOK * 128 + (tb * 4 + u) * 128 * 128
                                kv_snap[("v", sl)] = g_vs.put(
                                    kvin_flat[off_v:off_v + 16384].rearrange("(p q) -> p q", p=128),
                                    vev[sl][:, hh * 128:(hh + 1) * 128], rot=(hh == 0))
                            for _ in range(2):
                                if st_stages:
                                    st_stages.pop(0)()
                    for st in st_stages:
                        st()
                    if tb + 1 < nb:
                        aa_ln[0] = ln_stats_512(rstdb[(tb + 1) % 2], mrb[(tb + 1) % 2], PS[0], PS[2], 0)
                g_ks.wait(GP)
                g_vs.wait(GP)
                g_kn.wait(GP)
                ccs.inc(GP.collective_compute(
                    "AllGather", ALU.bypass,
                    replica_groups=[[0, 1], [2, 3], [4, 5], [6, 7]],
                    ins=[t_kvin.ap().opt()], outs=[t_kvout.ap().opt()]))
                ccs.wait(SY)
                vstep(VE.memset(sb_kmax2[:], 0.0))
                for rank in range(2):
                    for h in range(c.KVH):
                        vms.wait(SY)
                        off_kn = rank * cc_elems + 2 * kblk + h * c.NTOK
                        s_g.put(kngbuf[:], kvout_flat[off_kn:off_kn + c.NTOK].rearrange("(a b) -> a b", a=1))
                        s_g.wait_put(VE)
                        vstep(VE.tensor_reduce(sb_tiny[0:1, 20:21], kngbuf[:], AX.X, ALU.max))
                        vstep(VE.tensor_tensor(sb_kmax2[0:1, h:h + 1], sb_kmax2[0:1, h:h + 1],
                                               sb_tiny[0:1, 20:21], ALU.max))
                        VE.sem_inc(vms.sem, 1)
                        vms.n += 1
            barrier()

            wqb = [es.enter_context(nc.sbuf_tensor(f"a_wq{i}", [128, c.NI, 128], BF16)) for i in range(2)]
            qsb = [es.enter_context(nc.sbuf_tensor(f"a_q{i}", [128, 512], BF16)) for i in range(2)]
            qsq = es.enter_context(nc.sbuf_tensor("a_qsq", [128, 512], BF16))
            kch = [[es.enter_context(nc.sbuf_tensor(f"a_k{p}{i}", [128, 512], BF16)) for i in range(2)]
                   for p in range(2)]
            vch = [[es.enter_context(nc.sbuf_tensor(f"a_v{p}{i}", [128, 4, 128], BF16)) for i in range(2)]
                   for p in range(2)]
            pbuf = [es.enter_context(nc.sbuf_tensor(f"a_p{i}", [128, 512], BF16)) for i in range(3)]
            negm = [es.enter_context(nc.sbuf_tensor(f"a_negm{i}", [128, 1], F32)) for i in range(2)]
            recb = es.enter_context(nc.sbuf_tensor("a_recb", [128, 512], F32))
            ost = es.enter_context(nc.sbuf_tensor("a_ost", [128, c.NQH, 512], BF16))
            wwo = [es.enter_context(nc.sbuf_tensor(f"a_wwo{i}", [128, c.NQH, 128], BF16)) for i in range(3)]
            xrb = [es.enter_context(nc.sbuf_tensor(f"a_xr{i}", [128, 512], F32)) for i in range(2)]
            res = [es.enter_context(nc.sbuf_tensor(f"a_res{i}", [128, 512], F32)) for i in range(2)]

            def k_chunk_ap(rank, h, cq):
                base = rank * cc_elems + h * 128 * c.NTOK
                return kvout_flat[base:base + 128 * c.NTOK].rearrange(
                    "(p t) -> p t", p=128)[:, cq * 512:(cq + 1) * 512]

            def v_chunk_ap(rank, h, cq):
                base = rank * cc_elems + kblk + h * c.NTOK * 128 + cq * 512 * 128
                return kvout_flat[base:base + 512 * 128].rearrange(
                    "(a p q) -> p a q", p=128, a=4)

            s_y2w.wait(SY)
            c_s.wait(SY)
            wo_last_cnt = 0
            wo_st = []
            prev_buf_rs = {}
            wq_snap = {}
            qp_done = {}
            qn_read = {}
            qev_map = {}
            tqn_map = {}
            lastS_map = {}
            lastRS_map = {}
            vrec_map = {}
            onorm_map = {}
            obank = lambda hg: 5 if hg % 2 == 0 else 7
            for qb in range(c.QBN):
                i0 = qb * c.QBTILE
                E = 2 * (i0 + c.QBTILE)
                NCH = cdiv(E, 8)
                tms.wait(SY)
                np4 = max(1, c.NI // 4)
                for part in range(cdiv(c.NI, np4)):
                    p0, p1 = part * np4, min((part + 1) * np4, c.NI)
                    s_y.put(y2b[:, p0:p1, :],
                            d_y2[p0:p1, :, qb * 512:(qb + 1) * 512].rearrange("a p t -> p a t"),
                            rot=(part == 0))

                def prefetch_wq(hg2):
                    if hg2 >= c.NQH or (qb, hg2) in wq_snap:
                        return
                    pk = (qb, hg2 - 2) if hg2 >= 2 else (qb - 1, c.NQH - 2 + hg2)
                    if pk in qp_done:
                        GP.wait_ge(tms.sem, qp_done[pk])
                    wq_snap[(qb, hg2)] = g_wq.put(wqb[hg2 % 2][:], i_wq[hg2])

                def bound_stages(hg2):
                    h2 = hg2 // c.G
                    sl2 = hg2 % 2
                    key = (qb, hg2)

                    def s0():
                        prefetch_wq(hg2)
                        g_wq.wait_at(TE, wq_snap[key])
                        ams.wait(TE)
                        pk2 = (qb, hg2 - 1) if hg2 >= 1 else (qb - 1, c.NQH - 1)
                        TE.wait_ge(vms.sem, qn_read.get(pk2, 0))
                        sk = (qb, hg2 - 2) if hg2 >= 2 else (qb - 1, c.NQH - 2 + hg2)
                        if sk in lastS_map:
                            TE.wait_ge(tms.sem, lastS_map[sk])
                        s_y.wait_put(TE)
                        for i in range(c.NI):
                            last = TE.matmul(PS[0][:, :], wqb[sl2][:, i, :], y2b[:, i, :],
                                             start=(i == 0), stop=(i == c.NI - 1))
                        qp_done[key] = tms.inc(last)

                    def s1():
                        SC.wait_ge(tms.sem, qp_done[key])
                        qev_map[key] = ams.inc(SC.activation(qsb[sl2][:], PS[0][:, :], AF.Copy, scale=ISQ_HD))

                    def s2():
                        VE.wait_ge(ams.sem, qev_map[key])
                        pk2 = (qb, hg2 - 1) if hg2 >= 1 else (qb - 1, c.NQH - 1)
                        if pk2 in tqn_map:
                            VE.wait_ge(tms.sem, tqn_map[pk2])
                        qn_read[("v1", key)] = vms.inc(VE.tensor_mul(qsq[:], qsb[sl2][:], qsb[sl2][:]))

                    def s3():
                        TE.wait_ge(vms.sem, qn_read[("v1", key)])
                        tqn_map[key] = tms.inc(TE.matmul(PS[0][0:1, :], sb_ones_c[:], qsq[:],
                                                         start=True, stop=True))

                    def s4():
                        VE.wait_ge(tms.sem, tqn_map[key])
                        vstep(VE.tensor_reduce(sb_tiny[0:1, 24:25], PS[0][0:1, :], AX.X, ALU.max))
                        vms.inc(VE.tensor_mul(sb_tiny[0:1, 25:26], sb_tiny[0:1, 24:25],
                                              sb_kmax2[0:1, h2:h2 + 1]))
                        qn_read[key] = vms.n

                    def s5():
                        SC.wait_ge(vms.sem, qn_read[key])
                        astep(SC.activation(sb_tiny[0:1, 26:27], sb_tiny[0:1, 25:26], AF.Ln))
                        qn_read[("a2", key)] = ams.inc(SC.activation(sb_tiny[0:1, 27:28],
                                                                     sb_tiny[0:1, 26:27], AF.Exp, scale=0.5))

                    def s6():
                        VE.wait_ge(ams.sem, qn_read[("a2", key)])
                        qn_read[("v3", key)] = vms.inc(VE.tensor_scalar(
                            sb_rowb[0:1, 2047:2048], sb_tiny[0:1, 27:28], -1.0, None, ALU.mult))

                    def s7():
                        TE.wait_ge(vms.sem, qn_read[("v3", key)])
                        qn_read[("t3", key)] = tms.inc(TE.matmul(
                            PS[0][:, 0:1], sb_ones_r[:], sb_rowb[0:1, 2047:2048], start=True, stop=True))

                    def s8():
                        SC.wait_ge(tms.sem, qn_read[("t3", key)])
                        astep(SC.copy(negm[sl2][:], PS[0][:, 0:1]))
                        SC.sem_inc(ams.sem, 1)
                        ams.n += 1

                    return [s0, s1, s2, s3, s4, s5, s6, s7, s8]

                def tail_stages(hgp):
                    key = (qb, hgp)
                    ob = obank(hgp)

                    def t0():
                        VE.wait_ge(tms.sem, lastRS_map[key])
                        vstep(VE.reciprocal(sb_row[0:1, 1024:1536], PS[4][0:1, :]))
                        VE.sem_inc(vms.sem, 1)
                        vms.n += 1
                        vrec_map[key] = vms.n
                        vstep(VE.tensor_copy(sb_rowb[0:1, 0:512], sb_row[0:1, 1024:1536]))
                        qn_read[("v4", key)] = vms.inc(VE.tensor_sub(
                            sb_rowb[0:1, 512:1024], sb_row[0:1, 1024:1536], sb_rowb[0:1, 0:512]))

                    def t1():
                        TE.wait_ge(vms.sem, qn_read[("v4", key)])
                        TE.matmul(PS[6][:, :], sb_ones_r[:], sb_rowb[0:1, 0:512], start=True, stop=False)
                        qn_read[("t5", key)] = tms.inc(TE.matmul(
                            PS[6][:, :], sb_ones_r[:], sb_rowb[0:1, 512:1024], start=False, stop=True))

                    def t2():
                        SC.wait_ge(tms.sem, qn_read[("t5", key)])
                        pk = (qb, hgp - 1) if hgp >= 1 else (qb - 1, c.NQH - 1)
                        SC.wait_ge(vms.sem, onorm_map.get(pk, 0))
                        qn_read[("a5", key)] = ams.inc(SC.copy(recb[:], PS[6][:, :]))

                    def t3s():
                        VE.wait_ge(ams.sem, qn_read[("a5", key)])
                        if hgp == 0 and qb > 0:
                            VE.wait_ge(tms.sem, wo_last_cnt)
                        onorm_map[key] = vms.inc(VE.tensor_tensor(
                            ost[:, hgp, :], PS[ob][:, :], recb[:], ALU.mult))

                    return [t0, t1, t2, t3s]

                wq_snap_local = None
                for st in bound_stages(0):
                    st()
                prefetch_wq(1)
                for hg in range(c.NQH):
                    h = hg // c.G
                    sl = hg % 2
                    ob = obank(hg)
                    prefetch_wq(hg + 2)
                    pending = []
                    tl = tail_stages(hg - 1) if hg >= 1 else []
                    bd = bound_stages(hg + 1) if hg + 1 < c.NQH else []
                    order = []
                    if tl:
                        order.append(tl[0])
                    order += bd[0:4]
                    if tl:
                        order.append(tl[1])
                    order += bd[4:6]
                    if tl:
                        order.append(tl[2])
                    order += bd[6:8]
                    if tl:
                        order.append(tl[3])
                    order += bd[8:]
                    pending = order
                    last_S = {}
                    last_exp = {}
                    last_rs = {}
                    ch_lastS = {}
                    ch_lastRS = {}
                    ksnap = {}
                    vsnap = {}
                    s_k.align(4)
                    s_v.align(4)
                    prev_key = (qb, hg - 1) if hg >= 1 else (qb - 1, c.NQH - 1)
                    prevRS = lastRS_map.get(prev_key, 0)

                    def load_kv_chunk(p, cq):
                        cb = cq % 2
                        if (p, cq - 2) in ch_lastRS:
                            SY.wait_ge(tms.sem, ch_lastRS[(p, cq - 2)])
                        else:
                            g = prev_buf_rs.get((p, cb), prevRS)
                            if g:
                                SY.wait_ge(tms.sem, g)
                        ksnap[(p, cq)] = s_k.put(kch[p][cb][:], k_chunk_ap(p, h, cq))
                        vsnap[(p, cq)] = s_v.put(vch[p][cb][:], v_chunk_ap(p, h, cq))

                    def emit_S(j):
                        sfx = max(0, cdiv(j - 1, 2) - i0) if j >= 1 else 0
                        p, jj = j % 2, j // 2
                        cq, a = jj // 4, jj % 4
                        bank = 1 + j % 3
                        if j >= 3:
                            TE.wait_ge(ams.sem, last_exp[j - 3])
                        if a == 0:
                            s_k.wait_at(TE, ksnap[(p, cq)])
                        last = TE.matmul(PS[bank][:, 128 * sfx:512], kch[p][cq % 2][:, a * 128:(a + 1) * 128],
                                         qsb[sl][:, 128 * sfx:512], start=True, stop=True)
                        last_S[j] = tms.inc(last)
                        ch_lastS[(p, cq)] = last_S[j]
                        return sfx

                    load_kv_chunk(0, 0)
                    load_kv_chunk(1, 0)
                    if NCH > 1:
                        load_kv_chunk(0, 1)
                        load_kv_chunk(1, 1)
                    geom = {0: emit_S(0)}
                    if E > 1:
                        geom[1] = emit_S(1)
                    for j in range(E):
                        p, jj = j % 2, j // 2
                        bank = 1 + j % 3
                        sfx = geom[j]
                        if j + 2 < E:
                            geom[j + 2] = emit_S(j + 2)
                        if pending:
                            pending.pop(0)()
                        masked = (j >= 2 * i0)
                        if masked:
                            VE.wait_ge(tms.sem, last_S[j])
                            mk = 0 if (j % 2 == 0) else 1
                            mv = vms.inc(VE.tensor_tensor(PS[bank][:, 128 * sfx:128 * sfx + 128],
                                                          PS[bank][:, 128 * sfx:128 * sfx + 128],
                                                          sb_masks[:, mk, :], ALU.add))
                            SC.wait_ge(vms.sem, mv)
                        else:
                            SC.wait_ge(tms.sem, last_S[j])
                        if j >= 3:
                            SC.wait_ge(tms.sem, last_rs[j - 3])
                        elif prevRS:
                            SC.wait_ge(tms.sem, prevRS)
                        last_exp[j] = ams.inc(SC.activation(pbuf[j % 3][:, 128 * sfx:512],
                                                            PS[bank][:, 128 * sfx:512],
                                                            AF.Exp, bias=negm[sl][:]))
                        cq, a = jj // 4, jj % 4
                        if a == 0:
                            s_v.wait_at(TE, vsnap[(p, cq)])
                        TE.wait_ge(ams.sem, last_exp[j])
                        if j == 0:
                            pk2 = (qb, hg - 2) if hg >= 2 else (qb - 1, c.NQH - 2 + hg)
                            TE.wait_ge(vms.sem, onorm_map.get(pk2, 0))
                            TE.wait_ge(vms.sem, vrec_map.get(prev_key, 0))
                        TE.matmul(PS[ob][:, 128 * sfx:512], vch[p][cq % 2][:, a, :],
                                  pbuf[j % 3][:, 128 * sfx:512], start=(j == 0), stop=(j == E - 1))
                        rs = TE.matmul(PS[4][0:1, 128 * sfx:512], sb_ones_c[:],
                                       pbuf[j % 3][:, 128 * sfx:512], start=(j == 0), stop=(j == E - 1))
                        last_rs[j] = tms.inc(rs)
                        ch_lastRS[(p, jj // 4)] = last_rs[j]
                        if jj % 4 == 3 and p == 1 and (jj // 4) + 2 < NCH:
                            load_kv_chunk(0, jj // 4 + 2)
                            load_kv_chunk(1, jj // 4 + 2)
                    for st in pending:
                        st()
                    lastS_map[(qb, hg)] = last_S[E - 1]
                    lastRS_map[(qb, hg)] = last_rs[E - 1]
                    prev_buf_rs = {}
                    for (p_, cq_), v_ in ch_lastRS.items():
                        k_ = (p_, cq_ % 2)
                        prev_buf_rs[k_] = max(prev_buf_rs.get(k_, 0), v_)
                for st in tail_stages(c.NQH - 1):
                    st()
                onorm = onorm_map[(qb, c.NQH - 1)]
                stt_hist = []
                wo_hist = []
                g_wwo.align(3)
                wwo_snap = {}
                for o in range(min(2, c.NI)):
                    if wo_last_cnt:
                        GP.wait_ge(tms.sem, wo_last_cnt)
                    wwo_snap[o] = g_wwo.put(wwo[o % 3][:], i_wo[o])
                for o in range(c.NI):
                    sl = o % 3
                    if o + 2 < c.NI:
                        if len(wo_hist) >= 1:
                            GP.wait_ge(tms.sem, wo_hist[-1])
                        wwo_snap[o + 2] = g_wwo.put(wwo[(o + 2) % 3][:], i_wo[o + 2])
                    if o >= 2:
                        SY.wait_ge(vms.sem, stt_hist[-2])
                    elif qb > 0:
                        vms.wait(SY)
                    s_r.put(xrb[o % 2][:], d_x2[o * 128:(o + 1) * 128, qb * 512:(qb + 1) * 512])
                    g_wwo.wait_at(TE, wwo_snap[o])
                    if o == 0:
                        TE.wait_ge(vms.sem, onorm)
                    pb = 6 + (o % 2)
                    if o >= 2:
                        TE.wait_ge(vms.sem, stt_hist[-2])
                    for hd in range(c.NQH):
                        last = TE.matmul(PS[pb][:, :], wwo[sl][:, hd, :], ost[:, hd, :],
                                         start=(hd == 0), stop=(hd == c.NQH - 1))
                    wo_last_cnt = tms.inc(last)
                    wo_hist.append(wo_last_cnt)
                    VE.wait_ge(tms.sem, wo_last_cnt)
                    s_r.wait_put(VE)
                    if len(wo_st) >= 2:
                        c_s.wait_at(VE, wo_st[-2])
                    vv = vms.inc(VE.scalar_tensor_tensor(res[o % 2][:], PS[pb][:, :], 1.0,
                                                         xrb[o % 2][:], ALU.mult, ALU.add))
                    stt_hist.append(vv)
                    SC.wait_ge(vms.sem, vv)
                    wo_st.append(c_s.put(d_x3[o * 128:(o + 1) * 128, qb * 512:(qb + 1) * 512], res[o % 2][:]))
            c_s.wait(SC)
        return

    ffn_phase("f1", i_x, d_x2, t_ys[0], t_yso[0])
    barrier()
    attn_phase()
    barrier()
    ffn_phase("f3", d_x3, o_out, t_ys[1], t_yso[1])
    c_s.wait(SC)
    g_s.wait(GP)
    s_y2w.wait(SC)
    barrier()
    return nc


def _host_prep(inputs, cfg: CFG):
    c = cfg
    x = np.asarray(inputs["L0_attn_x"], dtype=np.float32)
    Wq = np.asarray(inputs["L0_attn_Wq"], dtype=np.float32)
    Wk = np.asarray(inputs["L0_attn_Wk"], dtype=np.float32)
    Wv = np.asarray(inputs["L0_attn_Wv"], dtype=np.float32)
    Wo = np.asarray(inputs["L0_attn_Wo"], dtype=np.float32)
    Worg = np.asarray(inputs["L0_ffn_W_org"], dtype=np.float32)
    down = np.asarray(inputs["L0_ffn_lora_down"], dtype=np.float32)
    up = np.asarray(inputs["L0_ffn_lora_up"], dtype=np.float32)
    keys = np.asarray(inputs["L0_ffn_lora_keys"], dtype=np.float32)
    Wqf = np.asarray(inputs["L0_ffn_Wq"], dtype=np.float32)

    NI = c.NI

    def blk_lhsT(wT):
        K, M = wT.shape
        return np.ascontiguousarray(
            wT.reshape(K // 128, 128, M // 128, 128).transpose(2, 1, 0, 3))

    def blk_rhs(wT):
        K, N = wT.shape
        return np.ascontiguousarray(wT.reshape(K // 128, 128, N).transpose(1, 0, 2))

    worg4 = blk_lhsT(Worg.T)
    wq4 = blk_lhsT(Wq.T)
    wo4 = blk_lhsT(Wo.T)
    wk4 = blk_rhs(Wk.T)
    wv4 = blk_rhs(Wv.T)
    downp = np.concatenate([down[0::2], down[1::2]], axis=2)
    down4 = np.ascontiguousarray(
        downp.reshape(c.NP, NI, 128, 128).transpose(0, 2, 1, 3))
    uppair = np.stack([np.concatenate([up[2 * p], up[2 * p + 1]], axis=0)
                       for p in range(c.NP)])
    up4 = np.ascontiguousarray(
        uppair.reshape(c.NP, 128, NI, 128).transpose(2, 1, 0, 3))
    wqf4 = blk_rhs(Wqf.T)
    wqf4 = np.ascontiguousarray(Wqf.T.reshape(NI, 128, 128).transpose(1, 0, 2))
    keysT = np.ascontiguousarray(keys.T)

    import ml_dtypes
    kl = np.arange(128)[:, None]
    ql = np.arange(128)[None, :]
    bf = ml_dtypes.bfloat16
    tri = np.where(kl <= ql, 0.0, NEG).astype(bf)
    masks = [np.stack([tri, np.full((128, 128), NEG, bf)]),
             np.stack([np.zeros((128, 128), bf), tri])]

    in_maps = []
    for core in range(8):
        b, par = core // 2, core % 2
        sel = x[b].reshape(2 * c.NTILE, 128, c.D)[par::2].reshape(c.NTOK, c.D)
        xT = np.ascontiguousarray(sel.T)
        in_maps.append({
            "x": xT, "masks": masks[par], "worg4": worg4, "wq4": wq4,
            "wo4": wo4, "wk4": wk4, "wv4": wv4, "down4": down4, "up4": up4,
            "wqf4": wqf4, "keysT": keysT,
        })
    return in_maps


def _host_assemble(results, cfg: CFG):
    c = cfg
    out = np.empty((4, c.SEQ, c.D), dtype=np.float32)
    for core in range(8):
        b, par = core // 2, core % 2
        oT = np.asarray(results[core]["out"])
        rows = np.ascontiguousarray(oT.T)
        out[b].reshape(2 * c.NTILE, 128, c.D)[par::2] = rows.reshape(c.NTILE, 128, c.D)
    return out


def kernel(**inputs):
    cfg = CFG(mini=False)
    in_maps = _host_prep(inputs, cfg)
    nc = build(cfg)
    res = run_bass_kernel_spmd(nc, in_maps, core_ids=list(range(8)))
    return _host_assemble(res.results, cfg)
